# version 11
# baseline (speedup 1.0000x reference)
import numpy as np
import ml_dtypes
from contextlib import ExitStack

import concourse.bass as bass
import concourse.mybir as mybir
from concourse.bass_utils import run_bass_kernel_spmd

F32 = mybir.dt.float32
BF16 = mybir.dt.bfloat16
I32 = mybir.dt.int32
U8 = mybir.dt.uint8
AF = mybir.ActivationFunctionType
OP = mybir.AluOpType
AX = mybir.AxisListType

NP_BF16 = ml_dtypes.bfloat16

L = 4096
D = 1024
NT = 32
NH = 16
HD = 64
STAGES = 6
FA_CHUNKS = NT

DEBUG_TAPS = []
DEBUG = False

ENGINES = ("sync", "vector", "scalar", "gpsimd", "tensor")
HW_Q = ("sync", "scalar")


class Tok:
    __slots__ = ("sem", "idx", "val")

    def __init__(self, sem, idx):
        self.sem = sem
        self.idx = idx
        self.val = None


class Buf:
    __slots__ = ("name", "wr", "rd")

    def __init__(self, name=""):
        self.name = name
        self.wr = None
        self.rd = []


class KB:

    DMA_POOL = 12

    def __init__(self, nc):
        self.nc = nc
        self.prog = {e: [] for e in ENGINES}
        self.sem_idx = {}
        self.consumed = set()
        self.hw = {e: {} for e in ENGINES}
        self.esem = {e: "c_" + e for e in ENGINES}
        self.qpool = {e: ["q_%s_%d" % (e, i) for i in range(self.DMA_POOL)]
                      for e in ("sync", "scalar", "gpsimd", "vector")}
        self.qrr = {e: 0 for e in self.qpool}
        self.sems = {}
        self._tq_rr = 0
        self.fence = []
        self.last_op = {e: None for e in ENGINES}

    def sem_names(self):
        out = [self.esem[e] for e in ENGINES]
        for e in self.qpool:
            out.extend(self.qpool[e])
        return out

    def _tok(self, sem):
        i = self.sem_idx.get(sem, 0) + 1
        self.sem_idx[sem] = i
        return Tok(sem, i)

    def _prune(self, e, waits):
        out = []
        seen = {}
        for t in waits:
            if t is None:
                continue
            if e == "tensor" and t.sem == self.esem["tensor"]:
                continue
            cur = seen.get(t.sem)
            if cur is None or t.idx > cur.idx:
                seen[t.sem] = t
        for sem, t in seen.items():
            if self.hw[e].get(sem, 0) >= t.idx:
                continue
            self.hw[e][sem] = t.idx
            out.append(t)
            self.consumed.add((t.sem, t.idx))
        return out

    def _collect(self, reads, writes, extra):
        waits = []
        for b in reads:
            if b.wr is not None:
                waits.append(b.wr)
        for b in writes:
            if b.wr is not None:
                waits.append(b.wr)
            waits.extend(b.rd)
        waits.extend(extra)
        return waits

    def _emit(self, e, fn, waits, is_dma, sem):
        t = self._tok(sem)
        self.prog[e].append((waits, fn, t, is_dma))
        return t

    SERIAL_CHAIN = False

    def op(self, e, fn, reads=(), writes=(), extra=()):
        waits = list(self._collect(reads, writes, extra))
        if self.SERIAL_CHAIN and e != "tensor" and self.last_op[e] is not None:
            waits.append(self.last_op[e])
        waits = self._prune(e, waits)
        t = self._emit(e, fn, waits, False, self.esem[e])
        self.last_op[e] = t
        for b in reads:
            b.rd.append(t)
        for b in writes:
            b.wr = t
            b.rd = []
        return t

    def _dma_emit(self, e, fn, reads, writes, extra):
        pool = self.qpool[e]
        sem = pool[self.qrr[e] % len(pool)]
        self.qrr[e] += 1
        waits = list(self._collect(reads, writes, extra))
        prev = self.sem_idx.get(sem, 0)
        if prev:
            waits.append(Tok(sem, prev))
        waits = self._prune(e, waits)
        t = self._emit(e, fn, waits, True, sem)
        for b in reads:
            b.rd.append(t)
        for b in writes:
            b.wr = t
            b.rd = []
        return t

    def dma(self, e, out, in_, reads=(), writes=(), extra=(), **kw):
        fn = lambda E, out=out, in_=in_, kw=kw: E.dma_start(out=out, in_=in_, **kw)
        return self._dma_emit(e, fn, reads, writes, extra)

    def dmat(self, e, out, in_, reads=(), writes=(), extra=()):
        assert e in HW_Q
        fn = lambda E, out=out, in_=in_: E.dma_start_transpose(out=out, in_=in_)
        return self._dma_emit(e, fn, reads, writes, extra)

    def tq(self):
        e = HW_Q[self._tq_rr % len(HW_Q)]
        self._tq_rr += 1
        return e

    def buf(self, name=""):
        b = Buf(name)
        b.rd = list(self.fence)
        return b

    def set_fence(self):
        self.fence = self.barrier_toks()

    def barrier_toks(self):
        out = []
        for sem, i in self.sem_idx.items():
            if i:
                out.append(Tok(sem, i))
        return out

    def finalize_into(self, block, sem_handles):
        self.sems = sem_handles
        val = {}
        for e in ENGINES:
            for waits, fn, t, is_dma in self.prog[e]:
                if is_dma or (t.sem, t.idx) in self.consumed:
                    inc = 16 if is_dma else 1
                    val[t.sem] = val.get(t.sem, 0) + inc
                    t.val = val[t.sem]
                else:
                    t.val = None

        sem_points = {}
        for e in ENGINES:
            for waits, fn, t, is_dma in self.prog[e]:
                if t.val is not None:
                    sem_points.setdefault(t.sem, []).append((t.idx, t.val))
        for sem in sem_points:
            sem_points[sem].sort()

        import bisect

        def resolve(t):
            pts = sem_points.get(t.sem)
            assert pts is not None, f"wait on sem {t.sem} with no incing ops"
            i = bisect.bisect_right(pts, (t.idx, 1 << 60)) - 1
            assert i >= 0 and pts[i][0] == t.idx, \
                f"consumed token lost {t.sem} {t.idx}"
            return pts[i][1]

        def mk_runner(e):
            ops = self.prog[e]

            def run(E):
                for waits, fn, t, is_dma in ops:
                    for w in waits:
                        E.wait_ge(self.sems[w.sem], resolve(w))
                    ins = fn(E)
                    if t.val is not None:
                        inc = 16 if is_dma else 1
                        ins.then_inc(self.sems[t.sem], inc)
            return run

        runners = {e: mk_runner(e) for e in ENGINES}
        block.sync(runners["sync"])
        block.vector(runners["vector"])
        block.scalar(runners["scalar"])
        block.gpsimd(runners["gpsimd"])
        block.tensor(runners["tensor"])


class Arena:

    def __init__(self, nc, base, nbytes):
        self.nc = nc
        self.base = base
        self.nbytes = nbytes
        self.off = 0
        self.marks = []
        self.n = 0

    def alloc(self, shape, dtype, name=""):
        assert len(shape) == 2
        if not hasattr(self, "log"):
            self.log = []
        per = int(np.prod(shape[1:])) * mybir.dt.size(dtype)
        off = (self.off + 31) & ~31
        assert off + per <= self._cap, f"arena overflow at {name}: {off}+{per} cap {self._cap}"
        self.off = off + per
        self.log.append((name, off, off + per))
        self.n += 1
        h = self.nc.alloc_sbuf_tensor_at(
            f"a{self.n}_{name or 'x'}", [shape[0], int(shape[1])], dtype,
            offset=self.base + off)
        return h.ap()

    def push(self):
        self.marks.append(self.off)

    def pop(self):
        self.off = self.marks.pop()
        self.limit = None

    def rebase(self, off, limit=None):
        self.off = off
        self.limit = limit

    @property
    def _cap(self):
        return self.limit if getattr(self, "limit", None) else self.nbytes


def _bf(x):
    return np.asarray(x, dtype=NP_BF16)


def host_constants():
    c = {}
    j = np.arange(128)
    c["triu"] = _bf(np.triu(np.ones((128, 128), np.float32)))
    c["ident"] = _bf(np.eye(128, dtype=np.float32))
    c["ones_col"] = _bf(np.ones((128, 1), np.float32))
    c["ones128"] = _bf(np.ones((128, 128), np.float32))
    c["o64_col"] = _bf(np.ones((64, 1), np.float32))
    c["identf32"] = np.eye(128, dtype=np.float32)
    c["ones_row"] = _bf(np.ones((1, 128), np.float32))
    n = (np.arange(L, dtype=np.float64) + 1.0).reshape(NT, 128).T
    c["normc"] = n.astype(np.float32)
    c["rnormc"] = (1.0 / n.astype(np.float32)).astype(np.float32)
    oh = np.zeros((8, 8 * 128), np.float32)
    for i in range(8):
        oh[i, i * 128:(i + 1) * 128] = 1.0
    c["onehot8"] = _bf(oh)
    return c


CONST_SPECS = {
    "triu": ([128, 128], BF16),
    "ident": ([128, 128], BF16),
    "ones_col": ([128, 1], BF16),
    "ones128": ([128, 128], BF16),
    "o64_col": ([64, 1], BF16),
    "identf32": ([128, 128], F32),
    "ones_row": ([1, 128], BF16),
    "normc": ([128, NT], F32),
    "rnormc": ([128, NT], F32),
    "onehot8": ([8, 8 * 128], BF16),
}

INPUT_SPECS = [
    ("L0_attn_x", [L, D], F32),
    ("L0_attn_Wq", [D, D], F32), ("L0_attn_bq", [D], F32),
    ("L0_attn_Wk", [D, D], F32), ("L0_attn_bk", [D], F32),
    ("L0_attn_Wv", [D, D], F32), ("L0_attn_bv", [D], F32),
    ("L0_attn_Wo", [D, D], F32), ("L0_attn_bo", [D], F32),
    ("L0_ffn_packed_weight", [262144], I32),
    ("L0_ffn_weight_scale", [1], F32),
    ("L0_ffn_bias", [D], F32),
    ("L1_attn_qkv_w", [3 * D, D], F32), ("L1_attn_qkv_b", [3 * D], F32),
    ("L1_attn_rpb", [16, 25], F32),
    ("L1_attn_proj_w", [D, D], F32), ("L1_attn_proj_b", [D], F32),
    ("L1_ffn_slot_init", [64, D], F32),
    ("L1_ffn_w1", [2 * D, 512], F32), ("L1_ffn_b1", [512], F32),
    ("L1_ffn_w2", [512, 64], F32), ("L1_ffn_b2", [64], F32),
    ("L1_ffn_tau", [1], F32),
    ("L1_ffn_Wih", [3 * D, D], F32), ("L1_ffn_Whh", [3 * D, D], F32),
    ("L1_ffn_bih", [3 * D], F32), ("L1_ffn_bhh", [3 * D], F32),
    ("L1_ffn_wq", [D, 256], F32), ("L1_ffn_bq", [256], F32),
    ("L1_ffn_wk", [D, 256], F32), ("L1_ffn_bk", [256], F32),
    ("L1_ffn_wv", [D, D], F32), ("L1_ffn_bv", [D], F32),
    ("L1_ffn_ln_g", [D], F32), ("L1_ffn_ln_b", [D], F32),
    ("L1_ffn_wvp", [D, D], F32), ("L1_ffn_bvp", [D], F32),
    ("L1_ffn_wo", [D, D], F32), ("L1_ffn_bo", [D], F32),
]



def l1attn_diag_intervals(ttype):
    ts = {0: 0, 1: 128, 2: L - 128}[ttype]
    out = []
    for m in range(52, 77):
        ls = []
        for l in range(128):
            c = l + m
            if not (0 <= c < 256):
                continue
            i = ts + l
            n = ts + c - 64
            ni = min(max(i - 6, 0), L - 13)
            if ni <= n <= ni + 12:
                assert 0 <= m - 52 <= 24
                ls.append(l)
        if ls:
            assert ls == list(range(ls[0], ls[-1] + 1)), (ttype, m, ls)
            out.append((m, ls[0], ls[-1] + 1))
    return out


def build_program():
    nc = bass.Bass("TRN2", target_bir_lowering=False)
    dram = {}
    for name, shape, dt in INPUT_SPECS:
        dram[name] = nc.dram_tensor(name, shape, dt, kind="ExternalInput").ap()
    for name, (shape, dt) in CONST_SPECS.items():
        dram[name] = nc.dram_tensor("c_" + name, shape, dt, kind="ExternalInput").ap()
    out = nc.dram_tensor("out", [L, D], F32, kind="ExternalOutput").ap()
    x_hbm = nc.dram_tensor("x_hbm", [L, D], F32, kind="Internal").ap()
    row_hbm = nc.dram_tensor("row_hbm", [64, 4096], F32, kind="Internal").ap()
    lnx_hbm = nc.dram_tensor("lnx_hbm", [L, D], BF16, kind="Internal").ap()
    bmat_hbm = nc.dram_tensor("bmat_hbm", [3, NH, 256, 128], BF16, kind="Internal").ap()

    ARENA_BYTES = 206 * 1024
    with ExitStack() as ctx:
        arena_t = ctx.enter_context(nc.sbuf_tensor("arena", [128, ARENA_BYTES], U8))
        A = Arena(nc, nc.lookup_mloc(arena_t).addr, ARENA_BYTES)
        kb = KB(nc)

        sem_handles = {}
        for name in kb.sem_names():
            sem_handles[name] = ctx.enter_context(nc.semaphore(name))

        psum = [ctx.enter_context(nc.psum_tensor(f"ps{i}", [128, 512], F32))
                for i in range(8)]
        psb = [Buf(f"ps{i}") for i in range(8)]

        build_body(nc, kb, A, dram, out, x_hbm, row_hbm, bmat_hbm, lnx_hbm,
                   psum, psb)

        block = ctx.enter_context(nc.Block())
        kb.finalize_into(block, sem_handles)
    return nc



def build_body(nc, kb, A, dram, out_d, x_hbm, row_hbm, bmat_hbm, lnx_hbm,
               psum, psb):
    l1f = {}

    _tapn = {}

    def tap(name, ap, bufs):
        if not DEBUG:
            return
        if name in _tapn:
            _tapn[name] += 1
            name = f"{name}_{_tapn[name]}"
        else:
            _tapn[name] = 0
        shape = list(ap.shape)
        d = nc.dram_tensor("tap_" + name, shape, ap.dtype, kind="ExternalOutput").ap()
        DEBUG_TAPS.append(("tap_" + name, shape, ap.dtype))
        kb.dma("sync", d, ap, reads=tuple(bufs))

    x_res = A.alloc([128, NT * D], F32, "x_res")
    xb = [Buf(f"x[{t}]") for t in range(NT)]
    xt = lambda t: x_res[:, t * D:(t + 1) * D]

    consts = {}
    cb = {}
    for name, (shape, dt) in CONST_SPECS.items():
        consts[name] = A.alloc(shape, dt, name)
        cb[name] = Buf(name)
        if name == "rnormc":
            continue
        kb.dma("sync", consts[name], dram[name], writes=(cb[name],))
    kb.op("vector", lambda E: E.reciprocal(consts["rnormc"], consts["normc"]),
          reads=(cb["normc"],), writes=(cb["rnormc"],))

    _cc = {}
    _cc_area = A.alloc([128, 16], F32, "constcols")
    _cc_buf = Buf("constcols")

    def constcol(v):
        if v not in _cc:
            i = len(_cc)
            assert i < 16
            _cc[v] = _cc_area[:, i:i + 1]
            kb.op("vector", lambda E, i=i, v=v: E.memset(_cc_area[:, i:i + 1], v),
                  writes=(_cc_buf,))
        return _cc[v]

    st_sum = A.alloc([128, NT], F32, "st_sum")
    st_ssq = A.alloc([128, NT], F32, "st_ssq")
    st_mu = A.alloc([128, NT], F32, "st_mu")
    st_rstd = A.alloc([128, NT], F32, "st_rstd")
    st_nmr = A.alloc([128, NT], F32, "st_nmr")
    st_tmp = A.alloc([128, NT], F32, "st_tmp")
    sb_sum = [Buf() for _ in range(NT)]
    sb_ssq = [Buf() for _ in range(NT)]
    sb_mu = Buf("mu"); sb_rstd = Buf("rstd"); sb_nmr = Buf("nmr"); sb_tmp = Buf()
    dump = A.alloc([128, D], F32, "dump")
    b_dump = Buf("dump")

    if STAGES >= 5:
        rpbbc = A.alloc([128, 16 * 32], F32, "rpb_bc")
        rpbbf = A.alloc([128, 16 * 32], BF16, "rpb_bf")
        negt = A.alloc([128, 128], BF16, "neg_tile")

    def bmat_build():
        b_rpb = kb.buf("rpb")
        b_neg = kb.buf("neg")
        kb.op("vector", lambda E: E.memset(rpbbc[:], 0.0), writes=(b_rpb,))
        for h in range(NH):
            kb.dma("sync", rpbbc[:, h * 32:h * 32 + 25],
                   dram["L1_attn_rpb"][h:h + 1, :].partition_broadcast(128),
                   writes=(b_rpb,))
        kb.op("vector", lambda E: E.tensor_copy(rpbbf, rpbbc),
              reads=(b_rpb,), writes=(b_rpb,))
        kb.op("vector", lambda E: E.memset(negt[:], -30000.0), writes=(b_neg,))
        bm = bmat_hbm
        b_bm = kb.buf("bmat")
        for tt in range(3):
            for h in range(NH):
                for cc in range(2):
                    kb.dma("gpsimd",
                           bm[tt, h, cc * 128:(cc + 1) * 128, :], negt[:],
                           reads=(b_neg,), writes=(b_bm,))
        for tt in range(3):
            ivals = l1attn_diag_intervals(tt)
            for h in range(NH):
                for (m, l0, l1) in ivals:
                    base = ((tt * NH + h) * 256) * 128
                    dst = bass.AP(bm.tensor, base + 128 * m + 129 * l0,
                                  [[129, l1 - l0]])
                    kb.dma("gpsimd", dst,
                           rpbbf[l0:l1, h * 32 + m - 52:m - 51 + h * 32],
                           reads=(b_rpb,), writes=(b_bm,),
                           allow_slow_non_contiguous=True)
        l1f["b_bm"] = b_bm

    x_in = dram["L0_attn_x"]
    for t in range(NT):
        kb.dma("sync", xt(t), x_in[t * 128:(t + 1) * 128, :], writes=(xb[t],))


    def ln_stats_pass(read_tile, n_tiles=NT):
        for t in range(n_tiles):
            ap, bufs = read_tile(t)
            sb_sum[t] = kb.buf()
            sb_ssq[t] = kb.buf()
            kb.op("vector",
                  lambda E, ap=ap, t=t: E.reduce_sum(st_sum[:, t:t + 1], ap, axis=AX.X),
                  reads=bufs, writes=(sb_sum[t],))
            kb.op("scalar",
                  lambda E, ap=ap, t=t: E.activation(
                      dump[:], ap, AF.Square, accum_out=st_ssq[:, t:t + 1]),
                  reads=bufs, writes=(b_dump, sb_ssq[t]))
        kb.op("vector",
              lambda E: E.tensor_scalar(st_mu[:, 0:n_tiles], st_sum[:, 0:n_tiles],
                                        1.0 / D, None, op0=OP.mult),
              reads=sb_sum[:n_tiles], writes=(sb_mu,))
        kb.op("vector",
              lambda E: E.tensor_tensor(st_tmp[:, 0:n_tiles], st_mu[:, 0:n_tiles],
                                        st_mu[:, 0:n_tiles], op=OP.mult),
              reads=(sb_mu,), writes=(sb_tmp,))
        kb.op("vector",
              lambda E: E.scalar_tensor_tensor(
                  st_rstd[:, 0:n_tiles], st_ssq[:, 0:n_tiles], 1.0 / D,
                  st_tmp[:, 0:n_tiles], op0=OP.mult, op1=OP.subtract),
              reads=sb_ssq[:n_tiles] + [sb_tmp], writes=(sb_rstd,))
        eps_c = constcol(1e-5)
        kb.op("scalar",
              lambda E: E.activation(st_tmp[:, 0:n_tiles], st_rstd[:, 0:n_tiles],
                                     AF.Ln, bias=eps_c),
              reads=(sb_rstd, _cc_buf), writes=(sb_tmp,))
        kb.op("scalar",
              lambda E: E.activation(st_rstd[:, 0:n_tiles], st_tmp[:, 0:n_tiles],
                                     AF.Exp, scale=-0.5),
              reads=(sb_tmp,), writes=(sb_rstd,))
        kb.op("vector",
              lambda E: E.tensor_tensor(st_nmr[:, 0:n_tiles], st_mu[:, 0:n_tiles],
                                        st_rstd[:, 0:n_tiles], op=OP.mult),
              reads=(sb_mu, sb_rstd), writes=(sb_nmr,))
        kb.op("vector",
              lambda E: E.tensor_scalar(st_nmr[:, 0:n_tiles], st_nmr[:, 0:n_tiles],
                                        -1.0, None, op0=OP.mult),
              reads=(sb_nmr,), writes=(sb_nmr,))

    bstage = {"f": None, "b": None, "buf": None, "w": 0}

    def set_bias_stage(width):
        bstage["f"] = A.alloc([1, width], F32, "bstg_f")
        bstage["b"] = A.alloc([1, width], BF16, "bstg_b")
        bstage["buf"] = kb.buf("bstg")
        bstage["w"] = width

    def bias_rows(name, src_ap, width):
        assert width <= bstage["w"]
        rows = A.alloc([2, width], BF16, name + "_rows")
        bsrc = bstage["f"][0:1, 0:width]
        lo_b = bstage["b"][0:1, 0:width]
        b_stage = bstage["buf"]
        b_rows = kb.buf()
        kb.dma("sync", bsrc, src_ap, writes=(b_stage,))
        kb.op("vector", lambda E, rows=rows, bsrc=bsrc: E.tensor_copy(
            rows[0:1, :], bsrc), reads=(b_stage,), writes=(b_rows,))
        kb.op("vector", lambda E, rows=rows, bsrc=bsrc, lo_b=lo_b: E.tensor_tensor(
            lo_b, bsrc, rows[0:1, :], op=OP.subtract),
            reads=(b_stage, b_rows), writes=(b_stage,))
        kb.dma("sync", rows[1:2, :], lo_b, reads=(b_stage,), writes=(b_rows,))
        return rows, b_rows

    ones2 = A.alloc([2, 128], BF16, "ones2")
    b_ones2 = Buf()
    kb.op("vector", lambda E: E.memset(ones2[0:2, :], 1.0), writes=(b_ones2,))

    w0T = A.alloc([128, 8 * D], BF16, "w0T")
    b_w0T = Buf("w0T")
    A.push()
    pk = A.alloc([128, 2 * D], I32, "packed")
    b_pk = Buf()
    w0 = A.alloc([128, 8 * D], BF16, "w0_om")
    b_w0 = Buf("w0")
    tmp_i = A.alloc([128, D], I32, "unp_i")
    b_tmpi = Buf()
    pw = dram["L0_ffn_packed_weight"]
    kb.dma("sync", pk[:, 0:D], pw.rearrange("(r f) -> r f", f=D)[0:128, :], writes=(b_pk,))
    kb.dma("sync", pk[:, D:2 * D], pw.rearrange("(r f) -> r f", f=D)[128:256, :], writes=(b_pk,))
    for s in range(4):
        for half in range(2):
            src = pk[:, half * D:(half + 1) * D]
            j = (s * 256 + half * 128) // 128
            kb.op("vector",
                  lambda E, src=src, s=s: E.tensor_scalar(
                      tmp_i, src, 2 * s, 3, op0=OP.logical_shift_right,
                      op1=OP.bitwise_and),
                  reads=(b_pk,), writes=(b_tmpi,))
            kb.op("vector",
                  lambda E, j=j: E.tensor_scalar(
                      w0[:, j * D:(j + 1) * D], tmp_i, -1.0, None, op0=OP.add),
                  reads=(b_tmpi,), writes=(b_w0,))
    for i in range(8):
        for j in range(8):
            kb.dmat(kb.tq(), w0T[:, i * D + j * 128:i * D + (j + 1) * 128],
                    w0[:, j * D + i * 128:j * D + (i + 1) * 128],
                    reads=(b_w0,), writes=(b_w0T,))

    A.pop()
    kb.set_fence()

    bias0_bc = A.alloc([128, D], F32, "bias0_bc")
    b_bias0 = kb.buf("bias0")
    kb.dma("sync", bias0_bc, dram["L0_ffn_bias"][None, :].partition_broadcast(128),
           writes=(b_bias0,))
    kb.op("vector", lambda E: E.tensor_scalar(bias0_bc, bias0_bc, 0.5, None,
                                              op0=OP.mult),
          reads=(b_bias0,), writes=(b_bias0,))
    ws0 = A.alloc([128, 1], F32, "ws0")
    b_ws0 = kb.buf("ws0")
    kb.dma("sync", ws0, dram["L0_ffn_weight_scale"][None, :].partition_broadcast(128),
           writes=(b_ws0,))
    TAIL_OFF = (A.off + 31) & ~31


    def l0_ffn():
        A.push()
        kb.set_fence()
        ln_stats_pass(lambda t: (xt(t), (xb[t],)))
        lnx = A.alloc([128, D], F32, "lnx")
        amx = A.alloc([128, 4], F32, "amx")
        qi = A.alloc([128, D], I32, "qi")
        qf = A.alloc([128, D], F32, "qf")
        qm = A.alloc([128, D], F32, "qm")
        qb = [A.alloc([128, D], BF16, f"qb{i}") for i in range(2)]
        qT = [A.alloc([128, D], BF16, f"qT{i}") for i in range(2)]
        sc = [A.alloc([128, 2], F32, f"sc{i}") for i in range(2)]
        ep = [A.alloc([128, D], F32, f"ep{i}") for i in range(2)]
        b_lnx, b_amx, b_qi = kb.buf(), kb.buf(), kb.buf()
        b_qf, b_qm = kb.buf(), kb.buf()
        b_qb = [kb.buf(), kb.buf()]
        b_qT = [kb.buf(), kb.buf()]
        b_sc = [kb.buf(), kb.buf()]
        b_ep = [kb.buf(), kb.buf()]
        for t in range(NT):
            pp = t % 2
            kb.op("vector",
                  lambda E, t=t: E.tensor_scalar(
                      lnx, xt(t), st_mu[:, t:t + 1], st_rstd[:, t:t + 1],
                      op0=OP.subtract, op1=OP.mult),
                  reads=(xb[t], sb_mu, sb_rstd), writes=(b_lnx,))
            kb.op("vector",
                  lambda E: E.reduce_max(amx[:, 0:1], lnx, axis=AX.X,
                                         apply_absolute_value=True),
                  reads=(b_lnx,), writes=(b_amx,))
            kb.op("vector",
                  lambda E: E.tensor_scalar(amx[:, 1:2], amx[:, 0:1], 1e-5, None,
                                            op0=OP.max),
                  reads=(b_amx,), writes=(b_amx,))
            kb.op("vector", lambda E: E.reciprocal(amx[:, 2:3], amx[:, 1:2]),
                  reads=(b_amx,), writes=(b_amx,))
            kb.op("vector",
                  lambda E, pp=pp: E.tensor_scalar(sc[pp][:, 0:1], amx[:, 2:3],
                                                   127.0, None, op0=OP.mult),
                  reads=(b_amx,), writes=(b_sc[pp],))
            kb.op("vector",
                  lambda E, pp=pp: E.tensor_scalar(sc[pp][:, 1:2], amx[:, 1:2],
                                                   ws0[:, 0:1], 0.5 / 127.0,
                                                   op0=OP.mult, op1=OP.mult),
                  reads=(b_amx, b_ws0), writes=(b_sc[pp],))
            kb.op("vector",
                  lambda E, pp=pp: E.tensor_scalar(lnx, lnx, sc[pp][:, 0:1], None,
                                                   op0=OP.mult),
                  reads=(b_lnx, b_sc[pp]), writes=(b_lnx,))
            kb.op("vector", lambda E: E.tensor_copy(qi, lnx),
                  reads=(b_lnx,), writes=(b_qi,))
            kb.op("scalar", lambda E: E.copy(qf, qi),
                  reads=(b_qi,), writes=(b_qf,))
            kb.op("vector", lambda E: E.tensor_tensor(lnx, lnx, qf,
                                                      op=OP.subtract),
                  reads=(b_lnx, b_qf), writes=(b_lnx,))
            kb.op("vector",
                  lambda E: E.scalar_tensor_tensor(qm, lnx, 0.5, qf,
                                                   op0=OP.is_ge, op1=OP.add),
                  reads=(b_lnx, b_qf), writes=(b_qm,))
            kb.op("vector",
                  lambda E: E.tensor_scalar(qf, lnx, -0.5, None, op0=OP.is_lt),
                  reads=(b_lnx,), writes=(b_qf,))
            kb.op("vector",
                  lambda E, pp=pp: E.tensor_tensor(qb[pp], qm, qf,
                                                   op=OP.subtract),
                  reads=(b_qm, b_qf), writes=(b_qb[pp],))
            if t == 0:
                tap("mu", st_mu[:, 0:NT], (sb_mu,))
                tap("rstd", st_rstd[:, 0:NT], (sb_rstd,))
                tap("sc", sc[0][:, 0:2], (b_sc[0],))
                tap("qb0", qb[0][:, 0:D], (b_qb[0],))
            for i in range(8):
                kb.dmat(kb.tq(), qT[pp][:, i * 128:(i + 1) * 128],
                        qb[pp][:, i * 128:(i + 1) * 128],
                        reads=(b_qb[pp],), writes=(b_qT[pp],))
            for half in range(2):
                pidx = 4 + 2 * pp + half
                for i in range(8):
                    kb.op("tensor",
                          lambda E, i=i, half=half, pidx=pidx, pp=pp: E.matmul(
                              psum[pidx][:, 0:512],
                              qT[pp][:, i * 128:(i + 1) * 128],
                              w0T[:, i * D + half * 512: i * D + half * 512 + 512],
                              start=(i == 0), stop=(i == 7)),
                          reads=(b_qT[pp], b_w0T), writes=(psb[pidx],))
            for half in range(2):
                pidx = 4 + 2 * pp + half
                kb.op("vector",
                      lambda E, pidx=pidx, half=half, pp=pp: E.scalar_tensor_tensor(
                          ep[pp][:, half * 512:(half + 1) * 512],
                          psum[pidx][:, 0:512], sc[pp][:, 1:2],
                          bias0_bc[:, half * 512:(half + 1) * 512],
                          op0=OP.mult, op1=OP.add),
                      reads=(psb[pidx], b_sc[pp], b_bias0), writes=(b_ep[pp],))
            if t == 0:
                tap("qT0", qT[0][:, 0:D], (b_qT[0],))
                tap("ep0", ep[0][:, 0:D], (b_ep[0],))
            kb.op("vector",
                  lambda E, t=t, pp=pp: E.tensor_tensor(
                      xt(t), xt(t), ep[pp][:], op=OP.add),
                  reads=(b_ep[pp], xb[t]), writes=(xb[t],))
        A.pop()

    tap("w0T", w0T[:, 0:8 * D], (b_w0T,))
    tap("bias0", bias0_bc[:, 0:D], (b_bias0,))


    def l0_attn():
        EPSF = 1e-6
        ln_stats_pass(lambda t: (xt(t), (xb[t],)))
        xh = [kb.buf(f"xh[{t}]") for t in range(NT)]
        for t in range(NT):
            kb.dma("sync", x_hbm[t * 128:(t + 1) * 128, :], xt(t),
                   reads=(xb[t],), writes=(xh[t],))
        A.push()
        A.rebase(0, limit=128 * 1024)
        kb.set_fence()

        W = {}
        bW = {}
        wstg = [A.alloc([128, D], F32, f"fa_wstg{j}") for j in range(2)]
        bwstg = [kb.buf(), kb.buf()]
        cast_engs = ("vector", "scalar", "gpsimd")
        nload = 0
        for nm, dnm in (("q", "L0_attn_Wq"), ("k", "L0_attn_Wk"),
                        ("v", "L0_attn_Wv"), ("o", "L0_attn_Wo")):
            W[nm] = A.alloc([128, 8 * D], BF16, f"W{nm}_r")
            bW[nm] = kb.buf(f"W{nm}")
            for i in range(8):
                j = nload % 2
                kb.dma("sync", wstg[j], dram[dnm][i * 128:(i + 1) * 128, :],
                       writes=(bwstg[j],))
                eng = cast_engs[nload % 3]
                if eng == "scalar":
                    fn = lambda E, nm=nm, i=i, j=j: E.copy(
                        W[nm][:, i * D:(i + 1) * D], wstg[j])
                else:
                    fn = lambda E, nm=nm, i=i, j=j: E.tensor_copy(
                        W[nm][:, i * D:(i + 1) * D], wstg[j])
                kb.op(eng, fn, reads=(bwstg[j],), writes=(bW[nm],))
                nload += 1
        set_bias_stage(D)
        BR = {}
        for nm, dnm in (("q", "L0_attn_bq"), ("k", "L0_attn_bk"),
                        ("v", "L0_attn_bv"), ("o", "L0_attn_bo")):
            BR[nm] = bias_rows("l0b" + nm, dram[dnm][None, :], D)

        A.rebase(TAIL_OFF)
        xtile = A.alloc([128, D], F32, "fa_x")
        lnx_bf = A.alloc([128, D], BF16, "fa_lnx")
        lnxT = A.alloc([128, D], BF16, "fa_lnxT")
        t_bf = A.alloc([128, D], BF16, "fa_tanh")
        q_bf = A.alloc([128, D], BF16, "fa_q")
        k_bf = A.alloc([128, D], BF16, "fa_k")
        v_bf = A.alloc([128, D], BF16, "fa_v")
        qe_bf = A.alloc([128, D], BF16, "fa_qe")
        ke_bf = A.alloc([128, D], BF16, "fa_ke")
        kT = A.alloc([128, D], BF16, "fa_kT")
        qsiT = A.alloc([128, D], BF16, "fa_qsiT")
        kso_bf = A.alloc([128, D], BF16, "fa_kso")
        qsir_bf = A.alloc([128, D], BF16, "fa_qsir")
        vsc_bf = A.alloc([128, D], BF16, "fa_vsc")
        scbf = A.alloc([128, 48], BF16, "fa_scbf")
        bc_t = A.alloc([128, D], BF16, "fa_bc")
        sobc = sibc = scbc = bc_t
        tmp_bf = A.alloc([128, D], BF16, "fa_tmp")
        att_bf = A.alloc([128, D], BF16, "fa_att")
        attT = A.alloc([128, D], BF16, "fa_attT")
        Sm_bf = A.alloc([128, 128], BF16, "fa_Sm")
        KV_sb = A.alloc([128, 512], BF16, "fa_KVsb")
        KV_stg = A.alloc([64, 512], BF16, "fa_KVstg")
        cols = A.alloc([128, 16 * 13], F32, "fa_cols")
        col = lambda i: cols[:, i * 16:(i + 1) * 16]
        cs_bf = A.alloc([128, 16], BF16, "fa_csbf")
        csp_sb = A.alloc([128, 16], BF16, "fa_cspsb")
        e_sb = A.alloc([128, 32], F32, "fa_esb")
        eT_bf = A.alloc([8, 4 * 128], BF16, "fa_eT")

        bufs = {}
        for nm in ("xtile lnx lnxT tanh q k v qe ke kT qsiT kso qsir "
                   "vsc tmp att attT Sm KVsb KVstg csbf cspsb esb eT cols "
                   "sob sib scb bc").split():
            bufs[nm] = kb.buf("fa_" + nm)
        B = bufs

        def hd(ap):
            return ap.rearrange("p (h d) -> p h d", d=64)

        def dh(ap):
            return ap.rearrange("p (d h) -> p h d", h=16)

        def bcast(dst, dst_buf, src_cols, src_buf):
            f = src_cols
            src = bass.AP(f.tensor, f.offset,
                          [[f.ap[0][0], 128], [0, 64], [1, 16]])
            kb.dma("gpsimd", dst.rearrange("p (d h) -> p d h", h=16), src,
                   reads=(src_buf,), writes=(dst_buf,),
                   allow_slow_non_contiguous=True)

        names4 = ("k", "q", "kso", "qsi")

        for c in range(FA_CHUNKS):
            tok0 = c * 128
            kb.dma("sync", xtile, x_hbm[tok0:tok0 + 128, :],
                   reads=(xh[c],), writes=(B["xtile"],))
            kb.op("vector",
                  lambda E, c=c: E.tensor_scalar(
                      lnx_bf, xtile, st_mu[:, c:c + 1], st_rstd[:, c:c + 1],
                      op0=OP.subtract, op1=OP.mult),
                  reads=(B["xtile"], sb_mu, sb_rstd), writes=(B["lnx"],))
            for i in range(8):
                kb.dmat(kb.tq(), lnxT[:, i * 128:(i + 1) * 128],
                        lnx_bf[:, i * 128:(i + 1) * 128],
                        reads=(B["lnx"],), writes=(B["lnxT"],))
            def proj(nm, pb):
                for half in range(2):
                    pidx = pb[half]
                    for i in range(8):
                        kb.op("tensor",
                              lambda E, i=i, half=half, pidx=pidx, nm=nm: E.matmul(
                                  psum[pidx][:, 0:512],
                                  lnxT[:, i * 128:(i + 1) * 128],
                                  W[nm][:, i * D + half * 512:i * D + half * 512 + 512],
                                  start=(i == 0), stop=False),
                              reads=(B["lnxT"], bW[nm]), writes=(psb[pidx],))
                    kb.op("tensor",
                          lambda E, half=half, pidx=pidx, nm=nm: E.matmul(
                              psum[pidx][:, 0:512], ones2[:],
                              BR[nm][0][:, half * 512:half * 512 + 512],
                              start=False, stop=True),
                          reads=(b_ones2, BR[nm][1]), writes=(psb[pidx],))

            def sig_evac(pb, dst, dbuf):
                for half in range(2):
                    kb.op("scalar",
                          lambda E, half=half, pb=pb: E.activation(
                              t_bf[:, half * 512:half * 512 + 512],
                              psum[pb[half]][:, 0:512], AF.Tanh, scale=0.5),
                          reads=(psb[pb[half]],), writes=(B["tanh"],))
                kb.op("vector",
                      lambda E, dst=dst: E.tensor_scalar(dst, t_bf, 0.5, 0.5,
                                                         op0=OP.mult, op1=OP.add),
                      reads=(B["tanh"],), writes=(dbuf,))

            proj("q", (0, 1))
            proj("k", (2, 3))
            sig_evac((0, 1), q_bf, B["q"])
            sig_evac((2, 3), k_bf, B["k"])
            proj("v", (0, 1))
            for half in range(2):
                kb.op("scalar",
                      lambda E, half=half: E.copy(
                          v_bf[:, half * 512:half * 512 + 512],
                          psum[0 + half][:, 0:512]),
                      reads=(psb[0 + half],), writes=(B["v"],))
            kb.op("gpsimd",
                  lambda E: E.tensor_scalar(qe_bf, q_bf, EPSF, None, op0=OP.add),
                  reads=(B["q"],), writes=(B["qe"],))
            kb.op("gpsimd",
                  lambda E: E.tensor_scalar(ke_bf, k_bf, EPSF, None, op0=OP.add),
                  reads=(B["k"],), writes=(B["ke"],))

            def cum_group(pidx, rhs_ap, rhs_buf, eT_slot, c):
                kb.op("tensor",
                      lambda E: E.matmul(psum[pidx][:, 0:512], consts["triu"],
                                         rhs_ap, start=True, stop=True,
                                         skip_group_check=True),
                      reads=(cb["triu"], rhs_buf), writes=(psb[pidx],))
                if c > 0:
                    for ii in range(4):
                        i = (pidx % 2) * 4 + ii
                        kb.op("tensor",
                              lambda E, i=i, ii=ii: E.matmul(
                                  psum[pidx][:, ii * 128:ii * 128 + 128],
                                  consts["onehot8"][:, i * 128:(i + 1) * 128],
                                  eT_bf[:, eT_slot * 128:(eT_slot + 1) * 128],
                                  start=False, stop=True,
                                  skip_group_check=True),
                              reads=(cb["onehot8"], B["eT"]), writes=(psb[pidx],))

            for half in range(2):
                cum_group(half, k_bf[:, half * 512:half * 512 + 512], B["k"], 0, c)
                cum_group(2 + half, q_bf[:, half * 512:half * 512 + 512], B["q"], 1, c)

            if c == 1:
                tap("fa_eT", eT_bf[0:8, 0:4 * 128], (B["eT"],))
            if c in (0, 1) and DEBUG:
                cumdump = A
                kb.op("vector", lambda E: E.tensor_copy(dump[:, 0:512], psum[0][:, 0:512]),
                      reads=(psb[0],), writes=(b_dump,))
                kb.op("vector", lambda E: E.tensor_copy(dump[:, 512:1024], psum[1][:, 0:512]),
                      reads=(psb[1],), writes=(b_dump,))
                tap("fa_cumk" if c == 0 else "fa_cumk1", dump[:, 0:D], (b_dump,))
                kb.op("vector", lambda E: E.tensor_copy(dump[:, 0:512], psum[2][:, 0:512]),
                      reads=(psb[2],), writes=(b_dump,))
                kb.op("vector", lambda E: E.tensor_copy(dump[:, 512:1024], psum[3][:, 0:512]),
                      reads=(psb[3],), writes=(b_dump,))
                tap("fa_cumq" if c == 0 else "fa_cumq1", dump[:, 0:D], (b_dump,))
            for half in range(2):
                kb.op("vector",
                      lambda E, half=half: E.scalar_tensor_tensor(
                          tmp_bf[:, half * 512:half * 512 + 512],
                          psum[half][:, 0:512], EPSF,
                          qe_bf[:, half * 512:half * 512 + 512],
                          op0=OP.add, op1=OP.mult),
                      reads=(psb[half], B["qe"]), writes=(B["tmp"],))
            kb.op("vector",
                  lambda E: E.tensor_reduce(
                      col(0).unsqueeze(2),
                      tmp_bf.rearrange("p (h d) -> p h d", d=64),
                      axis=AX.X, op=OP.add),
                  reads=(B["tmp"],), writes=(B["cols"],))
            for half in range(2):
                kb.op("vector",
                      lambda E, half=half: E.scalar_tensor_tensor(
                          tmp_bf[:, half * 512:half * 512 + 512],
                          psum[2 + half][:, 0:512], EPSF,
                          ke_bf[:, half * 512:half * 512 + 512],
                          op0=OP.add, op1=OP.mult),
                      reads=(psb[2 + half], B["ke"]), writes=(B["tmp"],))
            kb.op("vector",
                  lambda E: E.tensor_reduce(
                      col(1).unsqueeze(2),
                      tmp_bf.rearrange("p (h d) -> p h d", d=64),
                      axis=AX.X, op=OP.add),
                  reads=(B["tmp"],), writes=(B["cols"],))
            kb.op("vector", lambda E: E.reciprocal(col(2), col(0)),
                  reads=(B["cols"],), writes=(B["cols"],))
            kb.op("vector", lambda E: E.reciprocal(col(3), col(1)),
                  reads=(B["cols"],), writes=(B["cols"],))
            kb.op("vector",
                  lambda E, c=c: E.tensor_scalar(col(4), col(2),
                                                 consts["normc"][:, c:c + 1], None,
                                                 op0=OP.mult),
                  reads=(B["cols"], cb["normc"]), writes=(B["cols"],))
            kb.op("vector",
                  lambda E, c=c: E.tensor_scalar(col(5), col(3),
                                                 consts["normc"][:, c:c + 1], None,
                                                 op0=OP.mult),
                  reads=(B["cols"], cb["normc"]), writes=(B["cols"],))
            kb.op("gpsimd", lambda E: E.tensor_copy(scbf[:, 0:16], col(5)),
                  reads=(B["cols"],), writes=(B["sob"],))
            kb.op("scalar", lambda E: E.copy(scbf[:, 16:32], col(4)),
                  reads=(B["cols"],), writes=(B["sib"],))
            bcast(sobc, B["bc"], scbf[:, 0:16], B["sob"])
            bcast(sibc, B["bc"], scbf[:, 16:32], B["sib"])
            kb.op("vector",
                  lambda E: E.tensor_tensor(hd(kso_bf), hd(k_bf), dh(sobc),
                                            op=OP.mult),
                  reads=(B["k"], B["bc"]), writes=(B["kso"],))
            kb.op("vector",
                  lambda E: E.tensor_tensor(hd(qsir_bf), hd(q_bf), dh(sibc),
                                            op=OP.mult),
                  reads=(B["q"], B["bc"]), writes=(B["qsir"],))

            for half in range(2):
                cum_group(half, kso_bf[:, half * 512:half * 512 + 512], B["kso"], 2, c)
                cum_group(2 + half, qsir_bf[:, half * 512:half * 512 + 512], B["qsir"], 3, c)

            for half in range(2):
                kb.op("vector",
                      lambda E, half=half: E.scalar_tensor_tensor(
                          tmp_bf[:, half * 512:half * 512 + 512],
                          psum[half][:, 0:512], EPSF,
                          qe_bf[:, half * 512:half * 512 + 512],
                          op0=OP.add, op1=OP.mult),
                      reads=(psb[half], B["qe"]), writes=(B["tmp"],))
            kb.op("vector",
                  lambda E: E.tensor_reduce(
                      col(6).unsqueeze(2),
                      tmp_bf.rearrange("p (h d) -> p h d", d=64),
                      axis=AX.X, op=OP.add),
                  reads=(B["tmp"],), writes=(B["cols"],))
            for half in range(2):
                kb.op("vector",
                      lambda E, half=half: E.scalar_tensor_tensor(
                          tmp_bf[:, half * 512:half * 512 + 512],
                          psum[2 + half][:, 0:512], EPSF,
                          ke_bf[:, half * 512:half * 512 + 512],
                          op0=OP.add, op1=OP.mult),
                      reads=(psb[2 + half], B["ke"]), writes=(B["tmp"],))
            kb.op("vector",
                  lambda E: E.tensor_reduce(
                      col(7).unsqueeze(2),
                      tmp_bf.rearrange("p (h d) -> p h d", d=64),
                      axis=AX.X, op=OP.add),
                  reads=(B["tmp"],), writes=(B["cols"],))
            kb.op("vector",
                  lambda E, c=c: E.tensor_scalar(col(6), col(6),
                                                 consts["rnormc"][:, c:c + 1], None,
                                                 op0=OP.mult),
                  reads=(B["cols"], cb["rnormc"]), writes=(B["cols"],))
            kb.op("vector",
                  lambda E, c=c: E.tensor_scalar(col(7), col(7),
                                                 consts["rnormc"][:, c:c + 1], None,
                                                 op0=OP.mult),
                  reads=(B["cols"], cb["rnormc"]), writes=(B["cols"],))
            kb.op("scalar",
                  lambda E: E.activation(col(9), col(6), AF.Tanh, scale=0.5),
                  reads=(B["cols"],), writes=(B["cols"],))
            kb.op("vector",
                  lambda E: E.tensor_scalar(col(9), col(9), 0.5, 0.5,
                                            op0=OP.mult, op1=OP.add),
                  reads=(B["cols"],), writes=(B["cols"],))
            kb.op("vector",
                  lambda E: E.tensor_tensor(col(12), col(9), col(2), op=OP.mult),
                  reads=(B["cols"],), writes=(B["cols"],))
            kb.op("scalar",
                  lambda E: E.activation(col(8), col(7), AF.Exp),
                  reads=(B["cols"],), writes=(B["cols"],))
            kb.op("vector", lambda E: E.tensor_copy(cs_bf, col(8)),
                  reads=(B["cols"],), writes=(B["csbf"],))

            if c > 0:
                kb.op("vector", lambda E: E.tensor_copy(csp_sb, psum[7][:, 32:48]),
                      reads=(psb[7],), writes=(B["cspsb"],))
            kb.op("tensor",
                  lambda E: E.matmul(psum[5][:, 0:16], consts["triu"], cs_bf,
                                     start=True, stop=True,
                                     skip_group_check=True),
                  reads=(cb["triu"], B["csbf"]), writes=(psb[5],))
            if c > 0:
                kb.op("tensor",
                      lambda E: E.matmul(psum[5][:, 0:16], consts["ones128"],
                                         csp_sb, start=False, stop=True,
                                         skip_group_check=True),
                      reads=(cb["ones128"], B["cspsb"]), writes=(psb[5],))
            kb.op("vector", lambda E: E.tensor_copy(col(10), psum[5][:, 0:16]),
                  reads=(psb[5],), writes=(B["cols"],))
            kb.op("vector", lambda E: E.reciprocal(col(10), col(10)),
                  reads=(B["cols"],), writes=(B["cols"],))
            kb.op("vector",
                  lambda E: E.tensor_tensor(col(11), col(8), col(10), op=OP.mult),
                  reads=(B["cols"],), writes=(B["cols"],))
            kb.op("vector",
                  lambda E, c=c: E.tensor_scalar(col(11), col(11),
                                                 consts["normc"][:, c:c + 1], None,
                                                 op0=OP.mult),
                  reads=(B["cols"], cb["normc"]), writes=(B["cols"],))
            kb.op("tensor",
                  lambda E, c=c: E.matmul(psum[7][:, 32:48], consts["ident"],
                                          cs_bf, start=(c == 0), stop=True,
                                          skip_group_check=True),
                  reads=(cb["ident"], B["csbf"]), writes=(psb[7],))
            kb.op("gpsimd", lambda E: E.tensor_copy(scbf[:, 32:48], col(11)),
                  reads=(B["cols"],), writes=(B["scb"],))
            bcast(scbc, B["bc"], scbf[:, 32:48], B["scb"])
            kb.op("vector",
                  lambda E: E.tensor_tensor(hd(vsc_bf), hd(v_bf), dh(scbc),
                                            op=OP.mult),
                  reads=(B["v"], B["bc"]), writes=(B["vsc"],))

            for i in range(8):
                kb.dmat(kb.tq(), kT[:, i * 128:(i + 1) * 128],
                        k_bf[:, i * 128:(i + 1) * 128],
                        reads=(B["k"],), writes=(B["kT"],))
                kb.dmat(kb.tq(), qsiT[:, i * 128:(i + 1) * 128],
                        q_bf[:, i * 128:(i + 1) * 128],
                        reads=(B["q"],), writes=(B["qsiT"],))
            if c > 0:
                kb.op("vector",
                      lambda E: E.tensor_copy(KV_sb[0:64, :], psum[6][0:64, 0:512]),
                      reads=(psb[6],), writes=(B["KVsb"],))
                kb.op("vector",
                      lambda E: E.tensor_copy(KV_stg[:], psum[4][0:64, 0:512]),
                      reads=(psb[4],), writes=(B["KVstg"],))
                kb.dma("sync", KV_sb[64:128, :], KV_stg[:],
                       reads=(B["KVstg"],), writes=(B["KVsb"],))

            if c == 1:
                tap("fa_KV", KV_sb[:, 0:512], (B["KVsb"],))
            for h in range(NH):
                i_ = h // 2
                r0 = (h % 2) * 64
                kb.op("tensor",
                      lambda E, i_=i_, r0=r0: E.matmul(
                          psum[5][:, 0:128],
                          kT[r0:r0 + 64, i_ * 128:(i_ + 1) * 128],
                          qsiT[r0:r0 + 64, i_ * 128:(i_ + 1) * 128],
                          start=True, stop=True),
                      reads=(B["kT"], B["qsiT"]), writes=(psb[5],))
                kb.op("vector",
                      lambda E: E.tensor_tensor(Sm_bf, psum[5][:, 0:128],
                                                consts["triu"], op=OP.mult),
                      reads=(psb[5], cb["triu"]), writes=(B["Sm"],))
                kb.op("tensor",
                      lambda E, h=h: E.matmul(
                          psum[0][:, 0:64], Sm_bf,
                          vsc_bf[:, h * 64:(h + 1) * 64],
                          start=True, stop=True),
                      reads=(B["Sm"], B["vsc"]), writes=(psb[0],))
                if c > 0:
                    kb.op("tensor",
                          lambda E, h=h, i_=i_, r0=r0: E.matmul(
                              psum[0][:, 0:64],
                              qsiT[r0:r0 + 64, i_ * 128:(i_ + 1) * 128],
                              KV_sb[r0:r0 + 64, i_ * 64:(i_ + 1) * 64],
                              start=False, stop=True,
                              skip_group_check=True),
                          reads=(B["qsiT"], B["KVsb"]), writes=(psb[0],))
                kb.op("vector",
                      lambda E, h=h: E.tensor_scalar(
                          att_bf[:, h * 64:(h + 1) * 64], psum[0][:, 0:64],
                          col(12)[:, h:h + 1], None, op0=OP.mult),
                      reads=(psb[0], B["cols"]), writes=(B["att"],))
                kvb = 6 if h % 2 == 0 else 4
                kb.op("tensor",
                      lambda E, h=h, i_=i_, c=c, kvb=kvb: E.matmul(
                          psum[kvb][0:64, i_ * 64:(i_ + 1) * 64],
                          k_bf[:, h * 64:(h + 1) * 64],
                          vsc_bf[:, h * 64:(h + 1) * 64],
                          start=(c == 0 and h < 2), stop=True,
                          skip_group_check=True),
                      reads=(B["k"], B["vsc"]), writes=(psb[kvb],))

            srcs = {"k": (k_bf, "k"), "q": (q_bf, "q"),
                    "kso": (kso_bf, "kso"), "qsi": (qsir_bf, "qsir")}
            for qi_, nm in enumerate(names4):
                ap_, bn = srcs[nm]
                for i in range(8):
                    kb.op("tensor",
                          lambda E, ap_=ap_, i=i, qi_=qi_, c=c: E.matmul(
                              psum[7][:, qi_ * 8 + i:qi_ * 8 + i + 1],
                              ap_[:, i * 128:(i + 1) * 128],
                              consts["ones_col"], start=False, stop=True,
                              skip_group_check=True),
                          reads=(bufs[bn], cb["ones_col"]), writes=(psb[7],))
            if c < NT - 1:
                kb.op("vector", lambda E: E.tensor_copy(e_sb, psum[7][:, 0:32]),
                      reads=(psb[7],), writes=(B["esb"],))
                for qi_ in range(4):
                    kb.op("tensor",
                          lambda E, qi_=qi_: E.transpose(
                              psum[5][0:8, qi_ * 128:qi_ * 128 + 128],
                              e_sb[:, qi_ * 8:(qi_ + 1) * 8], consts["identf32"]),
                          reads=(B["esb"], cb["identf32"]), writes=(psb[5],))
                    kb.op("vector",
                          lambda E, qi_=qi_: E.tensor_copy(
                              eT_bf[:, qi_ * 128:(qi_ + 1) * 128],
                              psum[5][0:8, qi_ * 128:qi_ * 128 + 128]),
                          reads=(psb[5],), writes=(B["eT"],))

            if c == (1 if FA_CHUNKS > 1 else 0):
                tap("fa_qe", qe_bf[:, 0:D], (B["qe"],))
                tap("fa_tmp", tmp_bf[:, 0:D], (B["tmp"],))
                tap("fa_q", q_bf[:, 0:D], (B["q"],))
                tap("fa_k", k_bf[:, 0:D], (B["k"],))
                tap("fa_v", v_bf[:, 0:D], (B["v"],))
                tap("fa_cols", cols[:, 0:16 * 12], (B["cols"],))
                tap("fa_kso", kso_bf[:, 0:D], (B["kso"],))
                tap("fa_vsc", vsc_bf[:, 0:D], (B["vsc"],))
                tap("fa_att", att_bf[:, 0:D], (B["att"],))
            for i in range(8):
                kb.dmat(kb.tq(), attT[:, i * 128:(i + 1) * 128],
                        att_bf[:, i * 128:(i + 1) * 128],
                        reads=(B["att"],), writes=(B["attT"],))
            for half in range(2):
                pidx = 2 + half
                for i in range(8):
                    kb.op("tensor",
                          lambda E, i=i, half=half, pidx=pidx: E.matmul(
                              psum[pidx][:, 0:512],
                              attT[:, i * 128:(i + 1) * 128],
                              W["o"][:, i * D + half * 512:i * D + half * 512 + 512],
                              start=(i == 0), stop=False),
                          reads=(B["attT"], bW["o"]), writes=(psb[pidx],))
                kb.op("tensor",
                      lambda E, half=half, pidx=pidx: E.matmul(
                          psum[pidx][:, 0:512], ones2[:],
                          BR["o"][0][:, half * 512:half * 512 + 512],
                          start=False, stop=True),
                      reads=(b_ones2, BR["o"][1]), writes=(psb[pidx],))
                kb.op("vector",
                      lambda E, half=half, pidx=pidx: E.tensor_tensor(
                          xtile[:, half * 512:half * 512 + 512],
                          xtile[:, half * 512:half * 512 + 512],
                          psum[pidx][:, 0:512], op=OP.add),
                      reads=(psb[pidx],), writes=(B["xtile"],))
            kb.dma("sync", x_hbm[tok0:tok0 + 128, :], xtile,
                   reads=(B["xtile"],), writes=(xh[c],))
        A.pop()
        return xh

    if STAGES >= 1:
        l0_ffn()



    def hbm_reader(n=2):
        tiles = [A.alloc([128, D], F32, f"xr{j}") for j in range(n)]
        tb = [kb.buf() for j in range(n)]

        def read_tile(t):
            j = t % n
            kb.dma("sync", tiles[j], x_hbm[t * 128:(t + 1) * 128, :],
                   reads=(xh[t],), writes=(tb[j],))
            return tiles[j], (tb[j],)
        return read_tile


    def _stream_rhs_mm(w_dram, rows0, cols0, ncols, lhsT_ap, lhsT_buf, m_rows,
                       pidx, stg, b_stg, stgbf, b_stgbf, extra_reads=()):
        for i in range(8):
            kb.dma("sync", stg[:, 0:ncols],
                   w_dram[rows0 + i * 128:rows0 + (i + 1) * 128,
                          cols0:cols0 + ncols],
                   writes=(b_stg,))
            kb.op("vector", lambda E, ncols=ncols: E.tensor_copy(
                stgbf[:, 0:ncols], stg[:, 0:ncols]),
                reads=(b_stg,), writes=(b_stgbf,))
            kb.op("tensor",
                  lambda E, i=i, pidx=pidx, m_rows=m_rows, ncols=ncols: E.matmul(
                      psum[pidx][0:m_rows, 0:ncols],
                      lhsT_ap[:, i * m_rows:(i + 1) * m_rows],
                      stgbf[:, 0:ncols], start=(i == 0), stop=False),
                  reads=(b_stgbf, lhsT_buf) + tuple(extra_reads),
                  writes=(psb[pidx],))

    def bias_mm(pidx, m_rows, rows_t, cols0, ncols):
        kb.op("tensor",
              lambda E: E.matmul(psum[pidx][0:m_rows, 0:ncols],
                                 ones2[:, 0:m_rows],
                                 rows_t[0][:, cols0:cols0 + ncols],
                                 start=False, stop=True),
              reads=(b_ones2, rows_t[1]), writes=(psb[pidx],))

    def l1_ffn_prep():
        kb.set_fence()
        A.push()
        A.rebase(0, limit=128 * 1024)
        p = l1f
        p["w1a"] = A.alloc([128, 8 * 512], BF16, "lf_w1a")
        p["b_w1a"] = kb.buf()
        p["w2"] = A.alloc([128, 4 * 64], BF16, "lf_w2")
        p["b_w2"] = kb.buf()
        p["Sini"] = A.alloc([64, D], F32, "lf_Sini")
        p["b_Sini"] = kb.buf()
        p["SiniT"] = A.alloc([128, 512], BF16, "lf_SiniT")
        p["b_SiniT"] = kb.buf()
        p["gh"] = A.alloc([64, 3 * D], BF16, "lf_gh")
        p["b_gh"] = kb.buf()
        p["stau"] = A.alloc([128, 1], F32, "lf_stau")
        p["b_stau"] = kb.buf()
        p["lngb"] = A.alloc([64, 2 * D], F32, "lf_lngb")
        p["b_lngb"] = kb.buf()
        p["logits"] = A.alloc([128, NT * 64], F32, "lf_logits")
        p["top8"] = A.alloc([128, NT * 8], F32, "lf_top8")
        p["alpha"] = A.alloc([128, NT * 64], BF16, "lf_alpha")
        p["alphaT"] = A.alloc([64, NT * 128], BF16, "lf_alphaT")
        p["hb"] = A.alloc([2, 512], BF16, "lf_hb")
        p["b_hb"] = kb.buf()
        A.push()
        A.rebase(TAIL_OFF)
        set_bias_stage(3 * D)
        A.pop()
        p["bih"] = bias_rows("lf_bih", dram["L1_ffn_bih"][None, :], 3 * D)
        p["bv"] = bias_rows("lf_bv", dram["L1_ffn_bv"][None, :], D)
        p["bvp"] = bias_rows("lf_bvp", dram["L1_ffn_bvp"][None, :], D)
        p["bo"] = bias_rows("lf_bo", dram["L1_ffn_bo"][None, :], D)
        p["b2r"] = bias_rows("lf_b2", dram["L1_ffn_b2"][None, :], 64)
        b1r = bias_rows("lf_b1", dram["L1_ffn_b1"][None, :], 512)
        bhh = bias_rows("lf_bhh", dram["L1_ffn_bhh"][None, :], 3 * D)
        l1f["end"] = A.off
        kb.set_fence()
        A.push()
        A.rebase(TAIL_OFF)
        stg = A.alloc([128, D], F32, "lf_stg")
        b_stg = kb.buf()
        stgbf = A.alloc([128, D], BF16, "lf_stgbf")
        b_stgbf = kb.buf()
        for i in range(8):
            kb.dma("sync", stg[:, 0:512],
                   dram["L1_ffn_w1"][i * 128:(i + 1) * 128, :], writes=(b_stg,))
            kb.op("vector", lambda E, i=i: E.tensor_copy(
                p["w1a"][:, i * 512:(i + 1) * 512], stg[:, 0:512]),
                reads=(b_stg,), writes=(p["b_w1a"],))
        for i in range(4):
            kb.dma("sync", stg[:, 0:64],
                   dram["L1_ffn_w2"][i * 128:(i + 1) * 128, :], writes=(b_stg,))
            kb.op("vector", lambda E, i=i: E.tensor_copy(
                p["w2"][:, i * 64:(i + 1) * 64], stg[:, 0:64]),
                reads=(b_stg,), writes=(p["b_w2"],))
        kb.dma("sync", p["Sini"], dram["L1_ffn_slot_init"], writes=(p["b_Sini"],))
        kb.op("vector", lambda E: E.tensor_copy(stgbf[0:64, :], p["Sini"]),
              reads=(p["b_Sini"],), writes=(b_stgbf,))
        for i in range(8):
            kb.dmat(kb.tq(), p["SiniT"][:, i * 64:(i + 1) * 64],
                    stgbf[0:64, i * 128:(i + 1) * 128],
                    reads=(b_stgbf,), writes=(p["b_SiniT"],))
        kb.dma("sync", p["lngb"][0:64, 0:D],
               dram["L1_ffn_ln_g"][None, :].partition_broadcast(64),
               writes=(p["b_lngb"],))
        kb.dma("sync", p["lngb"][0:64, D:2 * D],
               dram["L1_ffn_ln_b"][None, :].partition_broadcast(64),
               writes=(p["b_lngb"],))
        tau1 = A.alloc([1, 4], F32, "lf_tau")
        b_tau = kb.buf()
        kb.dma("sync", tau1[0:1, 0:1], dram["L1_ffn_tau"][None, :], writes=(b_tau,))
        kb.op("scalar", lambda E: E.activation(
            tau1[0:1, 1:2], tau1[0:1, 0:1], AF.Abs),
            reads=(b_tau,), writes=(b_tau,))
        kb.op("vector", lambda E: E.tensor_scalar(
            tau1[0:1, 1:2], tau1[0:1, 1:2], 0.1, None, op0=OP.add),
            reads=(b_tau,), writes=(b_tau,))
        kb.op("vector", lambda E: E.reciprocal(tau1[0:1, 2:3], tau1[0:1, 1:2]),
              reads=(b_tau,), writes=(b_tau,))
        kb.dma("sync", row_hbm[0:1, 0:1], tau1[0:1, 2:3], reads=(b_tau,),
               writes=(b_tau,))
        kb.dma("sync", p["stau"], row_hbm[0:1, 0:1].partition_broadcast(128),
               reads=(b_tau,), writes=(p["b_stau"],))
        kb.op("tensor",
              lambda E: E.matmul(psum[0][0:1, 0:512], consts["o64_col"],
                                 stgbf[0:64, 0:512], start=True, stop=True),
              reads=(b_stgbf, cb["o64_col"]), writes=(psb[0],))
        kb.op("tensor",
              lambda E: E.matmul(psum[1][0:1, 0:512], consts["o64_col"],
                                 stgbf[0:64, 512:1024], start=True, stop=True),
              reads=(b_stgbf, cb["o64_col"]), writes=(psb[1],))
        sm = A.alloc([1, D], F32, "lf_sm")
        b_sm = kb.buf()
        kb.op("vector", lambda E: E.tensor_scalar(
            sm[0:1, 0:512], psum[0][0:1, 0:512], 1.0 / 64, None, op0=OP.mult),
            reads=(psb[0],), writes=(b_sm,))
        kb.op("vector", lambda E: E.tensor_scalar(
            sm[0:1, 512:1024], psum[1][0:1, 0:512], 1.0 / 64, None, op0=OP.mult),
            reads=(psb[1],), writes=(b_sm,))
        kb.dma("sync", row_hbm[1:2, 0:D], sm[0:1, :], reads=(b_sm,), writes=(b_sm,))
        smT = A.alloc([128, 8], F32, "lf_smT")
        smTb = A.alloc([128, 8], BF16, "lf_smTb")
        b_smT = kb.buf()
        for i in range(8):
            kb.dma("gpsimd", smT[:, i:i + 1],
                   row_hbm[1:2, i * 128:(i + 1) * 128].rearrange("a b -> b a"),
                   reads=(b_sm,), writes=(b_smT,), allow_slow_non_contiguous=True)
        kb.op("vector", lambda E: E.tensor_copy(smTb, smT),
              reads=(b_smT,), writes=(b_smT,))
        for i in range(8):
            kb.dma("sync", stg[:, 0:512],
                   dram["L1_ffn_w1"][D + i * 128:D + (i + 1) * 128, :],
                   writes=(b_stg,))
            kb.op("vector", lambda E: E.tensor_copy(stgbf[:, 0:512], stg[:, 0:512]),
                  reads=(b_stg,), writes=(b_stgbf,))
            kb.op("tensor",
                  lambda E, i=i: E.matmul(psum[2][0:1, 0:512], smTb[:, i:i + 1],
                                          stgbf[:, 0:512], start=(i == 0),
                                          stop=False),
                  reads=(b_smT, b_stgbf), writes=(psb[2],))
        bias_mm(2, 1, b1r, 0, 512)
        hbf = A.alloc([1, 512], F32, "lf_hbf")
        hblo = A.alloc([1, 512], BF16, "lf_hblo")
        b_hbf = kb.buf()
        kb.op("vector", lambda E: E.tensor_copy(hbf, psum[2][0:1, 0:512]),
              reads=(psb[2],), writes=(b_hbf,))
        kb.op("vector", lambda E: E.tensor_copy(p["hb"][0:1, :], hbf),
              reads=(b_hbf,), writes=(p["b_hb"],))
        kb.op("vector", lambda E: E.tensor_tensor(hblo, hbf, p["hb"][0:1, :],
                                                  op=OP.subtract),
              reads=(b_hbf, p["b_hb"]), writes=(b_hbf,))
        kb.dma("sync", p["hb"][1:2, :], hblo, reads=(b_hbf,), writes=(p["b_hb"],))
        _gru_mm(dram["L1_ffn_Whh"], p["SiniT"], p["b_SiniT"], bhh,
                p["gh"], p["b_gh"], stg, b_stg)
        A.pop()
        A.pop()

    def _gru_mm(w_dram, lhsT_ap, lhsT_buf, brows, out_sb, out_buf, stg, b_stg):
        A.push()
        wT = A.alloc([128, 8 * D], BF16, "gru_wT")
        b_wT = kb.buf()
        wbfs = A.alloc([128, D], BF16, "gru_wbf")
        b_wbfs = kb.buf()
        for g in range(3):
            for oc in range(8):
                kb.dma("sync", stg,
                       w_dram[g * D + oc * 128:g * D + (oc + 1) * 128, :],
                       writes=(b_stg,))
                kb.op("vector", lambda E: E.tensor_copy(wbfs, stg),
                      reads=(b_stg,), writes=(b_wbfs,))
                for i in range(8):
                    kb.dmat(kb.tq(), wT[:, i * D + oc * 128:i * D + (oc + 1) * 128],
                            wbfs[:, i * 128:(i + 1) * 128],
                            reads=(b_wbfs,), writes=(b_wT,))
            for half in range(2):
                pidx = half
                for i in range(8):
                    kb.op("tensor",
                          lambda E, i=i, half=half, pidx=pidx: E.matmul(
                              psum[pidx][0:64, 0:512],
                              lhsT_ap[:, i * 64:(i + 1) * 64],
                              wT[:, i * D + half * 512:i * D + half * 512 + 512],
                              start=(i == 0), stop=False),
                          reads=(lhsT_buf, b_wT), writes=(psb[pidx],))
                kb.op("tensor",
                      lambda E, half=half, pidx=pidx, g=g: E.matmul(
                          psum[pidx][0:64, 0:512], ones2[:, 0:64],
                          brows[0][:, g * D + half * 512:g * D + half * 512 + 512],
                          start=False, stop=True),
                      reads=(b_ones2, brows[1]), writes=(psb[pidx],))
                kb.op("vector",
                      lambda E, half=half, pidx=pidx, g=g: E.tensor_copy(
                          out_sb[0:64, g * D + half * 512:g * D + half * 512 + 512],
                          psum[pidx][0:64, 0:512]),
                      reads=(psb[pidx],), writes=(out_buf,))
        A.pop()

    def l1_ffn():
        p = l1f
        kb.set_fence()
        A.push()
        A.rebase(l1f["end"], limit=128 * 1024)
        lnx_bf = A.alloc([128, D], BF16, "l1_lnx")
        lnxT = A.alloc([128, D], BF16, "l1_lnxT")
        h_bf = A.alloc([128, 512], BF16, "l1_h")
        hT = A.alloc([128, 512], BF16, "l1_hT")
        t64 = A.alloc([128, 64], F32, "l1_t64")
        m64 = A.alloc([128, 8], F32, "l1_m64")
        cols = A.alloc([128, 8], F32, "l1_cols")
        alpha32 = A.alloc([128, 64], F32, "l1_a32")
        b_a32 = kb.buf()
        Bb = {nm: kb.buf("l1_" + nm) for nm in
              "lnx lnxT h hT t64 m64 cols".split()}
        A.push()
        rd_stats = hbm_reader()
        ln_stats_pass(rd_stats)
        rd_a = hbm_reader()
        for t in range(NT):
            xa, xab = rd_a(t)
            kb.op("vector",
                  lambda E, t=t, xa=xa: E.tensor_scalar(
                      lnx_bf, xa, st_mu[:, t:t + 1], st_rstd[:, t:t + 1],
                      op0=OP.subtract, op1=OP.mult),
                  reads=xab + (sb_mu, sb_rstd), writes=(Bb["lnx"],))
            for i in range(8):
                kb.dmat(kb.tq(), lnxT[:, i * 128:(i + 1) * 128],
                        lnx_bf[:, i * 128:(i + 1) * 128],
                        reads=(Bb["lnx"],), writes=(Bb["lnxT"],))
            for i in range(8):
                kb.op("tensor",
                      lambda E, i=i: E.matmul(
                          psum[0][:, 0:512], lnxT[:, i * 128:(i + 1) * 128],
                          p["w1a"][:, i * 512:(i + 1) * 512],
                          start=(i == 0), stop=False),
                      reads=(Bb["lnxT"], p["b_w1a"]), writes=(psb[0],))
            kb.op("tensor",
                  lambda E: E.matmul(psum[0][:, 0:512], ones2[:], p["hb"][:],
                                     start=False, stop=True),
                  reads=(b_ones2, p["b_hb"]), writes=(psb[0],))
            kb.op("scalar",
                  lambda E: E.activation(h_bf, psum[0][:, 0:512], AF.Gelu),
                  reads=(psb[0],), writes=(Bb["h"],))
            for i in range(4):
                kb.dmat(kb.tq(), hT[:, i * 128:(i + 1) * 128],
                        h_bf[:, i * 128:(i + 1) * 128],
                        reads=(Bb["h"],), writes=(Bb["hT"],))
            for i in range(4):
                kb.op("tensor",
                      lambda E, i=i: E.matmul(
                          psum[1][:, 0:64], hT[:, i * 128:(i + 1) * 128],
                          p["w2"][:, i * 64:(i + 1) * 64],
                          start=(i == 0), stop=False),
                      reads=(Bb["hT"], p["b_w2"]), writes=(psb[1],))
            bias_mm(1, 128, p["b2r"], 0, 64)
            kb.op("vector",
                  lambda E, t=t: E.tensor_scalar(
                      p["logits"][:, t * 64:(t + 1) * 64], psum[1][:, 0:64],
                      p["stau"][:, 0:1], None, op0=OP.mult),
                  reads=(psb[1], p["b_stau"]), writes=(kb.buf(),))
            kb.op("vector",
                  lambda E, t=t: E.max(p["top8"][:, t * 8:(t + 1) * 8],
                                       p["logits"][:, t * 64:(t + 1) * 64]),
                  reads=(), writes=(kb.buf(),))
        blog = kb.buf("logits_all")
        btop = kb.buf("top8_all")
        kb.op("vector", lambda E: E.engine_nop(), reads=(), writes=(blog, btop))
        rd_a2 = hbm_reader()
        for t in range(NT):
            lg = p["logits"][:, t * 64:(t + 1) * 64]
            t8 = p["top8"][:, t * 8:(t + 1) * 8]
            kb.op("vector",
                  lambda E, t8=t8: E.tensor_scalar(cols[:, 0:1], t8[:, 0:1], -1.0,
                                                   None, op0=OP.mult),
                  reads=(btop,), writes=(Bb["cols"],))
            kb.op("scalar",
                  lambda E, t8=t8: E.activation(m64[:, 0:8], t8, AF.Exp,
                                                bias=cols[:, 0:1],
                                                accum_out=cols[:, 1:2]),
                  reads=(btop, Bb["cols"]), writes=(Bb["m64"], Bb["cols"]))
            kb.op("scalar",
                  lambda E, lg=lg: E.activation(t64, lg, AF.Exp,
                                                bias=cols[:, 0:1]),
                  reads=(blog, Bb["cols"]), writes=(Bb["t64"],))
            kb.op("vector", lambda E: E.reciprocal(cols[:, 2:3], cols[:, 1:2]),
                  reads=(Bb["cols"],), writes=(Bb["cols"],))
            kb.op("vector",
                  lambda E, t8=t8: E.tensor_scalar(
                      m64[:, 0:1], t8[:, 7:8], 1.0, None, op0=OP.mult),
                  reads=(btop,), writes=(Bb["m64"],))
            kb.op("vector",
                  lambda E, lg=lg: E.scalar_tensor_tensor(
                      t64, lg, m64[:, 0:1], t64, op0=OP.is_ge, op1=OP.mult),
                  reads=(blog, Bb["m64"], Bb["t64"]), writes=(Bb["t64"],))
            kb.op("vector",
                  lambda E: E.tensor_scalar(alpha32, t64, cols[:, 2:3],
                                            None, op0=OP.mult),
                  reads=(Bb["t64"], Bb["cols"]), writes=(b_a32,))
            kb.op("vector",
                  lambda E, t=t: E.tensor_copy(
                      p["alpha"][:, t * 64:(t + 1) * 64], alpha32),
                  reads=(b_a32,), writes=(kb.buf(),))
            kb.op("tensor",
                  lambda E, t=t: E.transpose(
                      psum[2][0:64, 0:128], alpha32, consts["identf32"]),
                  reads=(b_a32, cb["identf32"]), writes=(psb[2],))
            kb.op("vector",
                  lambda E, t=t: E.tensor_copy(
                      p["alphaT"][0:64, t * 128:(t + 1) * 128],
                      psum[2][0:64, 0:128]),
                  reads=(psb[2],), writes=(kb.buf(),))
            xa, xab = rd_a2(t)
            kb.op("vector",
                  lambda E, t=t, xa=xa: E.tensor_scalar(
                      lnx_bf, xa, st_mu[:, t:t + 1], st_rstd[:, t:t + 1],
                      op0=OP.subtract, op1=OP.mult),
                  reads=xab + (sb_mu, sb_rstd), writes=(Bb["lnx"],))
            for half in range(2):
                kb.op("tensor",
                      lambda E, t=t, half=half: E.matmul(
                          psum[half][0:64, 0:512],
                          p["alpha"][:, t * 64:(t + 1) * 64],
                          lnx_bf[:, half * 512:half * 512 + 512],
                          start=(t == 0), stop=(t == NT - 1),
                          skip_group_check=True),
                      reads=(Bb["lnx"],), writes=(psb[half],))
            kb.op("tensor",
                  lambda E, t=t: E.matmul(
                      psum[3][0:64, 0:1], p["alpha"][:, t * 64:(t + 1) * 64],
                      consts["ones_col"], start=(t == 0), stop=(t == NT - 1),
                      skip_group_check=True),
                  reads=(cb["ones_col"],), writes=(psb[3],))
        balpha = kb.buf("alpha_all")
        balphaT = kb.buf("alphaT_all")
        kb.op("vector", lambda E: E.engine_nop(), reads=(), writes=(balpha, balphaT))

        A.pop()
        kb.set_fence()
        A.push()
        sl = A.alloc([64, D], F32, "l1_sl")
        sl2 = A.alloc([64, D], F32, "l1_sl2")
        sl3 = A.alloc([64, D], F32, "l1_sl3")
        Snew = A.alloc([64, D], F32, "l1_Snew")
        sbf = A.alloc([64, D], BF16, "l1_sbf")
        sT = A.alloc([128, 512], BF16, "l1_sT")
        qkT = A.alloc([128, 256], BF16, "l1_qkT")
        attn64 = A.alloc([64, 64], BF16, "l1_attn")
        sv_bf = A.alloc([64, D], BF16, "l1_sv")
        scol = A.alloc([64, 16], F32, "l1_scol")
        bqT = A.alloc([128, 8], F32, "l1_bqT")
        gi_sb = A.alloc([64, 3 * D], BF16, "l1_gi")
        stg2 = A.alloc([128, D], F32, "l1_stg")
        stg2bf = A.alloc([128, D], BF16, "l1_stgbf")
        S = {nm: kb.buf("l1s_" + nm) for nm in
             "sl sl2 sl3 Snew sbf sT qkT attn64 sv scol bqT gi stg stgbf".split()}

        def slot_T(src_ap, src_buf):
            kb.op("vector", lambda E, src_ap=src_ap: E.tensor_copy(sbf, src_ap),
                  reads=(src_buf,), writes=(S["sbf"],))
            for i in range(8):
                kb.dmat(kb.tq(), sT[:, i * 64:(i + 1) * 64],
                        sbf[0:64, i * 128:(i + 1) * 128],
                        reads=(S["sbf"],), writes=(S["sT"],))

        kb.op("vector", lambda E: E.tensor_scalar(
            scol[:, 0:1], psum[3][0:64, 0:1], 1e-8, None, op0=OP.add),
            reads=(psb[3],), writes=(S["scol"],))
        kb.op("vector", lambda E: E.reciprocal(scol[:, 1:2], scol[:, 0:1]),
              reads=(S["scol"],), writes=(S["scol"],))
        for half in range(2):
            kb.op("vector",
                  lambda E, half=half: E.tensor_scalar(
                      sl[0:64, half * 512:half * 512 + 512],
                      psum[half][0:64, 0:512], scol[:, 1:2], None, op0=OP.mult),
                  reads=(psb[half], S["scol"]), writes=(S["sl"],))
        slot_T(sl[0:64, :], S["sl"])
        _gru_mm(dram["L1_ffn_Wih"], sT, S["sT"], p["bih"], gi_sb, S["gi"],
                stg2, S["stg"])
        def gate_pre(g, dst, dbuf):
            kb.op("vector",
                  lambda E, g=g, dst=dst: E.tensor_tensor(
                      dst, gi_sb[0:64, g * D:(g + 1) * D],
                      p["gh"][0:64, g * D:(g + 1) * D], op=OP.add),
                  reads=(S["gi"], p["b_gh"]), writes=(dbuf,))

        def sigm(ap, buf):
            kb.op("scalar", lambda E: E.activation(ap, ap, AF.Tanh, scale=0.5),
                  reads=(buf,), writes=(buf,))
            kb.op("vector", lambda E: E.tensor_scalar(ap, ap, 0.5, 0.5,
                                                      op0=OP.mult, op1=OP.add),
                  reads=(buf,), writes=(buf,))

        gate_pre(0, sl2[0:64, :], S["sl2"])
        sigm(sl2[0:64, :], S["sl2"])
        gate_pre(1, sl3[0:64, :], S["sl3"])
        sigm(sl3[0:64, :], S["sl3"])
        kb.op("vector",
              lambda E: E.tensor_tensor(sl[0:64, :], sl2[0:64, :],
                                        p["gh"][0:64, 2 * D:3 * D], op=OP.mult),
              reads=(S["sl2"], p["b_gh"]), writes=(S["sl"],))
        kb.op("vector",
              lambda E: E.tensor_tensor(sl[0:64, :], sl[0:64, :],
                                        gi_sb[0:64, 2 * D:3 * D], op=OP.add),
              reads=(S["sl"], S["gi"]), writes=(S["sl"],))
        kb.op("scalar", lambda E: E.activation(sl[0:64, :], sl[0:64, :], AF.Tanh),
              reads=(S["sl"],), writes=(S["sl"],))
        kb.op("vector",
              lambda E: E.tensor_tensor(Snew[0:64, :], p["Sini"][0:64, :],
                                        sl[0:64, :], op=OP.subtract),
              reads=(p["b_Sini"], S["sl"]), writes=(S["Snew"],))
        kb.op("vector",
              lambda E: E.tensor_tensor(Snew[0:64, :], Snew[0:64, :],
                                        sl3[0:64, :], op=OP.mult),
              reads=(S["Snew"], S["sl3"]), writes=(S["Snew"],))
        kb.op("vector",
              lambda E: E.tensor_tensor(Snew[0:64, :], Snew[0:64, :],
                                        sl[0:64, :], op=OP.add),
              reads=(S["Snew"], S["sl"]), writes=(S["Snew"],))
        slot_T(Snew[0:64, :], S["Snew"])
        kb.dma("sync", row_hbm[2:3, 0:256], dram["L1_ffn_bq"][None, :],
               writes=(S["bqT"],))
        kb.dma("sync", row_hbm[3:4, 0:256], dram["L1_ffn_bk"][None, :],
               writes=(S["bqT"],))
        for i in range(2):
            kb.dma("gpsimd", bqT[:, i:i + 1],
                   row_hbm[2:3, i * 128:(i + 1) * 128].rearrange("a b -> b a"),
                   reads=(S["bqT"],), writes=(S["bqT"],),
                   allow_slow_non_contiguous=True)
            kb.dma("gpsimd", bqT[:, 2 + i:3 + i],
                   row_hbm[3:4, i * 128:(i + 1) * 128].rearrange("a b -> b a"),
                   reads=(S["bqT"],), writes=(S["bqT"],),
                   allow_slow_non_contiguous=True)
        kb.op("vector", lambda E: E.tensor_scalar(bqT[:, 0:2], bqT[:, 0:2],
                                                  0.0625, None, op0=OP.mult),
              reads=(S["bqT"],), writes=(S["bqT"],))
        for nm, base, bcol, sc_ in (("L1_ffn_wq", 0, 0, 0.0625),
                                    ("L1_ffn_wk", 128, 2, 1.0)):
            for i in range(8):
                kb.dma("sync", stg2[:, 0:256], dram[nm][i * 128:(i + 1) * 128, :],
                       writes=(S["stg"],))
                kb.op("vector", lambda E: E.tensor_copy(stg2bf[:, 0:256],
                                                        stg2[:, 0:256]),
                      reads=(S["stg"],), writes=(S["stgbf"],))
                for dqc in range(2):
                    kb.op("tensor",
                          lambda E, i=i, dqc=dqc: E.matmul(
                              psum[4 + dqc][0:128, 0:64],
                              stg2bf[:, dqc * 128:(dqc + 1) * 128],
                              sT[:, i * 64:(i + 1) * 64],
                              start=(i == 0), stop=(i == 7)),
                          reads=(S["stgbf"], S["sT"]), writes=(psb[4 + dqc],))
            for dqc in range(2):
                kb.op("scalar",
                      lambda E, dqc=dqc, base=base, sc_=sc_, bcol=bcol: E.activation(
                          qkT[:, base + dqc * 64:base + dqc * 64 + 64],
                          psum[4 + dqc][0:128, 0:64], AF.Identity,
                          bias=bqT[:, bcol + dqc:bcol + dqc + 1], scale=sc_),
                      reads=(psb[4 + dqc], S["bqT"]), writes=(S["qkT"],))
        for dqc in range(2):
            kb.op("tensor",
                  lambda E, dqc=dqc: E.matmul(
                      psum[2][0:64, 0:64], qkT[:, dqc * 64:dqc * 64 + 64],
                      qkT[:, 128 + dqc * 64:128 + dqc * 64 + 64],
                      start=(dqc == 0), stop=(dqc == 1)),
                  reads=(S["qkT"],), writes=(psb[2],))
        kb.op("vector",
              lambda E: E.reduce_max(scol[:, 2:3], psum[2][0:64, 0:64], axis=AX.X),
              reads=(psb[2],), writes=(S["scol"],))
        kb.op("vector", lambda E: E.tensor_scalar(scol[:, 2:3], scol[:, 2:3],
                                                  -1.0, None, op0=OP.mult),
              reads=(S["scol"],), writes=(S["scol"],))
        kb.op("scalar",
              lambda E: E.activation(sl2[0:64, 0:64], psum[2][0:64, 0:64], AF.Exp,
                                     bias=scol[:, 2:3], accum_out=scol[:, 3:4]),
              reads=(psb[2], S["scol"]), writes=(S["sl2"], S["scol"]))
        kb.op("vector", lambda E: E.reciprocal(scol[:, 4:5], scol[:, 3:4]),
              reads=(S["scol"],), writes=(S["scol"],))
        kb.op("vector",
              lambda E: E.tensor_scalar(sl2[0:64, 0:64], sl2[0:64, 0:64],
                                        scol[:, 4:5], None, op0=OP.mult),
              reads=(S["sl2"], S["scol"]), writes=(S["sl2"],))
        kb.op("tensor",
              lambda E: E.transpose(psum[2][0:64, 0:64], sl2[0:64, 0:64],
                                    consts["identf32"][0:64, 0:64]),
              reads=(S["sl2"], cb["identf32"]), writes=(psb[2],))
        kb.op("vector",
              lambda E: E.tensor_copy(attn64, psum[2][0:64, 0:64]),
              reads=(psb[2],), writes=(S["attn64"],))
        for half in range(2):
            _stream_rhs_mm(dram["L1_ffn_wv"], 0, half * 512, 512, sT, S["sT"],
                           64, 4 + half, stg2, S["stg"], stg2bf, S["stgbf"])
            bias_mm(4 + half, 64, p["bv"], half * 512, 512)
            kb.op("vector",
                  lambda E, half=half: E.tensor_copy(
                      sv_bf[0:64, half * 512:half * 512 + 512],
                      psum[4 + half][0:64, 0:512]),
                  reads=(psb[4 + half],), writes=(S["sv"],))
        for half in range(2):
            kb.op("tensor",
                  lambda E, half=half: E.matmul(
                      psum[4 + half][0:64, 0:512], attn64,
                      sv_bf[0:64, half * 512:half * 512 + 512],
                      start=True, stop=True),
                  reads=(S["attn64"], S["sv"]), writes=(psb[4 + half],))
            kb.op("vector",
                  lambda E, half=half: E.tensor_tensor(
                      sl[0:64, half * 512:half * 512 + 512],
                      Snew[0:64, half * 512:half * 512 + 512],
                      psum[4 + half][0:64, 0:512], op=OP.add),
                  reads=(S["Snew"], psb[4 + half]), writes=(S["sl"],))
        kb.op("vector", lambda E: E.reduce_sum(scol[:, 5:6], sl[0:64, :], axis=AX.X),
              reads=(S["sl"],), writes=(S["scol"],))
        kb.op("scalar",
              lambda E: E.activation(sl2[0:64, :], sl[0:64, :], AF.Square,
                                     accum_out=scol[:, 6:7]),
              reads=(S["sl"],), writes=(S["sl2"], S["scol"]))
        kb.op("vector", lambda E: E.tensor_scalar(scol[:, 7:8], scol[:, 5:6],
                                                  1.0 / D, None, op0=OP.mult),
              reads=(S["scol"],), writes=(S["scol"],))
        kb.op("vector",
              lambda E: E.tensor_tensor(scol[:, 8:9], scol[:, 7:8], scol[:, 7:8],
                                        op=OP.mult),
              reads=(S["scol"],), writes=(S["scol"],))
        kb.op("vector",
              lambda E: E.scalar_tensor_tensor(scol[:, 9:10], scol[:, 6:7],
                                               1.0 / D, scol[:, 8:9],
                                               op0=OP.mult, op1=OP.subtract),
              reads=(S["scol"],), writes=(S["scol"],))
        eps_c2 = constcol(1e-5)
        kb.op("scalar",
              lambda E, eps_c2=eps_c2: E.activation(scol[:, 10:11], scol[:, 9:10],
                                                    AF.Ln, bias=eps_c2[0:64, :]),
              reads=(S["scol"], _cc_buf), writes=(S["scol"],))
        kb.op("scalar",
              lambda E: E.activation(scol[:, 11:12], scol[:, 10:11], AF.Exp,
                                     scale=-0.5),
              reads=(S["scol"],), writes=(S["scol"],))
        kb.op("vector",
              lambda E: E.tensor_scalar(sl[0:64, :], sl[0:64, :], scol[:, 7:8],
                                        scol[:, 11:12], op0=OP.subtract,
                                        op1=OP.mult),
              reads=(S["sl"], S["scol"]), writes=(S["sl"],))
        kb.op("vector",
              lambda E: E.tensor_tensor(sl[0:64, :], sl[0:64, :],
                                        p["lngb"][0:64, 0:D], op=OP.mult),
              reads=(S["sl"], p["b_lngb"]), writes=(S["sl"],))
        kb.op("vector",
              lambda E: E.tensor_tensor(sl[0:64, :], sl[0:64, :],
                                        p["lngb"][0:64, D:2 * D], op=OP.add),
              reads=(S["sl"], p["b_lngb"]), writes=(S["sl"],))
        slot_T(sl[0:64, :], S["sl"])
        for half in range(2):
            _stream_rhs_mm(dram["L1_ffn_wvp"], 0, half * 512, 512, sT, S["sT"],
                           64, 4 + half, stg2, S["stg"], stg2bf, S["stgbf"])
            bias_mm(4 + half, 64, p["bvp"], half * 512, 512)
            kb.op("vector",
                  lambda E, half=half: E.tensor_copy(
                      sv_bf[0:64, half * 512:half * 512 + 512],
                      psum[4 + half][0:64, 0:512]),
                  reads=(psb[4 + half],), writes=(S["sv"],))

        kb.set_fence()
        A.push()
        A.rebase(TAIL_OFF)
        wo_c = A.alloc([128, 8 * D], BF16, "l1_woc")
        b_woc = kb.buf()
        for i in range(8):
            kb.dma("sync", stg2, dram["L1_ffn_wo"][i * 128:(i + 1) * 128, :],
                   writes=(S["stg"],))
            kb.op("vector", lambda E, i=i: E.tensor_copy(
                wo_c[:, i * D:(i + 1) * D], stg2),
                reads=(S["stg"],), writes=(b_woc,))
        o1_bf = A.alloc([128, D], BF16, "l1_o1")
        o1T = A.alloc([128, D], BF16, "l1_o1T")
        b_o1 = kb.buf(); b_o1T = kb.buf()
        rd_b = hbm_reader()
        for t in range(NT):
            for half in range(2):
                kb.op("tensor",
                      lambda E, t=t, half=half: E.matmul(
                          psum[half][:, 0:512],
                          p["alphaT"][0:64, t * 128:(t + 1) * 128],
                          sv_bf[0:64, half * 512:half * 512 + 512],
                          start=True, stop=True),
                      reads=(balphaT, S["sv"]), writes=(psb[half],))
                kb.op("vector",
                      lambda E, half=half: E.tensor_copy(
                          o1_bf[:, half * 512:half * 512 + 512],
                          psum[half][:, 0:512]),
                      reads=(psb[half],), writes=(b_o1,))
            for i in range(8):
                kb.dmat(kb.tq(), o1T[:, i * 128:(i + 1) * 128],
                        o1_bf[:, i * 128:(i + 1) * 128],
                        reads=(b_o1,), writes=(b_o1T,))
            xa, xab = rd_b(t)
            for half in range(2):
                pidx = 2 + half
                for i in range(8):
                    kb.op("tensor",
                          lambda E, i=i, half=half, pidx=pidx: E.matmul(
                              psum[pidx][:, 0:512], o1T[:, i * 128:(i + 1) * 128],
                              wo_c[:, i * D + half * 512:i * D + half * 512 + 512],
                              start=(i == 0), stop=False),
                          reads=(b_o1T, b_woc), writes=(psb[pidx],))
                bias_mm(pidx, 128, p["bo"], half * 512, 512)
                kb.op("vector",
                      lambda E, t=t, half=half, pidx=pidx, xa=xa:
                      E.scalar_tensor_tensor(
                          xa[:, half * 512:half * 512 + 512],
                          psum[pidx][:, 0:512], 0.5,
                          xa[:, half * 512:half * 512 + 512],
                          op0=OP.mult, op1=OP.add),
                      reads=(psb[pidx],) + xab, writes=xab)
            kb.dma("sync", x_hbm[t * 128:(t + 1) * 128, :], xa,
                   reads=xab, writes=(xh[t],))
        A.pop()
        A.pop()
        A.pop()



    def l1_attn_phase(ph):
        kb.set_fence()
        A.push()
        A.rebase(l1f["end"], limit=128 * 1024)
        h0 = ph * 8
        qkvT = A.alloc([128, 12 * D], BF16, "la_qkvT")
        b_qkvT = kb.buf()
        projT = A.alloc([128, 4 * D], BF16, "la_projT")
        b_projT = kb.buf()
        expB = A.alloc([128, 3 * 8 * 256], BF16, "la_expB")
        b_expB = kb.buf()
        stg = A.alloc([128, D], F32, "la_stg")
        b_stg = kb.buf()
        stgb = A.alloc([128, D], BF16, "la_stgb")
        b_stgb = kb.buf()
        qbT = A.alloc([128, 8], F32, "la_qbT")
        b_qbT = kb.buf()
        set_bias_stage(D)
        vbias = bias_rows("la_vb%d" % ph, dram["L1_attn_qkv_b"][None,
                          2 * D + h0 * 64:2 * D + h0 * 64 + 512], 512)
        pbias = None
        if ph == 0:
            pbias = bias_rows("la_pb", dram["L1_attn_proj_b"][None, :], D)

        ocs = [h0 * 64 + j * 128 for j in range(4)] + \
              [D + h0 * 64 + j * 128 for j in range(4)] + \
              [2 * D + h0 * 64 + j * 128 for j in range(4)]
        for jj, orow in enumerate(ocs):
            kb.dma("sync", stg, dram["L1_attn_qkv_w"][orow:orow + 128, :],
                   writes=(b_stg,))
            kb.op("vector", lambda E: E.tensor_copy(stgb, stg),
                  reads=(b_stg,), writes=(b_stgb,))
            for i in range(8):
                kb.dmat(kb.tq(),
                        qkvT[:, i * 1536 + jj * 128:i * 1536 + (jj + 1) * 128],
                        stgb[:, i * 128:(i + 1) * 128],
                        reads=(b_stgb,), writes=(b_qkvT,))
        for oc in range(8):
            kb.dma("sync", stg[:, 0:512],
                   dram["L1_attn_proj_w"][oc * 128:(oc + 1) * 128,
                                          ph * 512:ph * 512 + 512],
                   writes=(b_stg,))
            kb.op("vector", lambda E: E.tensor_copy(stgb[:, 0:512], stg[:, 0:512]),
                  reads=(b_stg,), writes=(b_stgb,))
            for fc in range(4):
                kb.dmat(kb.tq(),
                        projT[:, fc * D + oc * 128:fc * D + (oc + 1) * 128],
                        stgb[:, fc * 128:(fc + 1) * 128],
                        reads=(b_stgb,), writes=(b_projT,))
        for tt in range(3):
            for hh in range(8):
                for cc in range(2):
                    kb.dma("sync", stgb[:, 0:128],
                           bmat_hbm[tt, h0 + hh, cc * 128:(cc + 1) * 128, :],
                           reads=(l1f["b_bm"],), writes=(b_stgb,))
                    kb.op("scalar",
                          lambda E, tt=tt, hh=hh, cc=cc: E.activation(
                              expB[:, (tt * 8 + hh) * 256 + cc * 128:
                                   (tt * 8 + hh) * 256 + (cc + 1) * 128],
                              stgb[:, 0:128], AF.Exp),
                          reads=(b_stgb,), writes=(b_expB,))
        kb.dma("sync", row_hbm[4:5, 0:512],
               dram["L1_attn_qkv_b"][None, h0 * 64:h0 * 64 + 512],
               writes=(b_qbT,))
        kb.dma("sync", row_hbm[5:6, 0:512],
               dram["L1_attn_qkv_b"][None, D + h0 * 64:D + h0 * 64 + 512],
               writes=(b_qbT,))
        for j in range(4):
            kb.dma("gpsimd", qbT[:, j:j + 1],
                   row_hbm[4:5, j * 128:(j + 1) * 128].rearrange("a b -> b a"),
                   reads=(b_qbT,), writes=(b_qbT,), allow_slow_non_contiguous=True)
            kb.dma("gpsimd", qbT[:, 4 + j:5 + j],
                   row_hbm[5:6, j * 128:(j + 1) * 128].rearrange("a b -> b a"),
                   reads=(b_qbT,), writes=(b_qbT,), allow_slow_non_contiguous=True)
        kb.op("vector", lambda E: E.tensor_scalar(qbT[:, 0:4], qbT[:, 0:4],
                                                  0.125, None, op0=OP.mult),
              reads=(b_qbT,), writes=(b_qbT,))

        A.rebase(TAIL_OFF)
        kTr = A.alloc([128, 4 * 768], BF16, "la_kTr")
        b_kTr = kb.buf()
        qTc = A.alloc([128, 2 * 512], BF16, "la_qTc")
        b_qTc = kb.buf()
        vsh = A.alloc([128, 4 * 512], BF16, "la_vsh")
        b_vsh = kb.buf()
        v_bf = A.alloc([128, 512], BF16, "la_vbf")
        b_vbf = kb.buf()
        lnx_bf = A.alloc([128, D], BF16, "la_lnx")
        lnxT = A.alloc([128, D], BF16, "la_lnxT")
        b_lnx = kb.buf(); b_lnxT = kb.buf()
        et = A.alloc([128, 256], BF16, "la_et")
        b_et = kb.buf()
        pfm = A.alloc([128, 256], BF16, "la_pfm")
        b_pfm = kb.buf()
        att = A.alloc([128, 512], BF16, "la_att")
        aoT = A.alloc([128, 512], BF16, "la_aoT")
        b_att = kb.buf(); b_aoT = kb.buf()
        rcol = A.alloc([128, 2], F32, "la_rcol")
        b_rcol = kb.buf()
        xtl = [A.alloc([128, D], F32, f"la_x{j}") for j in range(3)]
        b_xtl = [kb.buf() for j in range(3)]
        kb.op("vector", lambda E: E.memset(kTr[:], 0.0), writes=(b_kTr,))
        kb.op("vector", lambda E: E.memset(vsh[:], 0.0), writes=(b_vsh,))

        if ph == 0:
            ln_stats_pass(hbm_reader())

        def qkv_tile(u):
            j = u % 3
            slot = u % 4
            qslot = u % 2
            kb.dma("sync", xtl[j], x_hbm[u * 128:(u + 1) * 128, :],
                   reads=(xh[u],), writes=(b_xtl[j],))
            if ph == 0:
                kb.op("vector",
                      lambda E, u=u, j=j: E.tensor_scalar(
                          lnx_bf, xtl[j], st_mu[:, u:u + 1], st_rstd[:, u:u + 1],
                          op0=OP.subtract, op1=OP.mult),
                      reads=(b_xtl[j], sb_mu, sb_rstd), writes=(b_lnx,))
                kb.dma("sync", lnx_hbm[u * 128:(u + 1) * 128, :], lnx_bf,
                       reads=(b_lnx,), writes=(b_lnx,))
            else:
                kb.dma("sync", lnx_bf, lnx_hbm[u * 128:(u + 1) * 128, :],
                       writes=(b_lnx,))
            for i in range(8):
                kb.dmat(kb.tq(), lnxT[:, i * 128:(i + 1) * 128],
                        lnx_bf[:, i * 128:(i + 1) * 128],
                        reads=(b_lnx,), writes=(b_lnxT,))
            for bofs, bank in ((0, 0), (4, 1)):
                for pr in range(4):
                    for i in range(8):
                        kb.op("tensor",
                              lambda E, bofs=bofs, pr=pr, i=i, bank=bank: E.matmul(
                                  psum[bank][:, pr * 128:(pr + 1) * 128],
                                  qkvT[:, i * 1536 + (bofs + pr) * 128:
                                       i * 1536 + (bofs + pr + 1) * 128],
                                  lnxT[:, i * 128:(i + 1) * 128],
                                  start=(pr == 0 and i == 0),
                                  stop=(i == 7),
                                  skip_group_check=True),
                              reads=(b_qkvT, b_lnxT), writes=(psb[bank],))
            for i in range(8):
                kb.op("tensor",
                      lambda E, i=i: E.matmul(
                          psum[2][:, 0:512], lnxT[:, i * 128:(i + 1) * 128],
                          qkvT[:, i * 1536 + 1024:i * 1536 + 1536],
                          start=(i == 0), stop=False),
                      reads=(b_qkvT, b_lnxT), writes=(psb[2],))
            kb.op("tensor",
                  lambda E: E.matmul(psum[2][:, 0:512], ones2[:], vbias[0][:],
                                     start=False, stop=True),
                  reads=(b_ones2, vbias[1]), writes=(psb[2],))
            for pr in range(4):
                kb.op("scalar",
                      lambda E, pr=pr, qslot=qslot: E.activation(
                          qTc[:, qslot * 512 + pr * 128:qslot * 512 + (pr + 1) * 128],
                          psum[0][:, pr * 128:(pr + 1) * 128], AF.Identity,
                          bias=qbT[:, pr:pr + 1], scale=0.125),
                      reads=(psb[0], b_qbT), writes=(b_qTc,))
                kb.op("scalar",
                      lambda E, pr=pr, slot=slot: E.activation(
                          kTr[:, pr * 768 + slot * 128:pr * 768 + (slot + 1) * 128],
                          psum[1][:, pr * 128:(pr + 1) * 128], AF.Identity,
                          bias=qbT[:, 4 + pr:5 + pr]),
                      reads=(psb[1], b_qbT), writes=(b_kTr,))
                if slot < 2:
                    kb.dma("sync",
                           kTr[:, pr * 768 + (slot + 4) * 128:
                               pr * 768 + (slot + 5) * 128],
                           kTr[:, pr * 768 + slot * 128:pr * 768 + (slot + 1) * 128],
                           reads=(b_kTr,), writes=(b_kTr,))
            kb.op("vector", lambda E: E.tensor_copy(v_bf, psum[2][:, 0:512]),
                  reads=(psb[2],), writes=(b_vbf,))
            kb.dma("sync", vsh[64:128, slot * 512:(slot + 1) * 512],
                   v_bf[0:64, :], reads=(b_vbf,), writes=(b_vsh,))
            kb.dma("sync", vsh[0:64, ((u + 1) % 4) * 512:((u + 1) % 4 + 1) * 512],
                   v_bf[64:128, :], reads=(b_vbf,), writes=(b_vsh,))

        def attn_tile(t):
            tt = 0 if t == 0 else (2 if t == NT - 1 else 1)
            qslot = t % 2
            win = ((t - 1) % 4) * 128 + 64
            for hh in range(8):
                pr = hh // 2
                r0 = (hh % 2) * 64
                for cc in range(2):
                    kb.op("tensor",
                          lambda E, pr=pr, r0=r0, cc=cc, win=win, qslot=qslot:
                          E.matmul(
                              psum[3][:, cc * 128:(cc + 1) * 128],
                              kTr[r0:r0 + 64,
                                  pr * 768 + win + cc * 128:
                                  pr * 768 + win + (cc + 1) * 128],
                              qTc[r0:r0 + 64,
                                  qslot * 512 + pr * 128:qslot * 512 + (pr + 1) * 128],
                              start=(cc == 0), stop=True,
                              skip_group_check=True),
                          reads=(b_kTr, b_qTc), writes=(psb[3],))
                kb.op("scalar",
                      lambda E: E.activation(et[:], psum[3][:, 0:256], AF.Exp),
                      reads=(psb[3],), writes=(b_et,))
                kb.op("vector",
                      lambda E, tt=tt, hh=hh: E.tensor_tensor(
                          pfm[:], et[:],
                          expB[:, (tt * 8 + hh) * 256:(tt * 8 + hh) * 256 + 256],
                          op=OP.mult),
                      reads=(b_et, b_expB), writes=(b_pfm,))
                for cc in range(2):
                    vs = ((t + cc) % 4) * 512 + hh * 64
                    kb.op("tensor",
                          lambda E, cc=cc, vs=vs: E.matmul(
                              psum[4][:, 0:64], pfm[:, cc * 128:(cc + 1) * 128],
                              vsh[:, vs:vs + 64],
                              start=(cc == 0), stop=True,
                              skip_group_check=True),
                          reads=(b_pfm, b_vsh), writes=(psb[4],))
                    kb.op("tensor",
                          lambda E, cc=cc: E.matmul(
                              psum[4][:, 64:65], pfm[:, cc * 128:(cc + 1) * 128],
                              consts["ones_col"],
                              start=False, stop=True,
                              skip_group_check=True),
                          reads=(b_pfm, cb["ones_col"]), writes=(psb[4],))
                kb.op("vector", lambda E: E.reciprocal(rcol[:, 0:1],
                                                       psum[4][:, 64:65]),
                      reads=(psb[4],), writes=(b_rcol,))
                kb.op("vector",
                      lambda E, hh=hh: E.tensor_scalar(
                          att[:, hh * 64:(hh + 1) * 64], psum[4][:, 0:64],
                          rcol[:, 0:1], None, op0=OP.mult),
                      reads=(psb[4], b_rcol), writes=(b_att,))
                if DEBUG and ph == 0 and t == 4 and hh == 0:
                    tap("la_pfm", pfm[:, 0:256], (b_pfm,))
                    tap("la_et", et[:, 0:256], (b_et,))
            if DEBUG and ph == 0 and t == 4:
                tap("la_att", att[:, 0:512], (b_att,))
                tap("la_qTc", qTc[:, 0:1024], (b_qTc,))
                tap("la_kTr", kTr[:, 0:768], (b_kTr,))
                tap("la_expB0", expB[:, 8 * 256:9 * 256], (b_expB,))
                tap("la_vsh", vsh[:, 0:2048], (b_vsh,))
            for fc in range(4):
                kb.dmat(kb.tq(), aoT[:, fc * 128:(fc + 1) * 128],
                        att[:, fc * 128:(fc + 1) * 128],
                        reads=(b_att,), writes=(b_aoT,))
            j = t % 3
            for half in range(2):
                pidx = 5 + half
                for fc in range(4):
                    kb.op("tensor",
                          lambda E, fc=fc, half=half, pidx=pidx: E.matmul(
                              psum[pidx][:, 0:512], aoT[:, fc * 128:(fc + 1) * 128],
                              projT[:, fc * D + half * 512:fc * D + half * 512 + 512],
                              start=(fc == 0), stop=(ph == 1 and fc == 3)),
                          reads=(b_aoT, b_projT), writes=(psb[pidx],))
                if ph == 0:
                    kb.op("tensor",
                          lambda E, half=half, pidx=pidx: E.matmul(
                              psum[pidx][:, 0:512], ones2[:],
                              pbias[0][:, half * 512:half * 512 + 512],
                              start=False, stop=True),
                          reads=(b_ones2, pbias[1]), writes=(psb[pidx],))
                kb.op("vector",
                      lambda E, half=half, pidx=pidx, j=j: E.tensor_tensor(
                          xtl[j][:, half * 512:half * 512 + 512],
                          xtl[j][:, half * 512:half * 512 + 512],
                          psum[pidx][:, 0:512], op=OP.add),
                      reads=(psb[pidx],), writes=(b_xtl[j],))
            kb.dma("sync", x_hbm[t * 128:(t + 1) * 128, :], xtl[j],
                   reads=(b_xtl[j],), writes=(xh[t],))

        qkv_tile(0)
        for t in range(NT):
            if t + 1 < NT:
                qkv_tile(t + 1)
            attn_tile(t)
        A.pop()

    x_in_hbm = False
    if STAGES >= 2:
        xh = l0_attn()
        x_in_hbm = True

    if STAGES >= 3:
        kb.set_fence()
        for t in range(NT):
            xb[t] = kb.buf(f"x[{t}]")
            kb.dma("sync", xt(t), x_hbm[t * 128:(t + 1) * 128, :],
                   reads=(xh[t],), writes=(xb[t],))
        x_in_hbm = False
        if STAGES >= 5:
            bmat_build()
        l0_ffn()

    if STAGES >= 4:
        kb.set_fence()
        for t in range(NT):
            xh[t] = kb.buf(f"xh4[{t}]")
            kb.dma("sync", x_hbm[t * 128:(t + 1) * 128, :], xt(t),
                   reads=(xb[t],), writes=(xh[t],))
        x_in_hbm = True
        l1_ffn_prep()
        l1_ffn()

    if STAGES >= 5:
        l1_attn_phase(0)
        l1_attn_phase(1)

    if STAGES >= 6:
        l1_ffn()

    out_toks = []
    for t in range(NT):
        if x_in_hbm:
            out_toks.append(
                kb.dma("sync", out_d[t * 128:(t + 1) * 128, :],
                       x_hbm[t * 128:(t + 1) * 128, :], reads=(xh[t],)))
        else:
            out_toks.append(
                kb.dma("sync", out_d[t * 128:(t + 1) * 128, :], xt(t),
                       reads=(xb[t],)))
    kb.op("sync", lambda E: E.nop(), extra=out_toks)


_CACHE = {}


def kernel(**inputs):
    if "nc" not in _CACHE:
        _CACHE["nc"] = build_program()
    nc = _CACHE["nc"]
    consts = host_constants()
    x = np.ascontiguousarray(inputs["L0_attn_x"])
    in_maps = []
    for c in range(8):
        m = {}
        for name, shape, dt in INPUT_SPECS:
            if name == "L0_attn_x":
                m[name] = np.ascontiguousarray(x[c])
            else:
                m[name] = np.ascontiguousarray(inputs[name])
        for cname, arr in consts.items():
            m["c_" + cname] = arr
        in_maps.append(m)
    res = run_bass_kernel_spmd(nc, in_maps, core_ids=list(range(8)))
    out = np.stack([r["out"] for r in res.results], axis=0)
    if DEBUG:
        kernel.taps = [{n: r[n] for n, _, _ in DEBUG_TAPS} for r in res.results]
    return out


# version 12
# speedup vs baseline: 1.0339x; 1.0339x over previous
import numpy as np
import ml_dtypes
from contextlib import ExitStack

import concourse.bass as bass
import concourse.mybir as mybir
from concourse.bass_utils import run_bass_kernel_spmd

F32 = mybir.dt.float32
BF16 = mybir.dt.bfloat16
I32 = mybir.dt.int32
U8 = mybir.dt.uint8
AF = mybir.ActivationFunctionType
OP = mybir.AluOpType
AX = mybir.AxisListType

NP_BF16 = ml_dtypes.bfloat16

L = 4096
D = 1024
NT = 32
NH = 16
HD = 64
STAGES = 6
FA_CHUNKS = NT

DEBUG_TAPS = []
DEBUG = False

ENGINES = ("sync", "vector", "scalar", "gpsimd", "tensor")
HW_Q = ("sync", "scalar")


class Tok:
    __slots__ = ("sem", "idx", "val")

    def __init__(self, sem, idx):
        self.sem = sem
        self.idx = idx
        self.val = None


class Buf:
    __slots__ = ("name", "wr", "rd")

    def __init__(self, name=""):
        self.name = name
        self.wr = None
        self.rd = []


class KB:

    DMA_POOL = 12

    def __init__(self, nc):
        self.nc = nc
        self.prog = {e: [] for e in ENGINES}
        self.sem_idx = {}
        self.consumed = set()
        self.hw = {e: {} for e in ENGINES}
        self.esem = {e: "c_" + e for e in ENGINES}
        self.qpool = {e: ["q_%s_%d" % (e, i) for i in range(self.DMA_POOL)]
                      for e in ("sync", "scalar", "gpsimd", "vector")}
        self.qrr = {e: 0 for e in self.qpool}
        self.sems = {}
        self._tq_rr = 0
        self.fence = []
        self.last_op = {e: None for e in ENGINES}

    def sem_names(self):
        out = [self.esem[e] for e in ENGINES]
        for e in self.qpool:
            out.extend(self.qpool[e])
        return out

    def _tok(self, sem):
        i = self.sem_idx.get(sem, 0) + 1
        self.sem_idx[sem] = i
        return Tok(sem, i)

    def _prune(self, e, waits):
        out = []
        seen = {}
        for t in waits:
            if t is None:
                continue
            if e == "tensor" and t.sem == self.esem["tensor"]:
                continue
            cur = seen.get(t.sem)
            if cur is None or t.idx > cur.idx:
                seen[t.sem] = t
        for sem, t in seen.items():
            if self.hw[e].get(sem, 0) >= t.idx:
                continue
            self.hw[e][sem] = t.idx
            out.append(t)
            self.consumed.add((t.sem, t.idx))
        return out

    def _collect(self, reads, writes, extra):
        waits = []
        for b in reads:
            if b.wr is not None:
                waits.append(b.wr)
        for b in writes:
            if b.wr is not None:
                waits.append(b.wr)
            waits.extend(b.rd)
        waits.extend(extra)
        return waits

    def _emit(self, e, fn, waits, is_dma, sem):
        t = self._tok(sem)
        self.prog[e].append((waits, fn, t, is_dma))
        return t

    SERIAL_CHAIN = False

    def op(self, e, fn, reads=(), writes=(), extra=()):
        waits = list(self._collect(reads, writes, extra))
        if self.SERIAL_CHAIN and e != "tensor" and self.last_op[e] is not None:
            waits.append(self.last_op[e])
        waits = self._prune(e, waits)
        t = self._emit(e, fn, waits, False, self.esem[e])
        self.last_op[e] = t
        for b in reads:
            b.rd.append(t)
        for b in writes:
            b.wr = t
            b.rd = []
        return t

    def _dma_emit(self, e, fn, reads, writes, extra):
        pool = self.qpool[e]
        sem = pool[self.qrr[e] % len(pool)]
        self.qrr[e] += 1
        waits = list(self._collect(reads, writes, extra))
        prev = self.sem_idx.get(sem, 0)
        if prev:
            waits.append(Tok(sem, prev))
        waits = self._prune(e, waits)
        t = self._emit(e, fn, waits, True, sem)
        for b in reads:
            b.rd.append(t)
        for b in writes:
            b.wr = t
            b.rd = []
        return t

    def dma(self, e, out, in_, reads=(), writes=(), extra=(), **kw):
        fn = lambda E, out=out, in_=in_, kw=kw: E.dma_start(out=out, in_=in_, **kw)
        return self._dma_emit(e, fn, reads, writes, extra)

    def dmat(self, e, out, in_, reads=(), writes=(), extra=()):
        assert e in HW_Q
        fn = lambda E, out=out, in_=in_: E.dma_start_transpose(out=out, in_=in_)
        return self._dma_emit(e, fn, reads, writes, extra)

    def tq(self):
        e = HW_Q[self._tq_rr % len(HW_Q)]
        self._tq_rr += 1
        return e

    def buf(self, name=""):
        b = Buf(name)
        b.rd = list(self.fence)
        return b

    def set_fence(self):
        self.fence = self.barrier_toks()

    def barrier_toks(self):
        out = []
        for sem, i in self.sem_idx.items():
            if i:
                out.append(Tok(sem, i))
        return out

    def finalize_into(self, block, sem_handles):
        self.sems = sem_handles
        val = {}
        for e in ENGINES:
            for waits, fn, t, is_dma in self.prog[e]:
                if is_dma or (t.sem, t.idx) in self.consumed:
                    inc = 16 if is_dma else 1
                    val[t.sem] = val.get(t.sem, 0) + inc
                    t.val = val[t.sem]
                else:
                    t.val = None

        sem_points = {}
        for e in ENGINES:
            for waits, fn, t, is_dma in self.prog[e]:
                if t.val is not None:
                    sem_points.setdefault(t.sem, []).append((t.idx, t.val))
        for sem in sem_points:
            sem_points[sem].sort()

        import bisect

        def resolve(t):
            pts = sem_points.get(t.sem)
            assert pts is not None, f"wait on sem {t.sem} with no incing ops"
            i = bisect.bisect_right(pts, (t.idx, 1 << 60)) - 1
            assert i >= 0 and pts[i][0] == t.idx, \
                f"consumed token lost {t.sem} {t.idx}"
            return pts[i][1]

        def mk_runner(e):
            ops = self.prog[e]

            def run(E):
                for waits, fn, t, is_dma in ops:
                    for w in waits:
                        E.wait_ge(self.sems[w.sem], resolve(w))
                    ins = fn(E)
                    if t.val is not None:
                        inc = 16 if is_dma else 1
                        ins.then_inc(self.sems[t.sem], inc)
            return run

        runners = {e: mk_runner(e) for e in ENGINES}
        block.sync(runners["sync"])
        block.vector(runners["vector"])
        block.scalar(runners["scalar"])
        block.gpsimd(runners["gpsimd"])
        block.tensor(runners["tensor"])


class Arena:

    def __init__(self, nc, base, nbytes):
        self.nc = nc
        self.base = base
        self.nbytes = nbytes
        self.off = 0
        self.marks = []
        self.n = 0

    def alloc(self, shape, dtype, name=""):
        assert len(shape) == 2
        if not hasattr(self, "log"):
            self.log = []
        per = int(np.prod(shape[1:])) * mybir.dt.size(dtype)
        off = (self.off + 31) & ~31
        assert off + per <= self._cap, f"arena overflow at {name}: {off}+{per} cap {self._cap}"
        self.off = off + per
        self.log.append((name, off, off + per))
        self.n += 1
        h = self.nc.alloc_sbuf_tensor_at(
            f"a{self.n}_{name or 'x'}", [shape[0], int(shape[1])], dtype,
            offset=self.base + off)
        return h.ap()

    def push(self):
        self.marks.append(self.off)

    def pop(self):
        self.off = self.marks.pop()
        self.limit = None

    def rebase(self, off, limit=None):
        self.off = off
        self.limit = limit

    @property
    def _cap(self):
        return self.limit if getattr(self, "limit", None) else self.nbytes


def _bf(x):
    return np.asarray(x, dtype=NP_BF16)


def host_constants():
    c = {}
    j = np.arange(128)
    c["triu"] = _bf(np.triu(np.ones((128, 128), np.float32)))
    c["ident"] = _bf(np.eye(128, dtype=np.float32))
    c["ones_col"] = _bf(np.ones((128, 1), np.float32))
    c["ones128"] = _bf(np.ones((128, 128), np.float32))
    c["o64_col"] = _bf(np.ones((64, 1), np.float32))
    c["identf32"] = np.eye(128, dtype=np.float32)
    c["ones_row"] = _bf(np.ones((1, 128), np.float32))
    n = (np.arange(L, dtype=np.float64) + 1.0).reshape(NT, 128).T
    c["normc"] = n.astype(np.float32)
    c["rnormc"] = (1.0 / n.astype(np.float32)).astype(np.float32)
    oh = np.zeros((8, 8 * 128), np.float32)
    for i in range(8):
        oh[i, i * 128:(i + 1) * 128] = 1.0
    c["onehot8"] = _bf(oh)
    return c


CONST_SPECS = {
    "triu": ([128, 128], BF16),
    "ident": ([128, 128], BF16),
    "ones_col": ([128, 1], BF16),
    "ones128": ([128, 128], BF16),
    "o64_col": ([64, 1], BF16),
    "identf32": ([128, 128], F32),
    "ones_row": ([1, 128], BF16),
    "normc": ([128, NT], F32),
    "rnormc": ([128, NT], F32),
    "onehot8": ([8, 8 * 128], BF16),
}

INPUT_SPECS = [
    ("L0_attn_x", [L, D], F32),
    ("L0_attn_Wq", [D, D], F32), ("L0_attn_bq", [D], F32),
    ("L0_attn_Wk", [D, D], F32), ("L0_attn_bk", [D], F32),
    ("L0_attn_Wv", [D, D], F32), ("L0_attn_bv", [D], F32),
    ("L0_attn_Wo", [D, D], F32), ("L0_attn_bo", [D], F32),
    ("L0_ffn_packed_weight", [262144], I32),
    ("L0_ffn_weight_scale", [1], F32),
    ("L0_ffn_bias", [D], F32),
    ("L1_attn_qkv_w", [3 * D, D], F32), ("L1_attn_qkv_b", [3 * D], F32),
    ("L1_attn_rpb", [16, 25], F32),
    ("L1_attn_proj_w", [D, D], F32), ("L1_attn_proj_b", [D], F32),
    ("L1_ffn_slot_init", [64, D], F32),
    ("L1_ffn_w1", [2 * D, 512], F32), ("L1_ffn_b1", [512], F32),
    ("L1_ffn_w2", [512, 64], F32), ("L1_ffn_b2", [64], F32),
    ("L1_ffn_tau", [1], F32),
    ("L1_ffn_Wih", [3 * D, D], F32), ("L1_ffn_Whh", [3 * D, D], F32),
    ("L1_ffn_bih", [3 * D], F32), ("L1_ffn_bhh", [3 * D], F32),
    ("L1_ffn_wq", [D, 256], F32), ("L1_ffn_bq", [256], F32),
    ("L1_ffn_wk", [D, 256], F32), ("L1_ffn_bk", [256], F32),
    ("L1_ffn_wv", [D, D], F32), ("L1_ffn_bv", [D], F32),
    ("L1_ffn_ln_g", [D], F32), ("L1_ffn_ln_b", [D], F32),
    ("L1_ffn_wvp", [D, D], F32), ("L1_ffn_bvp", [D], F32),
    ("L1_ffn_wo", [D, D], F32), ("L1_ffn_bo", [D], F32),
]



def l1attn_diag_intervals(ttype):
    ts = {0: 0, 1: 128, 2: L - 128}[ttype]
    out = []
    for m in range(52, 77):
        ls = []
        for l in range(128):
            c = l + m
            if not (0 <= c < 256):
                continue
            i = ts + l
            n = ts + c - 64
            ni = min(max(i - 6, 0), L - 13)
            if ni <= n <= ni + 12:
                assert 0 <= m - 52 <= 24
                ls.append(l)
        if ls:
            assert ls == list(range(ls[0], ls[-1] + 1)), (ttype, m, ls)
            out.append((m, ls[0], ls[-1] + 1))
    return out


def build_program():
    nc = bass.Bass("TRN2", target_bir_lowering=False)
    dram = {}
    for name, shape, dt in INPUT_SPECS:
        dram[name] = nc.dram_tensor(name, shape, dt, kind="ExternalInput").ap()
    for name, (shape, dt) in CONST_SPECS.items():
        dram[name] = nc.dram_tensor("c_" + name, shape, dt, kind="ExternalInput").ap()
    out = nc.dram_tensor("out", [L, D], F32, kind="ExternalOutput").ap()
    x_hbm = nc.dram_tensor("x_hbm", [L, D], F32, kind="Internal").ap()
    row_hbm = nc.dram_tensor("row_hbm", [64, 4096], F32, kind="Internal").ap()
    lnx_hbm = nc.dram_tensor("lnx_hbm", [L, D], BF16, kind="Internal").ap()
    bmat_hbm = nc.dram_tensor("bmat_hbm", [3, NH, 256, 128], BF16, kind="Internal").ap()

    ARENA_BYTES = 206 * 1024
    with ExitStack() as ctx:
        arena_t = ctx.enter_context(nc.sbuf_tensor("arena", [128, ARENA_BYTES], U8))
        A = Arena(nc, nc.lookup_mloc(arena_t).addr, ARENA_BYTES)
        kb = KB(nc)

        sem_handles = {}
        for name in kb.sem_names():
            sem_handles[name] = ctx.enter_context(nc.semaphore(name))

        psum = [ctx.enter_context(nc.psum_tensor(f"ps{i}", [128, 512], F32))
                for i in range(8)]
        psb = [Buf(f"ps{i}") for i in range(8)]

        build_body(nc, kb, A, dram, out, x_hbm, row_hbm, bmat_hbm, lnx_hbm,
                   psum, psb)

        block = ctx.enter_context(nc.Block())
        kb.finalize_into(block, sem_handles)
    return nc



def build_body(nc, kb, A, dram, out_d, x_hbm, row_hbm, bmat_hbm, lnx_hbm,
               psum, psb):
    l1f = {}

    _tapn = {}

    def tap(name, ap, bufs):
        if not DEBUG:
            return
        if name in _tapn:
            _tapn[name] += 1
            name = f"{name}_{_tapn[name]}"
        else:
            _tapn[name] = 0
        shape = list(ap.shape)
        d = nc.dram_tensor("tap_" + name, shape, ap.dtype, kind="ExternalOutput").ap()
        DEBUG_TAPS.append(("tap_" + name, shape, ap.dtype))
        kb.dma("sync", d, ap, reads=tuple(bufs))

    x_res = A.alloc([128, NT * D], F32, "x_res")
    xb = [Buf(f"x[{t}]") for t in range(NT)]
    xt = lambda t: x_res[:, t * D:(t + 1) * D]

    consts = {}
    cb = {}
    for name, (shape, dt) in CONST_SPECS.items():
        consts[name] = A.alloc(shape, dt, name)
        cb[name] = Buf(name)
        if name == "rnormc":
            continue
        kb.dma("sync", consts[name], dram[name], writes=(cb[name],))
    kb.op("vector", lambda E: E.reciprocal(consts["rnormc"], consts["normc"]),
          reads=(cb["normc"],), writes=(cb["rnormc"],))

    _cc = {}
    _cc_area = A.alloc([128, 16], F32, "constcols")
    _cc_buf = Buf("constcols")

    def constcol(v):
        if v not in _cc:
            i = len(_cc)
            assert i < 16
            _cc[v] = _cc_area[:, i:i + 1]
            kb.op("vector", lambda E, i=i, v=v: E.memset(_cc_area[:, i:i + 1], v),
                  writes=(_cc_buf,))
        return _cc[v]

    st_sum = A.alloc([128, NT], F32, "st_sum")
    st_ssq = A.alloc([128, NT], F32, "st_ssq")
    st_mu = A.alloc([128, NT], F32, "st_mu")
    st_rstd = A.alloc([128, NT], F32, "st_rstd")
    st_nmr = A.alloc([128, NT], F32, "st_nmr")
    st_tmp = A.alloc([128, NT], F32, "st_tmp")
    sb_sum = [Buf() for _ in range(NT)]
    sb_ssq = [Buf() for _ in range(NT)]
    sb_mu = Buf("mu"); sb_rstd = Buf("rstd"); sb_nmr = Buf("nmr"); sb_tmp = Buf()
    dump = A.alloc([128, D], F32, "dump")
    b_dump = Buf("dump")

    if STAGES >= 5:
        rpbbc = A.alloc([128, 16 * 32], F32, "rpb_bc")
        rpbbf = A.alloc([128, 16 * 32], BF16, "rpb_bf")
        negt = A.alloc([128, 128], BF16, "neg_tile")

    def bmat_build():
        b_rpb = kb.buf("rpb")
        b_neg = kb.buf("neg")
        kb.op("vector", lambda E: E.memset(rpbbc[:], 0.0), writes=(b_rpb,))
        for h in range(NH):
            kb.dma("sync", rpbbc[:, h * 32:h * 32 + 25],
                   dram["L1_attn_rpb"][h:h + 1, :].partition_broadcast(128),
                   writes=(b_rpb,))
        kb.op("vector", lambda E: E.tensor_copy(rpbbf, rpbbc),
              reads=(b_rpb,), writes=(b_rpb,))
        kb.op("vector", lambda E: E.memset(negt[:], -30000.0), writes=(b_neg,))
        bm = bmat_hbm
        b_bm = kb.buf("bmat")
        for tt in range(3):
            for h in range(NH):
                for cc in range(2):
                    kb.dma(("sync", "scalar")[(tt + h + cc) % 2],
                           bm[tt, h, cc * 128:(cc + 1) * 128, :], negt[:],
                           reads=(b_neg,), writes=(b_bm,))
        for tt in range(3):
            ivals = l1attn_diag_intervals(tt)
            for h in range(NH):
                for (m, l0, l1) in ivals:
                    base = ((tt * NH + h) * 256) * 128
                    dst = bass.AP(bm.tensor, base + 128 * m + 129 * l0,
                                  [[129, l1 - l0]])
                    kb.dma(("sync", "scalar")[(h + m) % 2], dst,
                           rpbbf[l0:l1, h * 32 + m - 52:m - 51 + h * 32],
                           reads=(b_rpb,), writes=(b_bm,),
                           allow_slow_non_contiguous=True)
        l1f["b_bm"] = b_bm

    if STAGES >= 5:
        bmat_build()

    x_in = dram["L0_attn_x"]
    for t in range(NT):
        kb.dma("sync", xt(t), x_in[t * 128:(t + 1) * 128, :], writes=(xb[t],))


    def ln_stats_pass(read_tile, n_tiles=NT):
        for t in range(n_tiles):
            ap, bufs = read_tile(t)
            sb_sum[t] = kb.buf()
            sb_ssq[t] = kb.buf()
            kb.op("vector",
                  lambda E, ap=ap, t=t: E.reduce_sum(st_sum[:, t:t + 1], ap, axis=AX.X),
                  reads=bufs, writes=(sb_sum[t],))
            kb.op("scalar",
                  lambda E, ap=ap, t=t: E.activation(
                      dump[:], ap, AF.Square, accum_out=st_ssq[:, t:t + 1]),
                  reads=bufs, writes=(b_dump, sb_ssq[t]))
        kb.op("vector",
              lambda E: E.tensor_scalar(st_mu[:, 0:n_tiles], st_sum[:, 0:n_tiles],
                                        1.0 / D, None, op0=OP.mult),
              reads=sb_sum[:n_tiles], writes=(sb_mu,))
        kb.op("vector",
              lambda E: E.tensor_tensor(st_tmp[:, 0:n_tiles], st_mu[:, 0:n_tiles],
                                        st_mu[:, 0:n_tiles], op=OP.mult),
              reads=(sb_mu,), writes=(sb_tmp,))
        kb.op("vector",
              lambda E: E.scalar_tensor_tensor(
                  st_rstd[:, 0:n_tiles], st_ssq[:, 0:n_tiles], 1.0 / D,
                  st_tmp[:, 0:n_tiles], op0=OP.mult, op1=OP.subtract),
              reads=sb_ssq[:n_tiles] + [sb_tmp], writes=(sb_rstd,))
        eps_c = constcol(1e-5)
        kb.op("scalar",
              lambda E: E.activation(st_tmp[:, 0:n_tiles], st_rstd[:, 0:n_tiles],
                                     AF.Ln, bias=eps_c),
              reads=(sb_rstd, _cc_buf), writes=(sb_tmp,))
        kb.op("scalar",
              lambda E: E.activation(st_rstd[:, 0:n_tiles], st_tmp[:, 0:n_tiles],
                                     AF.Exp, scale=-0.5),
              reads=(sb_tmp,), writes=(sb_rstd,))
        kb.op("vector",
              lambda E: E.tensor_tensor(st_nmr[:, 0:n_tiles], st_mu[:, 0:n_tiles],
                                        st_rstd[:, 0:n_tiles], op=OP.mult),
              reads=(sb_mu, sb_rstd), writes=(sb_nmr,))
        kb.op("vector",
              lambda E: E.tensor_scalar(st_nmr[:, 0:n_tiles], st_nmr[:, 0:n_tiles],
                                        -1.0, None, op0=OP.mult),
              reads=(sb_nmr,), writes=(sb_nmr,))

    bstage = {"f": None, "b": None, "buf": None, "w": 0}

    def set_bias_stage(width):
        bstage["f"] = A.alloc([1, width], F32, "bstg_f")
        bstage["b"] = A.alloc([1, width], BF16, "bstg_b")
        bstage["buf"] = kb.buf("bstg")
        bstage["w"] = width

    def bias_rows(name, src_ap, width):
        assert width <= bstage["w"]
        rows = A.alloc([2, width], BF16, name + "_rows")
        bsrc = bstage["f"][0:1, 0:width]
        lo_b = bstage["b"][0:1, 0:width]
        b_stage = bstage["buf"]
        b_rows = kb.buf()
        kb.dma("sync", bsrc, src_ap, writes=(b_stage,))
        kb.op("vector", lambda E, rows=rows, bsrc=bsrc: E.tensor_copy(
            rows[0:1, :], bsrc), reads=(b_stage,), writes=(b_rows,))
        kb.op("vector", lambda E, rows=rows, bsrc=bsrc, lo_b=lo_b: E.tensor_tensor(
            lo_b, bsrc, rows[0:1, :], op=OP.subtract),
            reads=(b_stage, b_rows), writes=(b_stage,))
        kb.dma("sync", rows[1:2, :], lo_b, reads=(b_stage,), writes=(b_rows,))
        return rows, b_rows

    ones2 = A.alloc([2, 128], BF16, "ones2")
    b_ones2 = Buf()
    kb.op("vector", lambda E: E.memset(ones2[0:2, :], 1.0), writes=(b_ones2,))

    w0T = A.alloc([128, 8 * D], BF16, "w0T")
    b_w0T = Buf("w0T")
    A.push()
    pk = A.alloc([128, 2 * D], I32, "packed")
    b_pk = Buf()
    w0 = A.alloc([128, 8 * D], BF16, "w0_om")
    b_w0 = Buf("w0")
    tmp_i = A.alloc([128, D], I32, "unp_i")
    b_tmpi = Buf()
    pw = dram["L0_ffn_packed_weight"]
    kb.dma("sync", pk[:, 0:D], pw.rearrange("(r f) -> r f", f=D)[0:128, :], writes=(b_pk,))
    kb.dma("sync", pk[:, D:2 * D], pw.rearrange("(r f) -> r f", f=D)[128:256, :], writes=(b_pk,))
    for s in range(4):
        for half in range(2):
            src = pk[:, half * D:(half + 1) * D]
            j = (s * 256 + half * 128) // 128
            kb.op("vector",
                  lambda E, src=src, s=s: E.tensor_scalar(
                      tmp_i, src, 2 * s, 3, op0=OP.logical_shift_right,
                      op1=OP.bitwise_and),
                  reads=(b_pk,), writes=(b_tmpi,))
            kb.op("vector",
                  lambda E, j=j: E.tensor_scalar(
                      w0[:, j * D:(j + 1) * D], tmp_i, -1.0, None, op0=OP.add),
                  reads=(b_tmpi,), writes=(b_w0,))
    for i in range(8):
        for j in range(8):
            kb.dmat(kb.tq(), w0T[:, i * D + j * 128:i * D + (j + 1) * 128],
                    w0[:, j * D + i * 128:j * D + (i + 1) * 128],
                    reads=(b_w0,), writes=(b_w0T,))

    A.pop()
    kb.set_fence()

    bias0_bc = A.alloc([128, D], F32, "bias0_bc")
    b_bias0 = kb.buf("bias0")
    kb.dma("sync", bias0_bc, dram["L0_ffn_bias"][None, :].partition_broadcast(128),
           writes=(b_bias0,))
    kb.op("vector", lambda E: E.tensor_scalar(bias0_bc, bias0_bc, 0.5, None,
                                              op0=OP.mult),
          reads=(b_bias0,), writes=(b_bias0,))
    ws0 = A.alloc([128, 1], F32, "ws0")
    b_ws0 = kb.buf("ws0")
    kb.dma("sync", ws0, dram["L0_ffn_weight_scale"][None, :].partition_broadcast(128),
           writes=(b_ws0,))
    TAIL_OFF = (A.off + 31) & ~31


    def l0_ffn():
        A.push()
        kb.set_fence()
        ln_stats_pass(lambda t: (xt(t), (xb[t],)))
        lnx = A.alloc([128, D], F32, "lnx")
        amx = A.alloc([128, 4], F32, "amx")
        qi = A.alloc([128, D], I32, "qi")
        qf = A.alloc([128, D], F32, "qf")
        qm = A.alloc([128, D], F32, "qm")
        qb = [A.alloc([128, D], BF16, f"qb{i}") for i in range(2)]
        qT = [A.alloc([128, D], BF16, f"qT{i}") for i in range(2)]
        sc = [A.alloc([128, 2], F32, f"sc{i}") for i in range(2)]
        ep = [A.alloc([128, D], F32, f"ep{i}") for i in range(2)]
        b_lnx, b_amx, b_qi = kb.buf(), kb.buf(), kb.buf()
        b_qf, b_qm = kb.buf(), kb.buf()
        b_qb = [kb.buf(), kb.buf()]
        b_qT = [kb.buf(), kb.buf()]
        b_sc = [kb.buf(), kb.buf()]
        b_ep = [kb.buf(), kb.buf()]
        for t in range(NT):
            pp = t % 2
            kb.op("vector",
                  lambda E, t=t: E.tensor_scalar(
                      lnx, xt(t), st_mu[:, t:t + 1], st_rstd[:, t:t + 1],
                      op0=OP.subtract, op1=OP.mult),
                  reads=(xb[t], sb_mu, sb_rstd), writes=(b_lnx,))
            kb.op("vector",
                  lambda E: E.reduce_max(amx[:, 0:1], lnx, axis=AX.X,
                                         apply_absolute_value=True),
                  reads=(b_lnx,), writes=(b_amx,))
            kb.op("vector",
                  lambda E: E.tensor_scalar(amx[:, 1:2], amx[:, 0:1], 1e-5, None,
                                            op0=OP.max),
                  reads=(b_amx,), writes=(b_amx,))
            kb.op("vector", lambda E: E.reciprocal(amx[:, 2:3], amx[:, 1:2]),
                  reads=(b_amx,), writes=(b_amx,))
            kb.op("vector",
                  lambda E, pp=pp: E.tensor_scalar(sc[pp][:, 0:1], amx[:, 2:3],
                                                   127.0, None, op0=OP.mult),
                  reads=(b_amx,), writes=(b_sc[pp],))
            kb.op("vector",
                  lambda E, pp=pp: E.tensor_scalar(sc[pp][:, 1:2], amx[:, 1:2],
                                                   ws0[:, 0:1], 0.5 / 127.0,
                                                   op0=OP.mult, op1=OP.mult),
                  reads=(b_amx, b_ws0), writes=(b_sc[pp],))
            kb.op("vector",
                  lambda E, pp=pp: E.tensor_scalar(lnx, lnx, sc[pp][:, 0:1], None,
                                                   op0=OP.mult),
                  reads=(b_lnx, b_sc[pp]), writes=(b_lnx,))
            kb.op("vector", lambda E: E.tensor_copy(qi, lnx),
                  reads=(b_lnx,), writes=(b_qi,))
            kb.op("scalar", lambda E: E.copy(qf, qi),
                  reads=(b_qi,), writes=(b_qf,))
            kb.op("vector", lambda E: E.tensor_tensor(lnx, lnx, qf,
                                                      op=OP.subtract),
                  reads=(b_lnx, b_qf), writes=(b_lnx,))
            kb.op("vector",
                  lambda E: E.scalar_tensor_tensor(qm, lnx, 0.5, qf,
                                                   op0=OP.is_ge, op1=OP.add),
                  reads=(b_lnx, b_qf), writes=(b_qm,))
            kb.op("vector",
                  lambda E: E.tensor_scalar(qf, lnx, -0.5, None, op0=OP.is_lt),
                  reads=(b_lnx,), writes=(b_qf,))
            kb.op("vector",
                  lambda E, pp=pp: E.tensor_tensor(qb[pp], qm, qf,
                                                   op=OP.subtract),
                  reads=(b_qm, b_qf), writes=(b_qb[pp],))
            if t == 0:
                tap("mu", st_mu[:, 0:NT], (sb_mu,))
                tap("rstd", st_rstd[:, 0:NT], (sb_rstd,))
                tap("sc", sc[0][:, 0:2], (b_sc[0],))
                tap("qb0", qb[0][:, 0:D], (b_qb[0],))
            for i in range(8):
                kb.dmat(kb.tq(), qT[pp][:, i * 128:(i + 1) * 128],
                        qb[pp][:, i * 128:(i + 1) * 128],
                        reads=(b_qb[pp],), writes=(b_qT[pp],))
            for half in range(2):
                pidx = 4 + 2 * pp + half
                for i in range(8):
                    kb.op("tensor",
                          lambda E, i=i, half=half, pidx=pidx, pp=pp: E.matmul(
                              psum[pidx][:, 0:512],
                              qT[pp][:, i * 128:(i + 1) * 128],
                              w0T[:, i * D + half * 512: i * D + half * 512 + 512],
                              start=(i == 0), stop=(i == 7)),
                          reads=(b_qT[pp], b_w0T), writes=(psb[pidx],))
            for half in range(2):
                pidx = 4 + 2 * pp + half
                kb.op("vector",
                      lambda E, pidx=pidx, half=half, pp=pp: E.scalar_tensor_tensor(
                          ep[pp][:, half * 512:(half + 1) * 512],
                          psum[pidx][:, 0:512], sc[pp][:, 1:2],
                          bias0_bc[:, half * 512:(half + 1) * 512],
                          op0=OP.mult, op1=OP.add),
                      reads=(psb[pidx], b_sc[pp], b_bias0), writes=(b_ep[pp],))
            if t == 0:
                tap("qT0", qT[0][:, 0:D], (b_qT[0],))
                tap("ep0", ep[0][:, 0:D], (b_ep[0],))
            kb.op("vector",
                  lambda E, t=t, pp=pp: E.tensor_tensor(
                      xt(t), xt(t), ep[pp][:], op=OP.add),
                  reads=(b_ep[pp], xb[t]), writes=(xb[t],))
        A.pop()

    tap("w0T", w0T[:, 0:8 * D], (b_w0T,))
    tap("bias0", bias0_bc[:, 0:D], (b_bias0,))


    def l0_attn():
        EPSF = 1e-6
        ln_stats_pass(lambda t: (xt(t), (xb[t],)))
        xh = [kb.buf(f"xh[{t}]") for t in range(NT)]
        for t in range(NT):
            kb.dma("sync", x_hbm[t * 128:(t + 1) * 128, :], xt(t),
                   reads=(xb[t],), writes=(xh[t],))
        A.push()
        A.rebase(0, limit=128 * 1024)
        kb.set_fence()

        W = {}
        bW = {}
        wstg = [A.alloc([128, D], F32, f"fa_wstg{j}") for j in range(2)]
        bwstg = [kb.buf(), kb.buf()]
        cast_engs = ("vector", "scalar", "gpsimd")
        nload = 0
        for nm, dnm in (("q", "L0_attn_Wq"), ("k", "L0_attn_Wk"),
                        ("v", "L0_attn_Wv"), ("o", "L0_attn_Wo")):
            W[nm] = A.alloc([128, 8 * D], BF16, f"W{nm}_r")
            bW[nm] = kb.buf(f"W{nm}")
            for i in range(8):
                j = nload % 2
                kb.dma("sync", wstg[j], dram[dnm][i * 128:(i + 1) * 128, :],
                       writes=(bwstg[j],))
                eng = cast_engs[nload % 3]
                if eng == "scalar":
                    fn = lambda E, nm=nm, i=i, j=j: E.copy(
                        W[nm][:, i * D:(i + 1) * D], wstg[j])
                else:
                    fn = lambda E, nm=nm, i=i, j=j: E.tensor_copy(
                        W[nm][:, i * D:(i + 1) * D], wstg[j])
                kb.op(eng, fn, reads=(bwstg[j],), writes=(bW[nm],))
                nload += 1
        set_bias_stage(D)
        BR = {}
        for nm, dnm in (("q", "L0_attn_bq"), ("k", "L0_attn_bk"),
                        ("v", "L0_attn_bv"), ("o", "L0_attn_bo")):
            BR[nm] = bias_rows("l0b" + nm, dram[dnm][None, :], D)

        A.rebase(TAIL_OFF)
        xtile = A.alloc([128, D], F32, "fa_x")
        lnx_bf = A.alloc([128, D], BF16, "fa_lnx")
        lnxT = A.alloc([128, D], BF16, "fa_lnxT")
        t_bf = A.alloc([128, D], BF16, "fa_tanh")
        q_bf = A.alloc([128, D], BF16, "fa_q")
        k_bf = A.alloc([128, D], BF16, "fa_k")
        v_bf = A.alloc([128, D], BF16, "fa_v")
        qe_bf = A.alloc([128, D], BF16, "fa_qe")
        ke_bf = A.alloc([128, D], BF16, "fa_ke")
        kT = A.alloc([128, D], BF16, "fa_kT")
        qsiT = A.alloc([128, D], BF16, "fa_qsiT")
        kso_bf = A.alloc([128, D], BF16, "fa_kso")
        qsir_bf = A.alloc([128, D], BF16, "fa_qsir")
        vsc_bf = A.alloc([128, D], BF16, "fa_vsc")
        scbf = A.alloc([128, 48], BF16, "fa_scbf")
        bc_t = A.alloc([128, D], BF16, "fa_bc")
        sobc = sibc = scbc = bc_t
        tmp_bf = A.alloc([128, D], BF16, "fa_tmp")
        att_bf = A.alloc([128, D], BF16, "fa_att")
        attT = A.alloc([128, D], BF16, "fa_attT")
        Sm_bf = A.alloc([128, 128], BF16, "fa_Sm")
        KV_sb = A.alloc([128, 512], BF16, "fa_KVsb")
        KV_stg = A.alloc([64, 512], BF16, "fa_KVstg")
        cols = A.alloc([128, 16 * 13], F32, "fa_cols")
        col = lambda i: cols[:, i * 16:(i + 1) * 16]
        cs_bf = A.alloc([128, 16], BF16, "fa_csbf")
        csp_sb = A.alloc([128, 16], BF16, "fa_cspsb")
        e_sb = A.alloc([128, 32], F32, "fa_esb")
        eT_bf = A.alloc([8, 4 * 128], BF16, "fa_eT")

        bufs = {}
        for nm in ("xtile lnx lnxT tanh q k v qe ke kT qsiT kso qsir "
                   "vsc tmp att attT Sm KVsb KVstg csbf cspsb esb eT cols "
                   "sob sib scb bc").split():
            bufs[nm] = kb.buf("fa_" + nm)
        B = bufs

        def hd(ap):
            return ap.rearrange("p (h d) -> p h d", d=64)

        def dh(ap):
            return ap.rearrange("p (d h) -> p h d", h=16)

        def bcast(dst, dst_buf, src_cols, src_buf):
            f = src_cols
            src = bass.AP(f.tensor, f.offset,
                          [[f.ap[0][0], 128], [0, 64], [1, 16]])
            kb.dma("gpsimd", dst.rearrange("p (d h) -> p d h", h=16), src,
                   reads=(src_buf,), writes=(dst_buf,),
                   allow_slow_non_contiguous=True)

        names4 = ("k", "q", "kso", "qsi")

        for c in range(FA_CHUNKS):
            tok0 = c * 128
            kb.dma("sync", xtile, x_hbm[tok0:tok0 + 128, :],
                   reads=(xh[c],), writes=(B["xtile"],))
            kb.op("vector",
                  lambda E, c=c: E.tensor_scalar(
                      lnx_bf, xtile, st_mu[:, c:c + 1], st_rstd[:, c:c + 1],
                      op0=OP.subtract, op1=OP.mult),
                  reads=(B["xtile"], sb_mu, sb_rstd), writes=(B["lnx"],))
            for i in range(8):
                kb.dmat(kb.tq(), lnxT[:, i * 128:(i + 1) * 128],
                        lnx_bf[:, i * 128:(i + 1) * 128],
                        reads=(B["lnx"],), writes=(B["lnxT"],))
            def proj(nm, pb):
                for half in range(2):
                    pidx = pb[half]
                    for i in range(8):
                        kb.op("tensor",
                              lambda E, i=i, half=half, pidx=pidx, nm=nm: E.matmul(
                                  psum[pidx][:, 0:512],
                                  lnxT[:, i * 128:(i + 1) * 128],
                                  W[nm][:, i * D + half * 512:i * D + half * 512 + 512],
                                  start=(i == 0), stop=False),
                              reads=(B["lnxT"], bW[nm]), writes=(psb[pidx],))
                    kb.op("tensor",
                          lambda E, half=half, pidx=pidx, nm=nm: E.matmul(
                              psum[pidx][:, 0:512], ones2[:],
                              BR[nm][0][:, half * 512:half * 512 + 512],
                              start=False, stop=True),
                          reads=(b_ones2, BR[nm][1]), writes=(psb[pidx],))

            def sig_evac(pb, dst, dbuf):
                for half in range(2):
                    kb.op("scalar",
                          lambda E, half=half, pb=pb: E.activation(
                              t_bf[:, half * 512:half * 512 + 512],
                              psum[pb[half]][:, 0:512], AF.Tanh, scale=0.5),
                          reads=(psb[pb[half]],), writes=(B["tanh"],))
                kb.op("vector",
                      lambda E, dst=dst: E.tensor_scalar(dst, t_bf, 0.5, 0.5,
                                                         op0=OP.mult, op1=OP.add),
                      reads=(B["tanh"],), writes=(dbuf,))

            proj("q", (0, 1))
            proj("k", (2, 3))
            sig_evac((0, 1), q_bf, B["q"])
            sig_evac((2, 3), k_bf, B["k"])
            proj("v", (0, 1))
            for half in range(2):
                kb.op("scalar",
                      lambda E, half=half: E.copy(
                          v_bf[:, half * 512:half * 512 + 512],
                          psum[0 + half][:, 0:512]),
                      reads=(psb[0 + half],), writes=(B["v"],))
            kb.op("gpsimd",
                  lambda E: E.tensor_scalar(qe_bf, q_bf, EPSF, None, op0=OP.add),
                  reads=(B["q"],), writes=(B["qe"],))
            kb.op("gpsimd",
                  lambda E: E.tensor_scalar(ke_bf, k_bf, EPSF, None, op0=OP.add),
                  reads=(B["k"],), writes=(B["ke"],))

            def cum_group(pidx, rhs_ap, rhs_buf, eT_slot, c):
                kb.op("tensor",
                      lambda E: E.matmul(psum[pidx][:, 0:512], consts["triu"],
                                         rhs_ap, start=True, stop=True,
                                         skip_group_check=True),
                      reads=(cb["triu"], rhs_buf), writes=(psb[pidx],))
                if c > 0:
                    for ii in range(4):
                        i = (pidx % 2) * 4 + ii
                        kb.op("tensor",
                              lambda E, i=i, ii=ii: E.matmul(
                                  psum[pidx][:, ii * 128:ii * 128 + 128],
                                  consts["onehot8"][:, i * 128:(i + 1) * 128],
                                  eT_bf[:, eT_slot * 128:(eT_slot + 1) * 128],
                                  start=False, stop=True,
                                  skip_group_check=True),
                              reads=(cb["onehot8"], B["eT"]), writes=(psb[pidx],))

            for half in range(2):
                cum_group(half, k_bf[:, half * 512:half * 512 + 512], B["k"], 0, c)
                cum_group(2 + half, q_bf[:, half * 512:half * 512 + 512], B["q"], 1, c)

            if c == 1:
                tap("fa_eT", eT_bf[0:8, 0:4 * 128], (B["eT"],))
            if c in (0, 1) and DEBUG:
                cumdump = A
                kb.op("vector", lambda E: E.tensor_copy(dump[:, 0:512], psum[0][:, 0:512]),
                      reads=(psb[0],), writes=(b_dump,))
                kb.op("vector", lambda E: E.tensor_copy(dump[:, 512:1024], psum[1][:, 0:512]),
                      reads=(psb[1],), writes=(b_dump,))
                tap("fa_cumk" if c == 0 else "fa_cumk1", dump[:, 0:D], (b_dump,))
                kb.op("vector", lambda E: E.tensor_copy(dump[:, 0:512], psum[2][:, 0:512]),
                      reads=(psb[2],), writes=(b_dump,))
                kb.op("vector", lambda E: E.tensor_copy(dump[:, 512:1024], psum[3][:, 0:512]),
                      reads=(psb[3],), writes=(b_dump,))
                tap("fa_cumq" if c == 0 else "fa_cumq1", dump[:, 0:D], (b_dump,))
            for half in range(2):
                kb.op("vector",
                      lambda E, half=half: E.scalar_tensor_tensor(
                          tmp_bf[:, half * 512:half * 512 + 512],
                          psum[half][:, 0:512], EPSF,
                          qe_bf[:, half * 512:half * 512 + 512],
                          op0=OP.add, op1=OP.mult),
                      reads=(psb[half], B["qe"]), writes=(B["tmp"],))
            kb.op("vector",
                  lambda E: E.tensor_reduce(
                      col(0).unsqueeze(2),
                      tmp_bf.rearrange("p (h d) -> p h d", d=64),
                      axis=AX.X, op=OP.add),
                  reads=(B["tmp"],), writes=(B["cols"],))
            for half in range(2):
                kb.op("vector",
                      lambda E, half=half: E.scalar_tensor_tensor(
                          tmp_bf[:, half * 512:half * 512 + 512],
                          psum[2 + half][:, 0:512], EPSF,
                          ke_bf[:, half * 512:half * 512 + 512],
                          op0=OP.add, op1=OP.mult),
                      reads=(psb[2 + half], B["ke"]), writes=(B["tmp"],))
            kb.op("vector",
                  lambda E: E.tensor_reduce(
                      col(1).unsqueeze(2),
                      tmp_bf.rearrange("p (h d) -> p h d", d=64),
                      axis=AX.X, op=OP.add),
                  reads=(B["tmp"],), writes=(B["cols"],))
            kb.op("vector", lambda E: E.reciprocal(col(2), col(0)),
                  reads=(B["cols"],), writes=(B["cols"],))
            kb.op("vector", lambda E: E.reciprocal(col(3), col(1)),
                  reads=(B["cols"],), writes=(B["cols"],))
            kb.op("vector",
                  lambda E, c=c: E.tensor_scalar(col(4), col(2),
                                                 consts["normc"][:, c:c + 1], None,
                                                 op0=OP.mult),
                  reads=(B["cols"], cb["normc"]), writes=(B["cols"],))
            kb.op("vector",
                  lambda E, c=c: E.tensor_scalar(col(5), col(3),
                                                 consts["normc"][:, c:c + 1], None,
                                                 op0=OP.mult),
                  reads=(B["cols"], cb["normc"]), writes=(B["cols"],))
            kb.op("gpsimd", lambda E: E.tensor_copy(scbf[:, 0:16], col(5)),
                  reads=(B["cols"],), writes=(B["sob"],))
            kb.op("scalar", lambda E: E.copy(scbf[:, 16:32], col(4)),
                  reads=(B["cols"],), writes=(B["sib"],))
            bcast(sobc, B["bc"], scbf[:, 0:16], B["sob"])
            bcast(sibc, B["bc"], scbf[:, 16:32], B["sib"])
            kb.op("vector",
                  lambda E: E.tensor_tensor(hd(kso_bf), hd(k_bf), dh(sobc),
                                            op=OP.mult),
                  reads=(B["k"], B["bc"]), writes=(B["kso"],))
            kb.op("vector",
                  lambda E: E.tensor_tensor(hd(qsir_bf), hd(q_bf), dh(sibc),
                                            op=OP.mult),
                  reads=(B["q"], B["bc"]), writes=(B["qsir"],))

            for half in range(2):
                cum_group(half, kso_bf[:, half * 512:half * 512 + 512], B["kso"], 2, c)
                cum_group(2 + half, qsir_bf[:, half * 512:half * 512 + 512], B["qsir"], 3, c)

            for half in range(2):
                kb.op("vector",
                      lambda E, half=half: E.scalar_tensor_tensor(
                          tmp_bf[:, half * 512:half * 512 + 512],
                          psum[half][:, 0:512], EPSF,
                          qe_bf[:, half * 512:half * 512 + 512],
                          op0=OP.add, op1=OP.mult),
                      reads=(psb[half], B["qe"]), writes=(B["tmp"],))
            kb.op("vector",
                  lambda E: E.tensor_reduce(
                      col(6).unsqueeze(2),
                      tmp_bf.rearrange("p (h d) -> p h d", d=64),
                      axis=AX.X, op=OP.add),
                  reads=(B["tmp"],), writes=(B["cols"],))
            for half in range(2):
                kb.op("vector",
                      lambda E, half=half: E.scalar_tensor_tensor(
                          tmp_bf[:, half * 512:half * 512 + 512],
                          psum[2 + half][:, 0:512], EPSF,
                          ke_bf[:, half * 512:half * 512 + 512],
                          op0=OP.add, op1=OP.mult),
                      reads=(psb[2 + half], B["ke"]), writes=(B["tmp"],))
            kb.op("vector",
                  lambda E: E.tensor_reduce(
                      col(7).unsqueeze(2),
                      tmp_bf.rearrange("p (h d) -> p h d", d=64),
                      axis=AX.X, op=OP.add),
                  reads=(B["tmp"],), writes=(B["cols"],))
            kb.op("vector",
                  lambda E, c=c: E.tensor_scalar(col(6), col(6),
                                                 consts["rnormc"][:, c:c + 1], None,
                                                 op0=OP.mult),
                  reads=(B["cols"], cb["rnormc"]), writes=(B["cols"],))
            kb.op("vector",
                  lambda E, c=c: E.tensor_scalar(col(7), col(7),
                                                 consts["rnormc"][:, c:c + 1], None,
                                                 op0=OP.mult),
                  reads=(B["cols"], cb["rnormc"]), writes=(B["cols"],))
            kb.op("scalar",
                  lambda E: E.activation(col(9), col(6), AF.Tanh, scale=0.5),
                  reads=(B["cols"],), writes=(B["cols"],))
            kb.op("vector",
                  lambda E: E.tensor_scalar(col(9), col(9), 0.5, 0.5,
                                            op0=OP.mult, op1=OP.add),
                  reads=(B["cols"],), writes=(B["cols"],))
            kb.op("vector",
                  lambda E: E.tensor_tensor(col(12), col(9), col(2), op=OP.mult),
                  reads=(B["cols"],), writes=(B["cols"],))
            kb.op("scalar",
                  lambda E: E.activation(col(8), col(7), AF.Exp),
                  reads=(B["cols"],), writes=(B["cols"],))
            kb.op("vector", lambda E: E.tensor_copy(cs_bf, col(8)),
                  reads=(B["cols"],), writes=(B["csbf"],))

            if c > 0:
                kb.op("vector", lambda E: E.tensor_copy(csp_sb, psum[7][:, 32:48]),
                      reads=(psb[7],), writes=(B["cspsb"],))
            kb.op("tensor",
                  lambda E: E.matmul(psum[5][:, 0:16], consts["triu"], cs_bf,
                                     start=True, stop=True,
                                     skip_group_check=True),
                  reads=(cb["triu"], B["csbf"]), writes=(psb[5],))
            if c > 0:
                kb.op("tensor",
                      lambda E: E.matmul(psum[5][:, 0:16], consts["ones128"],
                                         csp_sb, start=False, stop=True,
                                         skip_group_check=True),
                      reads=(cb["ones128"], B["cspsb"]), writes=(psb[5],))
            kb.op("vector", lambda E: E.tensor_copy(col(10), psum[5][:, 0:16]),
                  reads=(psb[5],), writes=(B["cols"],))
            kb.op("vector", lambda E: E.reciprocal(col(10), col(10)),
                  reads=(B["cols"],), writes=(B["cols"],))
            kb.op("vector",
                  lambda E: E.tensor_tensor(col(11), col(8), col(10), op=OP.mult),
                  reads=(B["cols"],), writes=(B["cols"],))
            kb.op("vector",
                  lambda E, c=c: E.tensor_scalar(col(11), col(11),
                                                 consts["normc"][:, c:c + 1], None,
                                                 op0=OP.mult),
                  reads=(B["cols"], cb["normc"]), writes=(B["cols"],))
            kb.op("tensor",
                  lambda E, c=c: E.matmul(psum[7][:, 32:48], consts["ident"],
                                          cs_bf, start=(c == 0), stop=True,
                                          skip_group_check=True),
                  reads=(cb["ident"], B["csbf"]), writes=(psb[7],))
            srcs = {"k": (k_bf, "k"), "q": (q_bf, "q"),
                    "kso": (kso_bf, "kso"), "qsi": (qsir_bf, "qsir")}
            for qi_, nm in enumerate(names4):
                ap_, bn = srcs[nm]
                for i in range(8):
                    kb.op("tensor",
                          lambda E, ap_=ap_, i=i, qi_=qi_, c=c: E.matmul(
                              psum[7][:, qi_ * 8 + i:qi_ * 8 + i + 1],
                              ap_[:, i * 128:(i + 1) * 128],
                              consts["ones_col"], start=False, stop=True,
                              skip_group_check=True),
                          reads=(bufs[bn], cb["ones_col"]), writes=(psb[7],))
            if c < NT - 1:
                kb.op("vector", lambda E: E.tensor_copy(e_sb, psum[7][:, 0:32]),
                      reads=(psb[7],), writes=(B["esb"],))
                for qi_ in range(4):
                    kb.op("tensor",
                          lambda E, qi_=qi_: E.transpose(
                              psum[2][0:8, qi_ * 128:qi_ * 128 + 128],
                              e_sb[:, qi_ * 8:(qi_ + 1) * 8], consts["identf32"]),
                          reads=(B["esb"], cb["identf32"]), writes=(psb[2],))
                    kb.op("vector",
                          lambda E, qi_=qi_: E.tensor_copy(
                              eT_bf[:, qi_ * 128:(qi_ + 1) * 128],
                              psum[2][0:8, qi_ * 128:qi_ * 128 + 128]),
                          reads=(psb[2],), writes=(B["eT"],))
            kb.op("gpsimd", lambda E: E.tensor_copy(scbf[:, 32:48], col(11)),
                  reads=(B["cols"],), writes=(B["scb"],))
            bcast(scbc, B["bc"], scbf[:, 32:48], B["scb"])
            kb.op("vector",
                  lambda E: E.tensor_tensor(hd(vsc_bf), hd(v_bf), dh(scbc),
                                            op=OP.mult),
                  reads=(B["v"], B["bc"]), writes=(B["vsc"],))

            for i in range(8):
                kb.dmat(kb.tq(), kT[:, i * 128:(i + 1) * 128],
                        k_bf[:, i * 128:(i + 1) * 128],
                        reads=(B["k"],), writes=(B["kT"],))
                kb.dmat(kb.tq(), qsiT[:, i * 128:(i + 1) * 128],
                        q_bf[:, i * 128:(i + 1) * 128],
                        reads=(B["q"],), writes=(B["qsiT"],))
            if c > 0:
                kb.op("vector",
                      lambda E: E.tensor_copy(KV_sb[0:64, :], psum[6][0:64, 0:512]),
                      reads=(psb[6],), writes=(B["KVsb"],))
                kb.op("vector",
                      lambda E: E.tensor_copy(KV_stg[:], psum[4][0:64, 0:512]),
                      reads=(psb[4],), writes=(B["KVstg"],))
                kb.dma("sync", KV_sb[64:128, :], KV_stg[:],
                       reads=(B["KVstg"],), writes=(B["KVsb"],))

            if c == 1:
                tap("fa_KV", KV_sb[:, 0:512], (B["KVsb"],))
            for h in range(NH):
                i_ = h // 2
                r0 = (h % 2) * 64
                stb = 5 if h % 2 == 0 else 3
                ob = h % 2
                Smt = Sm_bf if h % 2 == 0 else tmp_bf[:, 0:128]
                Smb = B["Sm"] if h % 2 == 0 else B["tmp"]
                kb.op("tensor",
                      lambda E, i_=i_, r0=r0, stb=stb: E.matmul(
                          psum[stb][:, 0:128],
                          kT[r0:r0 + 64, i_ * 128:(i_ + 1) * 128],
                          qsiT[r0:r0 + 64, i_ * 128:(i_ + 1) * 128],
                          start=True, stop=True),
                      reads=(B["kT"], B["qsiT"]), writes=(psb[stb],))
                kb.op("vector",
                      lambda E, stb=stb, Smt=Smt: E.tensor_tensor(
                          Smt, psum[stb][:, 0:128],
                          consts["triu"], op=OP.mult),
                      reads=(psb[stb], cb["triu"]), writes=(Smb,))
                kb.op("tensor",
                      lambda E, h=h, ob=ob, Smt=Smt: E.matmul(
                          psum[ob][:, 0:64], Smt,
                          vsc_bf[:, h * 64:(h + 1) * 64],
                          start=True, stop=True),
                      reads=(Smb, B["vsc"]), writes=(psb[ob],))
                if c > 0:
                    kb.op("tensor",
                          lambda E, h=h, i_=i_, r0=r0, ob=ob: E.matmul(
                              psum[ob][:, 0:64],
                              qsiT[r0:r0 + 64, i_ * 128:(i_ + 1) * 128],
                              KV_sb[r0:r0 + 64, i_ * 64:(i_ + 1) * 64],
                              start=False, stop=True,
                              skip_group_check=True),
                          reads=(B["qsiT"], B["KVsb"]), writes=(psb[ob],))
                kb.op("vector",
                      lambda E, h=h, ob=ob: E.tensor_scalar(
                          att_bf[:, h * 64:(h + 1) * 64], psum[ob][:, 0:64],
                          col(12)[:, h:h + 1], None, op0=OP.mult),
                      reads=(psb[ob], B["cols"]), writes=(B["att"],))
                kvb = 6 if h % 2 == 0 else 4
                kb.op("tensor",
                      lambda E, h=h, i_=i_, c=c, kvb=kvb: E.matmul(
                          psum[kvb][0:64, i_ * 64:(i_ + 1) * 64],
                          k_bf[:, h * 64:(h + 1) * 64],
                          vsc_bf[:, h * 64:(h + 1) * 64],
                          start=(c == 0 and h < 2), stop=True,
                          skip_group_check=True),
                      reads=(B["k"], B["vsc"]), writes=(psb[kvb],))


            if c == (1 if FA_CHUNKS > 1 else 0):
                tap("fa_qe", qe_bf[:, 0:D], (B["qe"],))
                tap("fa_tmp", tmp_bf[:, 0:D], (B["tmp"],))
                tap("fa_q", q_bf[:, 0:D], (B["q"],))
                tap("fa_k", k_bf[:, 0:D], (B["k"],))
                tap("fa_v", v_bf[:, 0:D], (B["v"],))
                tap("fa_cols", cols[:, 0:16 * 12], (B["cols"],))
                tap("fa_kso", kso_bf[:, 0:D], (B["kso"],))
                tap("fa_vsc", vsc_bf[:, 0:D], (B["vsc"],))
                tap("fa_att", att_bf[:, 0:D], (B["att"],))
            for i in range(8):
                kb.dmat(kb.tq(), attT[:, i * 128:(i + 1) * 128],
                        att_bf[:, i * 128:(i + 1) * 128],
                        reads=(B["att"],), writes=(B["attT"],))
            for half in range(2):
                pidx = 2 + half
                for i in range(8):
                    kb.op("tensor",
                          lambda E, i=i, half=half, pidx=pidx: E.matmul(
                              psum[pidx][:, 0:512],
                              attT[:, i * 128:(i + 1) * 128],
                              W["o"][:, i * D + half * 512:i * D + half * 512 + 512],
                              start=(i == 0), stop=False),
                          reads=(B["attT"], bW["o"]), writes=(psb[pidx],))
                kb.op("tensor",
                      lambda E, half=half, pidx=pidx: E.matmul(
                          psum[pidx][:, 0:512], ones2[:],
                          BR["o"][0][:, half * 512:half * 512 + 512],
                          start=False, stop=True),
                      reads=(b_ones2, BR["o"][1]), writes=(psb[pidx],))
                kb.op("vector",
                      lambda E, half=half, pidx=pidx: E.tensor_tensor(
                          xtile[:, half * 512:half * 512 + 512],
                          xtile[:, half * 512:half * 512 + 512],
                          psum[pidx][:, 0:512], op=OP.add),
                      reads=(psb[pidx],), writes=(B["xtile"],))
            kb.dma("sync", x_hbm[tok0:tok0 + 128, :], xtile,
                   reads=(B["xtile"],), writes=(xh[c],))
        A.pop()
        return xh

    if STAGES >= 1:
        l0_ffn()



    def hbm_reader(n=2):
        tiles = [A.alloc([128, D], F32, f"xr{j}") for j in range(n)]
        tb = [kb.buf() for j in range(n)]

        def read_tile(t):
            j = t % n
            kb.dma("sync", tiles[j], x_hbm[t * 128:(t + 1) * 128, :],
                   reads=(xh[t],), writes=(tb[j],))
            return tiles[j], (tb[j],)
        return read_tile


    def _stream_rhs_mm(w_dram, rows0, cols0, ncols, lhsT_ap, lhsT_buf, m_rows,
                       pidx, stg, b_stg, stgbf, b_stgbf, extra_reads=()):
        for i in range(8):
            kb.dma("sync", stg[:, 0:ncols],
                   w_dram[rows0 + i * 128:rows0 + (i + 1) * 128,
                          cols0:cols0 + ncols],
                   writes=(b_stg,))
            kb.op("vector", lambda E, ncols=ncols: E.tensor_copy(
                stgbf[:, 0:ncols], stg[:, 0:ncols]),
                reads=(b_stg,), writes=(b_stgbf,))
            kb.op("tensor",
                  lambda E, i=i, pidx=pidx, m_rows=m_rows, ncols=ncols: E.matmul(
                      psum[pidx][0:m_rows, 0:ncols],
                      lhsT_ap[:, i * m_rows:(i + 1) * m_rows],
                      stgbf[:, 0:ncols], start=(i == 0), stop=False),
                  reads=(b_stgbf, lhsT_buf) + tuple(extra_reads),
                  writes=(psb[pidx],))

    def bias_mm(pidx, m_rows, rows_t, cols0, ncols):
        kb.op("tensor",
              lambda E: E.matmul(psum[pidx][0:m_rows, 0:ncols],
                                 ones2[:, 0:m_rows],
                                 rows_t[0][:, cols0:cols0 + ncols],
                                 start=False, stop=True),
              reads=(b_ones2, rows_t[1]), writes=(psb[pidx],))

    def l1_ffn_prep():
        kb.set_fence()
        A.push()
        A.rebase(0, limit=128 * 1024)
        p = l1f
        p["w1a"] = A.alloc([128, 8 * 512], BF16, "lf_w1a")
        p["b_w1a"] = kb.buf()
        p["w2"] = A.alloc([128, 4 * 64], BF16, "lf_w2")
        p["b_w2"] = kb.buf()
        p["Sini"] = A.alloc([64, D], F32, "lf_Sini")
        p["b_Sini"] = kb.buf()
        p["SiniT"] = A.alloc([128, 512], BF16, "lf_SiniT")
        p["b_SiniT"] = kb.buf()
        p["gh"] = A.alloc([64, 3 * D], BF16, "lf_gh")
        p["b_gh"] = kb.buf()
        p["stau"] = A.alloc([128, 1], F32, "lf_stau")
        p["b_stau"] = kb.buf()
        p["lngb"] = A.alloc([64, 2 * D], F32, "lf_lngb")
        p["b_lngb"] = kb.buf()
        p["logits"] = A.alloc([128, NT * 64], F32, "lf_logits")
        p["top8"] = A.alloc([128, NT * 8], F32, "lf_top8")
        p["alpha"] = A.alloc([128, NT * 64], BF16, "lf_alpha")
        p["alphaT"] = A.alloc([64, NT * 128], BF16, "lf_alphaT")
        p["hb"] = A.alloc([2, 512], BF16, "lf_hb")
        p["b_hb"] = kb.buf()
        A.push()
        A.rebase(TAIL_OFF)
        set_bias_stage(3 * D)
        A.pop()
        p["bih"] = bias_rows("lf_bih", dram["L1_ffn_bih"][None, :], 3 * D)
        p["bv"] = bias_rows("lf_bv", dram["L1_ffn_bv"][None, :], D)
        p["bvp"] = bias_rows("lf_bvp", dram["L1_ffn_bvp"][None, :], D)
        p["bo"] = bias_rows("lf_bo", dram["L1_ffn_bo"][None, :], D)
        p["b2r"] = bias_rows("lf_b2", dram["L1_ffn_b2"][None, :], 64)
        b1r = bias_rows("lf_b1", dram["L1_ffn_b1"][None, :], 512)
        bhh = bias_rows("lf_bhh", dram["L1_ffn_bhh"][None, :], 3 * D)
        l1f["end"] = A.off
        kb.set_fence()
        A.push()
        A.rebase(TAIL_OFF)
        stg = A.alloc([128, D], F32, "lf_stg")
        b_stg = kb.buf()
        stgbf = A.alloc([128, D], BF16, "lf_stgbf")
        b_stgbf = kb.buf()
        for i in range(8):
            kb.dma("sync", stg[:, 0:512],
                   dram["L1_ffn_w1"][i * 128:(i + 1) * 128, :], writes=(b_stg,))
            kb.op("vector", lambda E, i=i: E.tensor_copy(
                p["w1a"][:, i * 512:(i + 1) * 512], stg[:, 0:512]),
                reads=(b_stg,), writes=(p["b_w1a"],))
        for i in range(4):
            kb.dma("sync", stg[:, 0:64],
                   dram["L1_ffn_w2"][i * 128:(i + 1) * 128, :], writes=(b_stg,))
            kb.op("vector", lambda E, i=i: E.tensor_copy(
                p["w2"][:, i * 64:(i + 1) * 64], stg[:, 0:64]),
                reads=(b_stg,), writes=(p["b_w2"],))
        kb.dma("sync", p["Sini"], dram["L1_ffn_slot_init"], writes=(p["b_Sini"],))
        kb.op("vector", lambda E: E.tensor_copy(stgbf[0:64, :], p["Sini"]),
              reads=(p["b_Sini"],), writes=(b_stgbf,))
        for i in range(8):
            kb.dmat(kb.tq(), p["SiniT"][:, i * 64:(i + 1) * 64],
                    stgbf[0:64, i * 128:(i + 1) * 128],
                    reads=(b_stgbf,), writes=(p["b_SiniT"],))
        kb.dma("sync", p["lngb"][0:64, 0:D],
               dram["L1_ffn_ln_g"][None, :].partition_broadcast(64),
               writes=(p["b_lngb"],))
        kb.dma("sync", p["lngb"][0:64, D:2 * D],
               dram["L1_ffn_ln_b"][None, :].partition_broadcast(64),
               writes=(p["b_lngb"],))
        tau1 = A.alloc([1, 4], F32, "lf_tau")
        b_tau = kb.buf()
        kb.dma("sync", tau1[0:1, 0:1], dram["L1_ffn_tau"][None, :], writes=(b_tau,))
        kb.op("scalar", lambda E: E.activation(
            tau1[0:1, 1:2], tau1[0:1, 0:1], AF.Abs),
            reads=(b_tau,), writes=(b_tau,))
        kb.op("vector", lambda E: E.tensor_scalar(
            tau1[0:1, 1:2], tau1[0:1, 1:2], 0.1, None, op0=OP.add),
            reads=(b_tau,), writes=(b_tau,))
        kb.op("vector", lambda E: E.reciprocal(tau1[0:1, 2:3], tau1[0:1, 1:2]),
              reads=(b_tau,), writes=(b_tau,))
        kb.dma("sync", row_hbm[0:1, 0:1], tau1[0:1, 2:3], reads=(b_tau,),
               writes=(b_tau,))
        kb.dma("sync", p["stau"], row_hbm[0:1, 0:1].partition_broadcast(128),
               reads=(b_tau,), writes=(p["b_stau"],))
        kb.op("tensor",
              lambda E: E.matmul(psum[0][0:1, 0:512], consts["o64_col"],
                                 stgbf[0:64, 0:512], start=True, stop=True),
              reads=(b_stgbf, cb["o64_col"]), writes=(psb[0],))
        kb.op("tensor",
              lambda E: E.matmul(psum[1][0:1, 0:512], consts["o64_col"],
                                 stgbf[0:64, 512:1024], start=True, stop=True),
              reads=(b_stgbf, cb["o64_col"]), writes=(psb[1],))
        sm = A.alloc([1, D], F32, "lf_sm")
        b_sm = kb.buf()
        kb.op("vector", lambda E: E.tensor_scalar(
            sm[0:1, 0:512], psum[0][0:1, 0:512], 1.0 / 64, None, op0=OP.mult),
            reads=(psb[0],), writes=(b_sm,))
        kb.op("vector", lambda E: E.tensor_scalar(
            sm[0:1, 512:1024], psum[1][0:1, 0:512], 1.0 / 64, None, op0=OP.mult),
            reads=(psb[1],), writes=(b_sm,))
        kb.dma("sync", row_hbm[1:2, 0:D], sm[0:1, :], reads=(b_sm,), writes=(b_sm,))
        smT = A.alloc([128, 8], F32, "lf_smT")
        smTb = A.alloc([128, 8], BF16, "lf_smTb")
        b_smT = kb.buf()
        for i in range(8):
            kb.dma("gpsimd", smT[:, i:i + 1],
                   row_hbm[1:2, i * 128:(i + 1) * 128].rearrange("a b -> b a"),
                   reads=(b_sm,), writes=(b_smT,), allow_slow_non_contiguous=True)
        kb.op("vector", lambda E: E.tensor_copy(smTb, smT),
              reads=(b_smT,), writes=(b_smT,))
        for i in range(8):
            kb.dma("sync", stg[:, 0:512],
                   dram["L1_ffn_w1"][D + i * 128:D + (i + 1) * 128, :],
                   writes=(b_stg,))
            kb.op("vector", lambda E: E.tensor_copy(stgbf[:, 0:512], stg[:, 0:512]),
                  reads=(b_stg,), writes=(b_stgbf,))
            kb.op("tensor",
                  lambda E, i=i: E.matmul(psum[2][0:1, 0:512], smTb[:, i:i + 1],
                                          stgbf[:, 0:512], start=(i == 0),
                                          stop=False),
                  reads=(b_smT, b_stgbf), writes=(psb[2],))
        bias_mm(2, 1, b1r, 0, 512)
        hbf = A.alloc([1, 512], F32, "lf_hbf")
        hblo = A.alloc([1, 512], BF16, "lf_hblo")
        b_hbf = kb.buf()
        kb.op("vector", lambda E: E.tensor_copy(hbf, psum[2][0:1, 0:512]),
              reads=(psb[2],), writes=(b_hbf,))
        kb.op("vector", lambda E: E.tensor_copy(p["hb"][0:1, :], hbf),
              reads=(b_hbf,), writes=(p["b_hb"],))
        kb.op("vector", lambda E: E.tensor_tensor(hblo, hbf, p["hb"][0:1, :],
                                                  op=OP.subtract),
              reads=(b_hbf, p["b_hb"]), writes=(b_hbf,))
        kb.dma("sync", p["hb"][1:2, :], hblo, reads=(b_hbf,), writes=(p["b_hb"],))
        _gru_mm(dram["L1_ffn_Whh"], p["SiniT"], p["b_SiniT"], bhh,
                p["gh"], p["b_gh"], stg, b_stg)
        A.pop()
        A.pop()

    def _gru_mm(w_dram, lhsT_ap, lhsT_buf, brows, out_sb, out_buf, stg, b_stg):
        A.push()
        wT = A.alloc([128, 8 * D], BF16, "gru_wT")
        b_wT = kb.buf()
        wbfs = A.alloc([128, D], BF16, "gru_wbf")
        b_wbfs = kb.buf()
        for g in range(3):
            for oc in range(8):
                kb.dma("sync", stg,
                       w_dram[g * D + oc * 128:g * D + (oc + 1) * 128, :],
                       writes=(b_stg,))
                kb.op("vector", lambda E: E.tensor_copy(wbfs, stg),
                      reads=(b_stg,), writes=(b_wbfs,))
                for i in range(8):
                    kb.dmat(kb.tq(), wT[:, i * D + oc * 128:i * D + (oc + 1) * 128],
                            wbfs[:, i * 128:(i + 1) * 128],
                            reads=(b_wbfs,), writes=(b_wT,))
            for half in range(2):
                pidx = half
                for i in range(8):
                    kb.op("tensor",
                          lambda E, i=i, half=half, pidx=pidx: E.matmul(
                              psum[pidx][0:64, 0:512],
                              lhsT_ap[:, i * 64:(i + 1) * 64],
                              wT[:, i * D + half * 512:i * D + half * 512 + 512],
                              start=(i == 0), stop=False),
                          reads=(lhsT_buf, b_wT), writes=(psb[pidx],))
                kb.op("tensor",
                      lambda E, half=half, pidx=pidx, g=g: E.matmul(
                          psum[pidx][0:64, 0:512], ones2[:, 0:64],
                          brows[0][:, g * D + half * 512:g * D + half * 512 + 512],
                          start=False, stop=True),
                      reads=(b_ones2, brows[1]), writes=(psb[pidx],))
                kb.op("vector",
                      lambda E, half=half, pidx=pidx, g=g: E.tensor_copy(
                          out_sb[0:64, g * D + half * 512:g * D + half * 512 + 512],
                          psum[pidx][0:64, 0:512]),
                      reads=(psb[pidx],), writes=(out_buf,))
        A.pop()

    def l1_ffn():
        p = l1f
        kb.set_fence()
        A.push()
        A.rebase(l1f["end"], limit=128 * 1024)
        lnx_bf = A.alloc([128, D], BF16, "l1_lnx")
        lnxT = A.alloc([128, D], BF16, "l1_lnxT")
        h_bf = A.alloc([128, 512], BF16, "l1_h")
        hT = A.alloc([128, 512], BF16, "l1_hT")
        t64 = A.alloc([128, 64], F32, "l1_t64")
        m64 = A.alloc([128, 8], F32, "l1_m64")
        cols = A.alloc([128, 8], F32, "l1_cols")
        alpha32 = A.alloc([128, 64], F32, "l1_a32")
        b_a32 = kb.buf()
        Bb = {nm: kb.buf("l1_" + nm) for nm in
              "lnx lnxT h hT t64 m64 cols".split()}
        A.push()
        rd_stats = hbm_reader()
        ln_stats_pass(rd_stats)
        rd_a = hbm_reader()
        for t in range(NT):
            xa, xab = rd_a(t)
            kb.op("vector",
                  lambda E, t=t, xa=xa: E.tensor_scalar(
                      lnx_bf, xa, st_mu[:, t:t + 1], st_rstd[:, t:t + 1],
                      op0=OP.subtract, op1=OP.mult),
                  reads=xab + (sb_mu, sb_rstd), writes=(Bb["lnx"],))
            for i in range(8):
                kb.dmat(kb.tq(), lnxT[:, i * 128:(i + 1) * 128],
                        lnx_bf[:, i * 128:(i + 1) * 128],
                        reads=(Bb["lnx"],), writes=(Bb["lnxT"],))
            for i in range(8):
                kb.op("tensor",
                      lambda E, i=i: E.matmul(
                          psum[0][:, 0:512], lnxT[:, i * 128:(i + 1) * 128],
                          p["w1a"][:, i * 512:(i + 1) * 512],
                          start=(i == 0), stop=False),
                      reads=(Bb["lnxT"], p["b_w1a"]), writes=(psb[0],))
            kb.op("tensor",
                  lambda E: E.matmul(psum[0][:, 0:512], ones2[:], p["hb"][:],
                                     start=False, stop=True),
                  reads=(b_ones2, p["b_hb"]), writes=(psb[0],))
            kb.op("scalar",
                  lambda E: E.activation(h_bf, psum[0][:, 0:512], AF.Gelu),
                  reads=(psb[0],), writes=(Bb["h"],))
            for i in range(4):
                kb.dmat(kb.tq(), hT[:, i * 128:(i + 1) * 128],
                        h_bf[:, i * 128:(i + 1) * 128],
                        reads=(Bb["h"],), writes=(Bb["hT"],))
            for i in range(4):
                kb.op("tensor",
                      lambda E, i=i: E.matmul(
                          psum[1][:, 0:64], hT[:, i * 128:(i + 1) * 128],
                          p["w2"][:, i * 64:(i + 1) * 64],
                          start=(i == 0), stop=False),
                      reads=(Bb["hT"], p["b_w2"]), writes=(psb[1],))
            bias_mm(1, 128, p["b2r"], 0, 64)
            kb.op("vector",
                  lambda E, t=t: E.tensor_scalar(
                      p["logits"][:, t * 64:(t + 1) * 64], psum[1][:, 0:64],
                      p["stau"][:, 0:1], None, op0=OP.mult),
                  reads=(psb[1], p["b_stau"]), writes=(kb.buf(),))
            kb.op("vector",
                  lambda E, t=t: E.max(p["top8"][:, t * 8:(t + 1) * 8],
                                       p["logits"][:, t * 64:(t + 1) * 64]),
                  reads=(), writes=(kb.buf(),))
        blog = kb.buf("logits_all")
        btop = kb.buf("top8_all")
        kb.op("vector", lambda E: E.engine_nop(), reads=(), writes=(blog, btop))
        rd_a2 = hbm_reader()
        for t in range(NT):
            lg = p["logits"][:, t * 64:(t + 1) * 64]
            t8 = p["top8"][:, t * 8:(t + 1) * 8]
            kb.op("vector",
                  lambda E, t8=t8: E.tensor_scalar(cols[:, 0:1], t8[:, 0:1], -1.0,
                                                   None, op0=OP.mult),
                  reads=(btop,), writes=(Bb["cols"],))
            kb.op("scalar",
                  lambda E, t8=t8: E.activation(m64[:, 0:8], t8, AF.Exp,
                                                bias=cols[:, 0:1],
                                                accum_out=cols[:, 1:2]),
                  reads=(btop, Bb["cols"]), writes=(Bb["m64"], Bb["cols"]))
            kb.op("scalar",
                  lambda E, lg=lg: E.activation(t64, lg, AF.Exp,
                                                bias=cols[:, 0:1]),
                  reads=(blog, Bb["cols"]), writes=(Bb["t64"],))
            kb.op("vector", lambda E: E.reciprocal(cols[:, 2:3], cols[:, 1:2]),
                  reads=(Bb["cols"],), writes=(Bb["cols"],))
            kb.op("vector",
                  lambda E, t8=t8: E.tensor_scalar(
                      m64[:, 0:1], t8[:, 7:8], 1.0, None, op0=OP.mult),
                  reads=(btop,), writes=(Bb["m64"],))
            kb.op("vector",
                  lambda E, lg=lg: E.scalar_tensor_tensor(
                      t64, lg, m64[:, 0:1], t64, op0=OP.is_ge, op1=OP.mult),
                  reads=(blog, Bb["m64"], Bb["t64"]), writes=(Bb["t64"],))
            kb.op("vector",
                  lambda E: E.tensor_scalar(alpha32, t64, cols[:, 2:3],
                                            None, op0=OP.mult),
                  reads=(Bb["t64"], Bb["cols"]), writes=(b_a32,))
            kb.op("vector",
                  lambda E, t=t: E.tensor_copy(
                      p["alpha"][:, t * 64:(t + 1) * 64], alpha32),
                  reads=(b_a32,), writes=(kb.buf(),))
            kb.op("tensor",
                  lambda E, t=t: E.transpose(
                      psum[2][0:64, 0:128], alpha32, consts["identf32"]),
                  reads=(b_a32, cb["identf32"]), writes=(psb[2],))
            kb.op("vector",
                  lambda E, t=t: E.tensor_copy(
                      p["alphaT"][0:64, t * 128:(t + 1) * 128],
                      psum[2][0:64, 0:128]),
                  reads=(psb[2],), writes=(kb.buf(),))
            xa, xab = rd_a2(t)
            kb.op("vector",
                  lambda E, t=t, xa=xa: E.tensor_scalar(
                      lnx_bf, xa, st_mu[:, t:t + 1], st_rstd[:, t:t + 1],
                      op0=OP.subtract, op1=OP.mult),
                  reads=xab + (sb_mu, sb_rstd), writes=(Bb["lnx"],))
            for half in range(2):
                kb.op("tensor",
                      lambda E, t=t, half=half: E.matmul(
                          psum[half][0:64, 0:512],
                          p["alpha"][:, t * 64:(t + 1) * 64],
                          lnx_bf[:, half * 512:half * 512 + 512],
                          start=(t == 0), stop=(t == NT - 1),
                          skip_group_check=True),
                      reads=(Bb["lnx"],), writes=(psb[half],))
            kb.op("tensor",
                  lambda E, t=t: E.matmul(
                      psum[3][0:64, 0:1], p["alpha"][:, t * 64:(t + 1) * 64],
                      consts["ones_col"], start=(t == 0), stop=(t == NT - 1),
                      skip_group_check=True),
                  reads=(cb["ones_col"],), writes=(psb[3],))
        balpha = kb.buf("alpha_all")
        balphaT = kb.buf("alphaT_all")
        kb.op("vector", lambda E: E.engine_nop(), reads=(), writes=(balpha, balphaT))

        A.pop()
        kb.set_fence()
        A.push()
        sl = A.alloc([64, D], F32, "l1_sl")
        sl2 = A.alloc([64, D], F32, "l1_sl2")
        sl3 = A.alloc([64, D], F32, "l1_sl3")
        Snew = A.alloc([64, D], F32, "l1_Snew")
        sbf = A.alloc([64, D], BF16, "l1_sbf")
        sT = A.alloc([128, 512], BF16, "l1_sT")
        qkT = A.alloc([128, 256], BF16, "l1_qkT")
        attn64 = A.alloc([64, 64], BF16, "l1_attn")
        sv_bf = A.alloc([64, D], BF16, "l1_sv")
        scol = A.alloc([64, 16], F32, "l1_scol")
        bqT = A.alloc([128, 8], F32, "l1_bqT")
        gi_sb = A.alloc([64, 3 * D], BF16, "l1_gi")
        stg2 = A.alloc([128, D], F32, "l1_stg")
        stg2bf = A.alloc([128, D], BF16, "l1_stgbf")
        S = {nm: kb.buf("l1s_" + nm) for nm in
             "sl sl2 sl3 Snew sbf sT qkT attn64 sv scol bqT gi stg stgbf".split()}

        def slot_T(src_ap, src_buf):
            kb.op("vector", lambda E, src_ap=src_ap: E.tensor_copy(sbf, src_ap),
                  reads=(src_buf,), writes=(S["sbf"],))
            for i in range(8):
                kb.dmat(kb.tq(), sT[:, i * 64:(i + 1) * 64],
                        sbf[0:64, i * 128:(i + 1) * 128],
                        reads=(S["sbf"],), writes=(S["sT"],))

        kb.op("vector", lambda E: E.tensor_scalar(
            scol[:, 0:1], psum[3][0:64, 0:1], 1e-8, None, op0=OP.add),
            reads=(psb[3],), writes=(S["scol"],))
        kb.op("vector", lambda E: E.reciprocal(scol[:, 1:2], scol[:, 0:1]),
              reads=(S["scol"],), writes=(S["scol"],))
        for half in range(2):
            kb.op("vector",
                  lambda E, half=half: E.tensor_scalar(
                      sl[0:64, half * 512:half * 512 + 512],
                      psum[half][0:64, 0:512], scol[:, 1:2], None, op0=OP.mult),
                  reads=(psb[half], S["scol"]), writes=(S["sl"],))
        slot_T(sl[0:64, :], S["sl"])
        _gru_mm(dram["L1_ffn_Wih"], sT, S["sT"], p["bih"], gi_sb, S["gi"],
                stg2, S["stg"])
        def gate_pre(g, dst, dbuf):
            kb.op("vector",
                  lambda E, g=g, dst=dst: E.tensor_tensor(
                      dst, gi_sb[0:64, g * D:(g + 1) * D],
                      p["gh"][0:64, g * D:(g + 1) * D], op=OP.add),
                  reads=(S["gi"], p["b_gh"]), writes=(dbuf,))

        def sigm(ap, buf):
            kb.op("scalar", lambda E: E.activation(ap, ap, AF.Tanh, scale=0.5),
                  reads=(buf,), writes=(buf,))
            kb.op("vector", lambda E: E.tensor_scalar(ap, ap, 0.5, 0.5,
                                                      op0=OP.mult, op1=OP.add),
                  reads=(buf,), writes=(buf,))

        gate_pre(0, sl2[0:64, :], S["sl2"])
        sigm(sl2[0:64, :], S["sl2"])
        gate_pre(1, sl3[0:64, :], S["sl3"])
        sigm(sl3[0:64, :], S["sl3"])
        kb.op("vector",
              lambda E: E.tensor_tensor(sl[0:64, :], sl2[0:64, :],
                                        p["gh"][0:64, 2 * D:3 * D], op=OP.mult),
              reads=(S["sl2"], p["b_gh"]), writes=(S["sl"],))
        kb.op("vector",
              lambda E: E.tensor_tensor(sl[0:64, :], sl[0:64, :],
                                        gi_sb[0:64, 2 * D:3 * D], op=OP.add),
              reads=(S["sl"], S["gi"]), writes=(S["sl"],))
        kb.op("scalar", lambda E: E.activation(sl[0:64, :], sl[0:64, :], AF.Tanh),
              reads=(S["sl"],), writes=(S["sl"],))
        kb.op("vector",
              lambda E: E.tensor_tensor(Snew[0:64, :], p["Sini"][0:64, :],
                                        sl[0:64, :], op=OP.subtract),
              reads=(p["b_Sini"], S["sl"]), writes=(S["Snew"],))
        kb.op("vector",
              lambda E: E.tensor_tensor(Snew[0:64, :], Snew[0:64, :],
                                        sl3[0:64, :], op=OP.mult),
              reads=(S["Snew"], S["sl3"]), writes=(S["Snew"],))
        kb.op("vector",
              lambda E: E.tensor_tensor(Snew[0:64, :], Snew[0:64, :],
                                        sl[0:64, :], op=OP.add),
              reads=(S["Snew"], S["sl"]), writes=(S["Snew"],))
        slot_T(Snew[0:64, :], S["Snew"])
        kb.dma("sync", row_hbm[2:3, 0:256], dram["L1_ffn_bq"][None, :],
               writes=(S["bqT"],))
        kb.dma("sync", row_hbm[3:4, 0:256], dram["L1_ffn_bk"][None, :],
               writes=(S["bqT"],))
        for i in range(2):
            kb.dma("gpsimd", bqT[:, i:i + 1],
                   row_hbm[2:3, i * 128:(i + 1) * 128].rearrange("a b -> b a"),
                   reads=(S["bqT"],), writes=(S["bqT"],),
                   allow_slow_non_contiguous=True)
            kb.dma("gpsimd", bqT[:, 2 + i:3 + i],
                   row_hbm[3:4, i * 128:(i + 1) * 128].rearrange("a b -> b a"),
                   reads=(S["bqT"],), writes=(S["bqT"],),
                   allow_slow_non_contiguous=True)
        kb.op("vector", lambda E: E.tensor_scalar(bqT[:, 0:2], bqT[:, 0:2],
                                                  0.0625, None, op0=OP.mult),
              reads=(S["bqT"],), writes=(S["bqT"],))
        for nm, base, bcol, sc_ in (("L1_ffn_wq", 0, 0, 0.0625),
                                    ("L1_ffn_wk", 128, 2, 1.0)):
            for i in range(8):
                kb.dma("sync", stg2[:, 0:256], dram[nm][i * 128:(i + 1) * 128, :],
                       writes=(S["stg"],))
                kb.op("vector", lambda E: E.tensor_copy(stg2bf[:, 0:256],
                                                        stg2[:, 0:256]),
                      reads=(S["stg"],), writes=(S["stgbf"],))
                for dqc in range(2):
                    kb.op("tensor",
                          lambda E, i=i, dqc=dqc: E.matmul(
                              psum[4 + dqc][0:128, 0:64],
                              stg2bf[:, dqc * 128:(dqc + 1) * 128],
                              sT[:, i * 64:(i + 1) * 64],
                              start=(i == 0), stop=(i == 7)),
                          reads=(S["stgbf"], S["sT"]), writes=(psb[4 + dqc],))
            for dqc in range(2):
                kb.op("scalar",
                      lambda E, dqc=dqc, base=base, sc_=sc_, bcol=bcol: E.activation(
                          qkT[:, base + dqc * 64:base + dqc * 64 + 64],
                          psum[4 + dqc][0:128, 0:64], AF.Identity,
                          bias=bqT[:, bcol + dqc:bcol + dqc + 1], scale=sc_),
                      reads=(psb[4 + dqc], S["bqT"]), writes=(S["qkT"],))
        for dqc in range(2):
            kb.op("tensor",
                  lambda E, dqc=dqc: E.matmul(
                      psum[2][0:64, 0:64], qkT[:, dqc * 64:dqc * 64 + 64],
                      qkT[:, 128 + dqc * 64:128 + dqc * 64 + 64],
                      start=(dqc == 0), stop=(dqc == 1)),
                  reads=(S["qkT"],), writes=(psb[2],))
        kb.op("vector",
              lambda E: E.reduce_max(scol[:, 2:3], psum[2][0:64, 0:64], axis=AX.X),
              reads=(psb[2],), writes=(S["scol"],))
        kb.op("vector", lambda E: E.tensor_scalar(scol[:, 2:3], scol[:, 2:3],
                                                  -1.0, None, op0=OP.mult),
              reads=(S["scol"],), writes=(S["scol"],))
        kb.op("scalar",
              lambda E: E.activation(sl2[0:64, 0:64], psum[2][0:64, 0:64], AF.Exp,
                                     bias=scol[:, 2:3], accum_out=scol[:, 3:4]),
              reads=(psb[2], S["scol"]), writes=(S["sl2"], S["scol"]))
        kb.op("vector", lambda E: E.reciprocal(scol[:, 4:5], scol[:, 3:4]),
              reads=(S["scol"],), writes=(S["scol"],))
        kb.op("vector",
              lambda E: E.tensor_scalar(sl2[0:64, 0:64], sl2[0:64, 0:64],
                                        scol[:, 4:5], None, op0=OP.mult),
              reads=(S["sl2"], S["scol"]), writes=(S["sl2"],))
        kb.op("tensor",
              lambda E: E.transpose(psum[2][0:64, 0:64], sl2[0:64, 0:64],
                                    consts["identf32"][0:64, 0:64]),
              reads=(S["sl2"], cb["identf32"]), writes=(psb[2],))
        kb.op("vector",
              lambda E: E.tensor_copy(attn64, psum[2][0:64, 0:64]),
              reads=(psb[2],), writes=(S["attn64"],))
        for half in range(2):
            _stream_rhs_mm(dram["L1_ffn_wv"], 0, half * 512, 512, sT, S["sT"],
                           64, 4 + half, stg2, S["stg"], stg2bf, S["stgbf"])
            bias_mm(4 + half, 64, p["bv"], half * 512, 512)
            kb.op("vector",
                  lambda E, half=half: E.tensor_copy(
                      sv_bf[0:64, half * 512:half * 512 + 512],
                      psum[4 + half][0:64, 0:512]),
                  reads=(psb[4 + half],), writes=(S["sv"],))
        for half in range(2):
            kb.op("tensor",
                  lambda E, half=half: E.matmul(
                      psum[4 + half][0:64, 0:512], attn64,
                      sv_bf[0:64, half * 512:half * 512 + 512],
                      start=True, stop=True),
                  reads=(S["attn64"], S["sv"]), writes=(psb[4 + half],))
            kb.op("vector",
                  lambda E, half=half: E.tensor_tensor(
                      sl[0:64, half * 512:half * 512 + 512],
                      Snew[0:64, half * 512:half * 512 + 512],
                      psum[4 + half][0:64, 0:512], op=OP.add),
                  reads=(S["Snew"], psb[4 + half]), writes=(S["sl"],))
        kb.op("vector", lambda E: E.reduce_sum(scol[:, 5:6], sl[0:64, :], axis=AX.X),
              reads=(S["sl"],), writes=(S["scol"],))
        kb.op("scalar",
              lambda E: E.activation(sl2[0:64, :], sl[0:64, :], AF.Square,
                                     accum_out=scol[:, 6:7]),
              reads=(S["sl"],), writes=(S["sl2"], S["scol"]))
        kb.op("vector", lambda E: E.tensor_scalar(scol[:, 7:8], scol[:, 5:6],
                                                  1.0 / D, None, op0=OP.mult),
              reads=(S["scol"],), writes=(S["scol"],))
        kb.op("vector",
              lambda E: E.tensor_tensor(scol[:, 8:9], scol[:, 7:8], scol[:, 7:8],
                                        op=OP.mult),
              reads=(S["scol"],), writes=(S["scol"],))
        kb.op("vector",
              lambda E: E.scalar_tensor_tensor(scol[:, 9:10], scol[:, 6:7],
                                               1.0 / D, scol[:, 8:9],
                                               op0=OP.mult, op1=OP.subtract),
              reads=(S["scol"],), writes=(S["scol"],))
        eps_c2 = constcol(1e-5)
        kb.op("scalar",
              lambda E, eps_c2=eps_c2: E.activation(scol[:, 10:11], scol[:, 9:10],
                                                    AF.Ln, bias=eps_c2[0:64, :]),
              reads=(S["scol"], _cc_buf), writes=(S["scol"],))
        kb.op("scalar",
              lambda E: E.activation(scol[:, 11:12], scol[:, 10:11], AF.Exp,
                                     scale=-0.5),
              reads=(S["scol"],), writes=(S["scol"],))
        kb.op("vector",
              lambda E: E.tensor_scalar(sl[0:64, :], sl[0:64, :], scol[:, 7:8],
                                        scol[:, 11:12], op0=OP.subtract,
                                        op1=OP.mult),
              reads=(S["sl"], S["scol"]), writes=(S["sl"],))
        kb.op("vector",
              lambda E: E.tensor_tensor(sl[0:64, :], sl[0:64, :],
                                        p["lngb"][0:64, 0:D], op=OP.mult),
              reads=(S["sl"], p["b_lngb"]), writes=(S["sl"],))
        kb.op("vector",
              lambda E: E.tensor_tensor(sl[0:64, :], sl[0:64, :],
                                        p["lngb"][0:64, D:2 * D], op=OP.add),
              reads=(S["sl"], p["b_lngb"]), writes=(S["sl"],))
        slot_T(sl[0:64, :], S["sl"])
        for half in range(2):
            _stream_rhs_mm(dram["L1_ffn_wvp"], 0, half * 512, 512, sT, S["sT"],
                           64, 4 + half, stg2, S["stg"], stg2bf, S["stgbf"])
            bias_mm(4 + half, 64, p["bvp"], half * 512, 512)
            kb.op("vector",
                  lambda E, half=half: E.tensor_copy(
                      sv_bf[0:64, half * 512:half * 512 + 512],
                      psum[4 + half][0:64, 0:512]),
                  reads=(psb[4 + half],), writes=(S["sv"],))

        kb.set_fence()
        A.push()
        A.rebase(TAIL_OFF)
        wo_c = A.alloc([128, 8 * D], BF16, "l1_woc")
        b_woc = kb.buf()
        for i in range(8):
            kb.dma("sync", stg2, dram["L1_ffn_wo"][i * 128:(i + 1) * 128, :],
                   writes=(S["stg"],))
            kb.op("vector", lambda E, i=i: E.tensor_copy(
                wo_c[:, i * D:(i + 1) * D], stg2),
                reads=(S["stg"],), writes=(b_woc,))
        o1_bf = A.alloc([128, D], BF16, "l1_o1")
        o1T = A.alloc([128, D], BF16, "l1_o1T")
        b_o1 = kb.buf(); b_o1T = kb.buf()
        rd_b = hbm_reader()
        for t in range(NT):
            for half in range(2):
                kb.op("tensor",
                      lambda E, t=t, half=half: E.matmul(
                          psum[half][:, 0:512],
                          p["alphaT"][0:64, t * 128:(t + 1) * 128],
                          sv_bf[0:64, half * 512:half * 512 + 512],
                          start=True, stop=True),
                      reads=(balphaT, S["sv"]), writes=(psb[half],))
                kb.op("vector",
                      lambda E, half=half: E.tensor_copy(
                          o1_bf[:, half * 512:half * 512 + 512],
                          psum[half][:, 0:512]),
                      reads=(psb[half],), writes=(b_o1,))
            for i in range(8):
                kb.dmat(kb.tq(), o1T[:, i * 128:(i + 1) * 128],
                        o1_bf[:, i * 128:(i + 1) * 128],
                        reads=(b_o1,), writes=(b_o1T,))
            xa, xab = rd_b(t)
            for half in range(2):
                pidx = 2 + half
                for i in range(8):
                    kb.op("tensor",
                          lambda E, i=i, half=half, pidx=pidx: E.matmul(
                              psum[pidx][:, 0:512], o1T[:, i * 128:(i + 1) * 128],
                              wo_c[:, i * D + half * 512:i * D + half * 512 + 512],
                              start=(i == 0), stop=False),
                          reads=(b_o1T, b_woc), writes=(psb[pidx],))
                bias_mm(pidx, 128, p["bo"], half * 512, 512)
                kb.op("vector",
                      lambda E, t=t, half=half, pidx=pidx, xa=xa:
                      E.scalar_tensor_tensor(
                          xa[:, half * 512:half * 512 + 512],
                          psum[pidx][:, 0:512], 0.5,
                          xa[:, half * 512:half * 512 + 512],
                          op0=OP.mult, op1=OP.add),
                      reads=(psb[pidx],) + xab, writes=xab)
            kb.dma("sync", x_hbm[t * 128:(t + 1) * 128, :], xa,
                   reads=xab, writes=(xh[t],))
        A.pop()
        A.pop()
        A.pop()



    def l1_attn_phase(ph):
        kb.set_fence()
        A.push()
        A.rebase(l1f["end"], limit=128 * 1024)
        h0 = ph * 8
        qkvT = A.alloc([128, 12 * D], BF16, "la_qkvT")
        b_qkvT = kb.buf()
        projT = A.alloc([128, 4 * D], BF16, "la_projT")
        b_projT = kb.buf()
        expB = A.alloc([128, 3 * 8 * 256], BF16, "la_expB")
        b_expB = kb.buf()
        stg = A.alloc([128, D], F32, "la_stg")
        b_stg = kb.buf()
        stgb = A.alloc([128, D], BF16, "la_stgb")
        b_stgb = kb.buf()
        qbT = A.alloc([128, 8], F32, "la_qbT")
        b_qbT = kb.buf()
        set_bias_stage(D)
        vbias = bias_rows("la_vb%d" % ph, dram["L1_attn_qkv_b"][None,
                          2 * D + h0 * 64:2 * D + h0 * 64 + 512], 512)
        pbias = None
        if ph == 0:
            pbias = bias_rows("la_pb", dram["L1_attn_proj_b"][None, :], D)

        ocs = [h0 * 64 + j * 128 for j in range(4)] + \
              [D + h0 * 64 + j * 128 for j in range(4)] + \
              [2 * D + h0 * 64 + j * 128 for j in range(4)]
        for jj, orow in enumerate(ocs):
            kb.dma("sync", stg, dram["L1_attn_qkv_w"][orow:orow + 128, :],
                   writes=(b_stg,))
            kb.op("vector", lambda E: E.tensor_copy(stgb, stg),
                  reads=(b_stg,), writes=(b_stgb,))
            for i in range(8):
                kb.dmat(kb.tq(),
                        qkvT[:, i * 1536 + jj * 128:i * 1536 + (jj + 1) * 128],
                        stgb[:, i * 128:(i + 1) * 128],
                        reads=(b_stgb,), writes=(b_qkvT,))
        for oc in range(8):
            kb.dma("sync", stg[:, 0:512],
                   dram["L1_attn_proj_w"][oc * 128:(oc + 1) * 128,
                                          ph * 512:ph * 512 + 512],
                   writes=(b_stg,))
            kb.op("vector", lambda E: E.tensor_copy(stgb[:, 0:512], stg[:, 0:512]),
                  reads=(b_stg,), writes=(b_stgb,))
            for fc in range(4):
                kb.dmat(kb.tq(),
                        projT[:, fc * D + oc * 128:fc * D + (oc + 1) * 128],
                        stgb[:, fc * 128:(fc + 1) * 128],
                        reads=(b_stgb,), writes=(b_projT,))
        for tt in range(3):
            for hh in range(8):
                for cc in range(2):
                    kb.dma("sync", stgb[:, 0:128],
                           bmat_hbm[tt, h0 + hh, cc * 128:(cc + 1) * 128, :],
                           reads=(l1f["b_bm"],), writes=(b_stgb,))
                    kb.op("scalar",
                          lambda E, tt=tt, hh=hh, cc=cc: E.activation(
                              expB[:, (tt * 8 + hh) * 256 + cc * 128:
                                   (tt * 8 + hh) * 256 + (cc + 1) * 128],
                              stgb[:, 0:128], AF.Exp),
                          reads=(b_stgb,), writes=(b_expB,))
        kb.dma("sync", row_hbm[4:5, 0:512],
               dram["L1_attn_qkv_b"][None, h0 * 64:h0 * 64 + 512],
               writes=(b_qbT,))
        kb.dma("sync", row_hbm[5:6, 0:512],
               dram["L1_attn_qkv_b"][None, D + h0 * 64:D + h0 * 64 + 512],
               writes=(b_qbT,))
        for j in range(4):
            kb.dma("gpsimd", qbT[:, j:j + 1],
                   row_hbm[4:5, j * 128:(j + 1) * 128].rearrange("a b -> b a"),
                   reads=(b_qbT,), writes=(b_qbT,), allow_slow_non_contiguous=True)
            kb.dma("gpsimd", qbT[:, 4 + j:5 + j],
                   row_hbm[5:6, j * 128:(j + 1) * 128].rearrange("a b -> b a"),
                   reads=(b_qbT,), writes=(b_qbT,), allow_slow_non_contiguous=True)
        kb.op("vector", lambda E: E.tensor_scalar(qbT[:, 0:4], qbT[:, 0:4],
                                                  0.125, None, op0=OP.mult),
              reads=(b_qbT,), writes=(b_qbT,))

        A.rebase(TAIL_OFF)
        kTr = A.alloc([128, 4 * 768], BF16, "la_kTr")
        b_kTr = kb.buf()
        qTc = A.alloc([128, 2 * 512], BF16, "la_qTc")
        b_qTc = kb.buf()
        vsh = A.alloc([128, 4 * 512], BF16, "la_vsh")
        b_vsh = kb.buf()
        v_bf = A.alloc([128, 512], BF16, "la_vbf")
        b_vbf = kb.buf()
        lnx_bf = A.alloc([128, D], BF16, "la_lnx")
        lnxT = A.alloc([128, D], BF16, "la_lnxT")
        b_lnx = kb.buf(); b_lnxT = kb.buf()
        et = [A.alloc([128, 256], BF16, f"la_et{j}") for j in range(2)]
        b_et = [kb.buf(), kb.buf()]
        pfm = [A.alloc([128, 256], BF16, f"la_pfm{j}") for j in range(2)]
        b_pfm = [kb.buf(), kb.buf()]
        att = A.alloc([128, 512], BF16, "la_att")
        aoT = A.alloc([128, 512], BF16, "la_aoT")
        b_att = kb.buf(); b_aoT = kb.buf()
        rcol = A.alloc([128, 2], F32, "la_rcol")
        b_rcol = kb.buf()
        xtl = [A.alloc([128, D], F32, f"la_x{j}") for j in range(3)]
        b_xtl = [kb.buf() for j in range(3)]
        kb.op("vector", lambda E: E.memset(kTr[:], 0.0), writes=(b_kTr,))
        kb.op("vector", lambda E: E.memset(vsh[:], 0.0), writes=(b_vsh,))

        if ph == 0:
            ln_stats_pass(hbm_reader())

        def qkv_tile(u):
            j = u % 3
            slot = u % 4
            qslot = u % 2
            kb.dma("sync", xtl[j], x_hbm[u * 128:(u + 1) * 128, :],
                   reads=(xh[u],), writes=(b_xtl[j],))
            if ph == 0:
                kb.op("vector",
                      lambda E, u=u, j=j: E.tensor_scalar(
                          lnx_bf, xtl[j], st_mu[:, u:u + 1], st_rstd[:, u:u + 1],
                          op0=OP.subtract, op1=OP.mult),
                      reads=(b_xtl[j], sb_mu, sb_rstd), writes=(b_lnx,))
                kb.dma("sync", lnx_hbm[u * 128:(u + 1) * 128, :], lnx_bf,
                       reads=(b_lnx,), writes=(b_lnx,))
            else:
                kb.dma("sync", lnx_bf, lnx_hbm[u * 128:(u + 1) * 128, :],
                       writes=(b_lnx,))
            for i in range(8):
                kb.dmat(kb.tq(), lnxT[:, i * 128:(i + 1) * 128],
                        lnx_bf[:, i * 128:(i + 1) * 128],
                        reads=(b_lnx,), writes=(b_lnxT,))
            for bofs, bank in ((0, 0), (4, 1)):
                for pr in range(4):
                    for i in range(8):
                        kb.op("tensor",
                              lambda E, bofs=bofs, pr=pr, i=i, bank=bank: E.matmul(
                                  psum[bank][:, pr * 128:(pr + 1) * 128],
                                  qkvT[:, i * 1536 + (bofs + pr) * 128:
                                       i * 1536 + (bofs + pr + 1) * 128],
                                  lnxT[:, i * 128:(i + 1) * 128],
                                  start=(pr == 0 and i == 0),
                                  stop=(i == 7),
                                  skip_group_check=True),
                              reads=(b_qkvT, b_lnxT), writes=(psb[bank],))
            for i in range(8):
                kb.op("tensor",
                      lambda E, i=i: E.matmul(
                          psum[2][:, 0:512], lnxT[:, i * 128:(i + 1) * 128],
                          qkvT[:, i * 1536 + 1024:i * 1536 + 1536],
                          start=(i == 0), stop=False),
                      reads=(b_qkvT, b_lnxT), writes=(psb[2],))
            kb.op("tensor",
                  lambda E: E.matmul(psum[2][:, 0:512], ones2[:], vbias[0][:],
                                     start=False, stop=True),
                  reads=(b_ones2, vbias[1]), writes=(psb[2],))
            for pr in range(4):
                kb.op("scalar",
                      lambda E, pr=pr, qslot=qslot: E.activation(
                          qTc[:, qslot * 512 + pr * 128:qslot * 512 + (pr + 1) * 128],
                          psum[0][:, pr * 128:(pr + 1) * 128], AF.Identity,
                          bias=qbT[:, pr:pr + 1], scale=0.125),
                      reads=(psb[0], b_qbT), writes=(b_qTc,))
                kb.op("scalar",
                      lambda E, pr=pr, slot=slot: E.activation(
                          kTr[:, pr * 768 + slot * 128:pr * 768 + (slot + 1) * 128],
                          psum[1][:, pr * 128:(pr + 1) * 128], AF.Identity,
                          bias=qbT[:, 4 + pr:5 + pr]),
                      reads=(psb[1], b_qbT), writes=(b_kTr,))
                if slot < 2:
                    kb.dma("sync",
                           kTr[:, pr * 768 + (slot + 4) * 128:
                               pr * 768 + (slot + 5) * 128],
                           kTr[:, pr * 768 + slot * 128:pr * 768 + (slot + 1) * 128],
                           reads=(b_kTr,), writes=(b_kTr,))
            kb.op("vector", lambda E: E.tensor_copy(v_bf, psum[2][:, 0:512]),
                  reads=(psb[2],), writes=(b_vbf,))
            kb.dma("sync", vsh[64:128, slot * 512:(slot + 1) * 512],
                   v_bf[0:64, :], reads=(b_vbf,), writes=(b_vsh,))
            kb.dma("sync", vsh[0:64, ((u + 1) % 4) * 512:((u + 1) % 4 + 1) * 512],
                   v_bf[64:128, :], reads=(b_vbf,), writes=(b_vsh,))

        def attn_tile(t):
            tt = 0 if t == 0 else (2 if t == NT - 1 else 1)
            qslot = t % 2
            win = ((t - 1) % 4) * 128 + 64
            for hh in range(8):
                pr = hh // 2
                r0 = (hh % 2) * 64
                sb5 = 3 if hh % 2 == 0 else 7
                pp = hh % 2
                for cc in range(2):
                    kb.op("tensor",
                          lambda E, pr=pr, r0=r0, cc=cc, win=win, qslot=qslot,
                          sb5=sb5: E.matmul(
                              psum[sb5][:, cc * 128:(cc + 1) * 128],
                              kTr[r0:r0 + 64,
                                  pr * 768 + win + cc * 128:
                                  pr * 768 + win + (cc + 1) * 128],
                              qTc[r0:r0 + 64,
                                  qslot * 512 + pr * 128:qslot * 512 + (pr + 1) * 128],
                              start=(cc == 0), stop=True,
                              skip_group_check=True),
                          reads=(b_kTr, b_qTc), writes=(psb[sb5],))
                kb.op("scalar",
                      lambda E, sb5=sb5, pp=pp: E.activation(
                          et[pp][:], psum[sb5][:, 0:256], AF.Exp),
                      reads=(psb[sb5],), writes=(b_et[pp],))
                kb.op("vector",
                      lambda E, tt=tt, hh=hh, pp=pp: E.tensor_tensor(
                          pfm[pp][:], et[pp][:],
                          expB[:, (tt * 8 + hh) * 256:(tt * 8 + hh) * 256 + 256],
                          op=OP.mult),
                      reads=(b_et[pp], b_expB), writes=(b_pfm[pp],))
                for cc in range(2):
                    vs = ((t + cc) % 4) * 512 + hh * 64
                    kb.op("tensor",
                          lambda E, cc=cc, vs=vs, pp=pp: E.matmul(
                              psum[4][:, 0:64],
                              pfm[pp][:, cc * 128:(cc + 1) * 128],
                              vsh[:, vs:vs + 64],
                              start=(cc == 0), stop=True,
                              skip_group_check=True),
                          reads=(b_pfm[pp], b_vsh), writes=(psb[4],))
                    kb.op("tensor",
                          lambda E, cc=cc, pp=pp: E.matmul(
                              psum[4][:, 64:65],
                              pfm[pp][:, cc * 128:(cc + 1) * 128],
                              consts["ones_col"],
                              start=False, stop=True,
                              skip_group_check=True),
                          reads=(b_pfm[pp], cb["ones_col"]), writes=(psb[4],))
                kb.op("vector", lambda E: E.reciprocal(rcol[:, 0:1],
                                                       psum[4][:, 64:65]),
                      reads=(psb[4],), writes=(b_rcol,))
                kb.op("vector",
                      lambda E, hh=hh: E.tensor_scalar(
                          att[:, hh * 64:(hh + 1) * 64], psum[4][:, 0:64],
                          rcol[:, 0:1], None, op0=OP.mult),
                      reads=(psb[4], b_rcol), writes=(b_att,))
                if DEBUG and ph == 0 and t == 4 and hh == 0:
                    tap("la_pfm", pfm[0][:, 0:256], (b_pfm[0],))
                    tap("la_et", et[0][:, 0:256], (b_et[0],))
            if DEBUG and ph == 0 and t == 4:
                tap("la_att", att[:, 0:512], (b_att,))
                tap("la_qTc", qTc[:, 0:1024], (b_qTc,))
                tap("la_kTr", kTr[:, 0:768], (b_kTr,))
                tap("la_expB0", expB[:, 8 * 256:9 * 256], (b_expB,))
                tap("la_vsh", vsh[:, 0:2048], (b_vsh,))
            for fc in range(4):
                kb.dmat(kb.tq(), aoT[:, fc * 128:(fc + 1) * 128],
                        att[:, fc * 128:(fc + 1) * 128],
                        reads=(b_att,), writes=(b_aoT,))
            j = t % 3
            for half in range(2):
                pidx = 5 + half
                for fc in range(4):
                    kb.op("tensor",
                          lambda E, fc=fc, half=half, pidx=pidx: E.matmul(
                              psum[pidx][:, 0:512], aoT[:, fc * 128:(fc + 1) * 128],
                              projT[:, fc * D + half * 512:fc * D + half * 512 + 512],
                              start=(fc == 0), stop=(ph == 1 and fc == 3)),
                          reads=(b_aoT, b_projT), writes=(psb[pidx],))
                if ph == 0:
                    kb.op("tensor",
                          lambda E, half=half, pidx=pidx: E.matmul(
                              psum[pidx][:, 0:512], ones2[:],
                              pbias[0][:, half * 512:half * 512 + 512],
                              start=False, stop=True),
                          reads=(b_ones2, pbias[1]), writes=(psb[pidx],))
                kb.op("vector",
                      lambda E, half=half, pidx=pidx, j=j: E.tensor_tensor(
                          xtl[j][:, half * 512:half * 512 + 512],
                          xtl[j][:, half * 512:half * 512 + 512],
                          psum[pidx][:, 0:512], op=OP.add),
                      reads=(psb[pidx],), writes=(b_xtl[j],))
            kb.dma("sync", x_hbm[t * 128:(t + 1) * 128, :], xtl[j],
                   reads=(b_xtl[j],), writes=(xh[t],))

        qkv_tile(0)
        for t in range(NT):
            if t + 1 < NT:
                qkv_tile(t + 1)
            attn_tile(t)
        A.pop()

    x_in_hbm = False
    if STAGES >= 2:
        xh = l0_attn()
        x_in_hbm = True

    if STAGES >= 3:
        kb.set_fence()
        for t in range(NT):
            xb[t] = kb.buf(f"x[{t}]")
            kb.dma("sync", xt(t), x_hbm[t * 128:(t + 1) * 128, :],
                   reads=(xh[t],), writes=(xb[t],))
        x_in_hbm = False
        l0_ffn()

    if STAGES >= 4:
        kb.set_fence()
        for t in range(NT):
            xh[t] = kb.buf(f"xh4[{t}]")
            kb.dma("sync", x_hbm[t * 128:(t + 1) * 128, :], xt(t),
                   reads=(xb[t],), writes=(xh[t],))
        x_in_hbm = True
        l1_ffn_prep()
        l1_ffn()

    if STAGES >= 5:
        l1_attn_phase(0)
        l1_attn_phase(1)

    if STAGES >= 6:
        l1_ffn()

    out_toks = []
    for t in range(NT):
        if x_in_hbm:
            out_toks.append(
                kb.dma("sync", out_d[t * 128:(t + 1) * 128, :],
                       x_hbm[t * 128:(t + 1) * 128, :], reads=(xh[t],)))
        else:
            out_toks.append(
                kb.dma("sync", out_d[t * 128:(t + 1) * 128, :], xt(t),
                       reads=(xb[t],)))
    kb.op("sync", lambda E: E.nop(), extra=out_toks)


_CACHE = {}


def kernel(**inputs):
    if "nc" not in _CACHE:
        _CACHE["nc"] = build_program()
    nc = _CACHE["nc"]
    consts = host_constants()
    x = np.ascontiguousarray(inputs["L0_attn_x"])
    in_maps = []
    for c in range(8):
        m = {}
        for name, shape, dt in INPUT_SPECS:
            if name == "L0_attn_x":
                m[name] = np.ascontiguousarray(x[c])
            else:
                m[name] = np.ascontiguousarray(inputs[name])
        for cname, arr in consts.items():
            m["c_" + cname] = arr
        in_maps.append(m)
    res = run_bass_kernel_spmd(nc, in_maps, core_ids=list(range(8)))
    out = np.stack([r["out"] for r in res.results], axis=0)
    if DEBUG:
        kernel.taps = [{n: r[n] for n, _, _ in DEBUG_TAPS} for r in res.results]
    return out


# version 14
# speedup vs baseline: 1.0742x; 1.0389x over previous
import numpy as np
import ml_dtypes
from contextlib import ExitStack

import concourse.bass as bass
import concourse.mybir as mybir
from concourse.bass_utils import run_bass_kernel_spmd

F32 = mybir.dt.float32
BF16 = mybir.dt.bfloat16
I32 = mybir.dt.int32
U8 = mybir.dt.uint8
AF = mybir.ActivationFunctionType
OP = mybir.AluOpType
AX = mybir.AxisListType

NP_BF16 = ml_dtypes.bfloat16

L = 4096
D = 1024
NT = 32
NH = 16
HD = 64
STAGES = 6
FA_CHUNKS = NT

DEBUG_TAPS = []
DEBUG = False

ENGINES = ("sync", "vector", "scalar", "gpsimd", "tensor")
HW_Q = ("sync", "scalar")


class Tok:
    __slots__ = ("sem", "idx", "val")

    def __init__(self, sem, idx):
        self.sem = sem
        self.idx = idx
        self.val = None


class Buf:
    __slots__ = ("name", "wr", "rd")

    def __init__(self, name=""):
        self.name = name
        self.wr = None
        self.rd = []


class KB:

    DMA_POOL = 12

    def __init__(self, nc):
        self.nc = nc
        self.prog = {e: [] for e in ENGINES}
        self.sem_idx = {}
        self.consumed = set()
        self.hw = {e: {} for e in ENGINES}
        self.esem = {e: "c_" + e for e in ENGINES}
        self.qpool = {e: ["q_%s_%d" % (e, i) for i in range(self.DMA_POOL)]
                      for e in ("sync", "scalar", "gpsimd", "vector")}
        self.qrr = {e: 0 for e in self.qpool}
        self.sems = {}
        self._tq_rr = 0
        self.fence = []
        self.last_op = {e: None for e in ENGINES}

    def sem_names(self):
        out = [self.esem[e] for e in ENGINES]
        for e in self.qpool:
            out.extend(self.qpool[e])
        return out

    def _tok(self, sem):
        i = self.sem_idx.get(sem, 0) + 1
        self.sem_idx[sem] = i
        return Tok(sem, i)

    def _prune(self, e, waits):
        out = []
        seen = {}
        for t in waits:
            if t is None:
                continue
            if e == "tensor" and t.sem == self.esem["tensor"]:
                continue
            cur = seen.get(t.sem)
            if cur is None or t.idx > cur.idx:
                seen[t.sem] = t
        for sem, t in seen.items():
            if self.hw[e].get(sem, 0) >= t.idx:
                continue
            self.hw[e][sem] = t.idx
            out.append(t)
            self.consumed.add((t.sem, t.idx))
        return out

    def _collect(self, reads, writes, extra):
        waits = []
        for b in reads:
            if b.wr is not None:
                waits.append(b.wr)
        for b in writes:
            if b.wr is not None:
                waits.append(b.wr)
            waits.extend(b.rd)
        waits.extend(extra)
        return waits

    def _emit(self, e, fn, waits, is_dma, sem):
        t = self._tok(sem)
        self.prog[e].append((waits, fn, t, is_dma))
        return t

    SERIAL_CHAIN = False

    def op(self, e, fn, reads=(), writes=(), extra=()):
        waits = list(self._collect(reads, writes, extra))
        if self.SERIAL_CHAIN and e != "tensor" and self.last_op[e] is not None:
            waits.append(self.last_op[e])
        waits = self._prune(e, waits)
        t = self._emit(e, fn, waits, False, self.esem[e])
        self.last_op[e] = t
        for b in reads:
            b.rd.append(t)
        for b in writes:
            b.wr = t
            b.rd = []
        return t

    def _dma_emit(self, e, fn, reads, writes, extra):
        pool = self.qpool[e]
        sem = pool[self.qrr[e] % len(pool)]
        self.qrr[e] += 1
        waits = list(self._collect(reads, writes, extra))
        prev = self.sem_idx.get(sem, 0)
        if prev:
            waits.append(Tok(sem, prev))
        waits = self._prune(e, waits)
        t = self._emit(e, fn, waits, True, sem)
        for b in reads:
            b.rd.append(t)
        for b in writes:
            b.wr = t
            b.rd = []
        return t

    def dma(self, e, out, in_, reads=(), writes=(), extra=(), **kw):
        fn = lambda E, out=out, in_=in_, kw=kw: E.dma_start(out=out, in_=in_, **kw)
        return self._dma_emit(e, fn, reads, writes, extra)

    def dmat(self, e, out, in_, reads=(), writes=(), extra=()):
        assert e in HW_Q
        fn = lambda E, out=out, in_=in_: E.dma_start_transpose(out=out, in_=in_)
        return self._dma_emit(e, fn, reads, writes, extra)

    def tq(self):
        e = HW_Q[self._tq_rr % len(HW_Q)]
        self._tq_rr += 1
        return e

    def buf(self, name=""):
        b = Buf(name)
        b.rd = list(self.fence)
        return b

    def set_fence(self):
        self.fence = self.barrier_toks()

    def barrier_toks(self):
        out = []
        for sem, i in self.sem_idx.items():
            if i:
                out.append(Tok(sem, i))
        return out

    def finalize_into(self, block, sem_handles):
        self.sems = sem_handles
        val = {}
        for e in ENGINES:
            for waits, fn, t, is_dma in self.prog[e]:
                if is_dma or (t.sem, t.idx) in self.consumed:
                    inc = 16 if is_dma else 1
                    val[t.sem] = val.get(t.sem, 0) + inc
                    t.val = val[t.sem]
                else:
                    t.val = None

        sem_points = {}
        for e in ENGINES:
            for waits, fn, t, is_dma in self.prog[e]:
                if t.val is not None:
                    sem_points.setdefault(t.sem, []).append((t.idx, t.val))
        for sem in sem_points:
            sem_points[sem].sort()

        import bisect

        def resolve(t):
            pts = sem_points.get(t.sem)
            assert pts is not None, f"wait on sem {t.sem} with no incing ops"
            i = bisect.bisect_right(pts, (t.idx, 1 << 60)) - 1
            assert i >= 0 and pts[i][0] == t.idx, \
                f"consumed token lost {t.sem} {t.idx}"
            return pts[i][1]

        def mk_runner(e):
            ops = self.prog[e]

            def run(E):
                for waits, fn, t, is_dma in ops:
                    for w in waits:
                        E.wait_ge(self.sems[w.sem], resolve(w))
                    ins = fn(E)
                    if t.val is not None:
                        inc = 16 if is_dma else 1
                        ins.then_inc(self.sems[t.sem], inc)
            return run

        runners = {e: mk_runner(e) for e in ENGINES}
        block.sync(runners["sync"])
        block.vector(runners["vector"])
        block.scalar(runners["scalar"])
        block.gpsimd(runners["gpsimd"])
        block.tensor(runners["tensor"])


class Arena:

    def __init__(self, nc, base, nbytes):
        self.nc = nc
        self.base = base
        self.nbytes = nbytes
        self.off = 0
        self.marks = []
        self.n = 0

    def alloc(self, shape, dtype, name=""):
        assert len(shape) == 2
        if not hasattr(self, "log"):
            self.log = []
        per = int(np.prod(shape[1:])) * mybir.dt.size(dtype)
        off = (self.off + 31) & ~31
        assert off + per <= self._cap, f"arena overflow at {name}: {off}+{per} cap {self._cap}"
        self.off = off + per
        self.log.append((name, off, off + per))
        self.n += 1
        h = self.nc.alloc_sbuf_tensor_at(
            f"a{self.n}_{name or 'x'}", [shape[0], int(shape[1])], dtype,
            offset=self.base + off)
        return h.ap()

    def push(self):
        self.marks.append(self.off)

    def pop(self):
        self.off = self.marks.pop()
        self.limit = None

    def rebase(self, off, limit=None):
        self.off = off
        self.limit = limit

    @property
    def _cap(self):
        return self.limit if getattr(self, "limit", None) else self.nbytes


def _bf(x):
    return np.asarray(x, dtype=NP_BF16)


def host_constants():
    c = {}
    j = np.arange(128)
    c["triu"] = _bf(np.triu(np.ones((128, 128), np.float32)))
    c["ident"] = _bf(np.eye(128, dtype=np.float32))
    c["ones_col"] = _bf(np.ones((128, 1), np.float32))
    c["ones128"] = _bf(np.ones((128, 128), np.float32))
    c["o64_col"] = _bf(np.ones((64, 1), np.float32))
    c["identf32"] = np.eye(128, dtype=np.float32)
    c["ones_row"] = _bf(np.ones((1, 128), np.float32))
    n = (np.arange(L, dtype=np.float64) + 1.0).reshape(NT, 128).T
    c["normc"] = n.astype(np.float32)
    c["rnormc"] = (1.0 / n.astype(np.float32)).astype(np.float32)
    oh = np.zeros((8, 8 * 128), np.float32)
    for i in range(8):
        oh[i, i * 128:(i + 1) * 128] = 1.0
    c["onehot8"] = _bf(oh)
    return c


CONST_SPECS = {
    "triu": ([128, 128], BF16),
    "ident": ([128, 128], BF16),
    "ones_col": ([128, 1], BF16),
    "ones128": ([128, 128], BF16),
    "o64_col": ([64, 1], BF16),
    "identf32": ([128, 128], F32),
    "ones_row": ([1, 128], BF16),
    "normc": ([128, NT], F32),
    "rnormc": ([128, NT], F32),
    "onehot8": ([8, 8 * 128], BF16),
}

INPUT_SPECS = [
    ("L0_attn_x", [L, D], F32),
    ("L0_attn_Wq", [D, D], F32), ("L0_attn_bq", [D], F32),
    ("L0_attn_Wk", [D, D], F32), ("L0_attn_bk", [D], F32),
    ("L0_attn_Wv", [D, D], F32), ("L0_attn_bv", [D], F32),
    ("L0_attn_Wo", [D, D], F32), ("L0_attn_bo", [D], F32),
    ("L0_ffn_packed_weight", [262144], I32),
    ("L0_ffn_weight_scale", [1], F32),
    ("L0_ffn_bias", [D], F32),
    ("L1_attn_qkv_w", [3 * D, D], F32), ("L1_attn_qkv_b", [3 * D], F32),
    ("L1_attn_rpb", [16, 25], F32),
    ("L1_attn_proj_w", [D, D], F32), ("L1_attn_proj_b", [D], F32),
    ("L1_ffn_slot_init", [64, D], F32),
    ("L1_ffn_w1", [2 * D, 512], F32), ("L1_ffn_b1", [512], F32),
    ("L1_ffn_w2", [512, 64], F32), ("L1_ffn_b2", [64], F32),
    ("L1_ffn_tau", [1], F32),
    ("L1_ffn_Wih", [3 * D, D], F32), ("L1_ffn_Whh", [3 * D, D], F32),
    ("L1_ffn_bih", [3 * D], F32), ("L1_ffn_bhh", [3 * D], F32),
    ("L1_ffn_wq", [D, 256], F32), ("L1_ffn_bq", [256], F32),
    ("L1_ffn_wk", [D, 256], F32), ("L1_ffn_bk", [256], F32),
    ("L1_ffn_wv", [D, D], F32), ("L1_ffn_bv", [D], F32),
    ("L1_ffn_ln_g", [D], F32), ("L1_ffn_ln_b", [D], F32),
    ("L1_ffn_wvp", [D, D], F32), ("L1_ffn_bvp", [D], F32),
    ("L1_ffn_wo", [D, D], F32), ("L1_ffn_bo", [D], F32),
]



def l1attn_diag_intervals(ttype):
    ts = {0: 0, 1: 128, 2: L - 128}[ttype]
    out = []
    for m in range(52, 77):
        ls = []
        for l in range(128):
            c = l + m
            if not (0 <= c < 256):
                continue
            i = ts + l
            n = ts + c - 64
            ni = min(max(i - 6, 0), L - 13)
            if ni <= n <= ni + 12:
                assert 0 <= m - 52 <= 24
                ls.append(l)
        if ls:
            assert ls == list(range(ls[0], ls[-1] + 1)), (ttype, m, ls)
            out.append((m, ls[0], ls[-1] + 1))
    return out


def build_program():
    nc = bass.Bass("TRN2", target_bir_lowering=False)
    dram = {}
    for name, shape, dt in INPUT_SPECS:
        dram[name] = nc.dram_tensor(name, shape, dt, kind="ExternalInput").ap()
    for name, (shape, dt) in CONST_SPECS.items():
        dram[name] = nc.dram_tensor("c_" + name, shape, dt, kind="ExternalInput").ap()
    out = nc.dram_tensor("out", [L, D], F32, kind="ExternalOutput").ap()
    x_hbm = nc.dram_tensor("x_hbm", [L, D], F32, kind="Internal").ap()
    row_hbm = nc.dram_tensor("row_hbm", [64, 4096], F32, kind="Internal").ap()
    lnx_hbm = nc.dram_tensor("lnx_hbm", [L, D], BF16, kind="Internal").ap()
    bmat_hbm = nc.dram_tensor("bmat_hbm", [3, NH, 256, 128], BF16, kind="Internal").ap()

    ARENA_BYTES = 206 * 1024
    with ExitStack() as ctx:
        arena_t = ctx.enter_context(nc.sbuf_tensor("arena", [128, ARENA_BYTES], U8))
        A = Arena(nc, nc.lookup_mloc(arena_t).addr, ARENA_BYTES)
        kb = KB(nc)

        sem_handles = {}
        for name in kb.sem_names():
            sem_handles[name] = ctx.enter_context(nc.semaphore(name))

        psum = [ctx.enter_context(nc.psum_tensor(f"ps{i}", [128, 512], F32))
                for i in range(8)]
        psb = [Buf(f"ps{i}") for i in range(8)]

        build_body(nc, kb, A, dram, out, x_hbm, row_hbm, bmat_hbm, lnx_hbm,
                   psum, psb)

        block = ctx.enter_context(nc.Block())
        kb.finalize_into(block, sem_handles)
    return nc



def build_body(nc, kb, A, dram, out_d, x_hbm, row_hbm, bmat_hbm, lnx_hbm,
               psum, psb):
    l1f = {}

    _tapn = {}

    def tap(name, ap, bufs):
        if not DEBUG:
            return
        if name in _tapn:
            _tapn[name] += 1
            name = f"{name}_{_tapn[name]}"
        else:
            _tapn[name] = 0
        shape = list(ap.shape)
        d = nc.dram_tensor("tap_" + name, shape, ap.dtype, kind="ExternalOutput").ap()
        DEBUG_TAPS.append(("tap_" + name, shape, ap.dtype))
        kb.dma("sync", d, ap, reads=tuple(bufs))

    x_res = A.alloc([128, NT * D], F32, "x_res")
    xb = [Buf(f"x[{t}]") for t in range(NT)]
    xt = lambda t: x_res[:, t * D:(t + 1) * D]

    consts = {}
    cb = {}
    for name, (shape, dt) in CONST_SPECS.items():
        consts[name] = A.alloc(shape, dt, name)
        cb[name] = Buf(name)
        if name == "rnormc":
            continue
        kb.dma("sync", consts[name], dram[name], writes=(cb[name],))
    kb.op("vector", lambda E: E.reciprocal(consts["rnormc"], consts["normc"]),
          reads=(cb["normc"],), writes=(cb["rnormc"],))

    _cc = {}
    _cc_area = A.alloc([128, 16], F32, "constcols")
    _cc_buf = Buf("constcols")

    def constcol(v):
        if v not in _cc:
            i = len(_cc)
            assert i < 16
            _cc[v] = _cc_area[:, i:i + 1]
            kb.op("vector", lambda E, i=i, v=v: E.memset(_cc_area[:, i:i + 1], v),
                  writes=(_cc_buf,))
        return _cc[v]

    st_sum = A.alloc([128, NT], F32, "st_sum")
    st_ssq = A.alloc([128, NT], F32, "st_ssq")
    st_mu = A.alloc([128, NT], F32, "st_mu")
    st_rstd = A.alloc([128, NT], F32, "st_rstd")
    st_nmr = A.alloc([128, NT], F32, "st_nmr")
    st_tmp = A.alloc([128, NT], F32, "st_tmp")
    sb_sum = [Buf() for _ in range(NT)]
    sb_ssq = [Buf() for _ in range(NT)]
    sb_mu = Buf("mu"); sb_rstd = Buf("rstd"); sb_nmr = Buf("nmr"); sb_tmp = Buf()
    dump = A.alloc([128, D], F32, "dump")
    b_dump = Buf("dump")

    if STAGES >= 5:
        rpbbc = A.alloc([128, 16 * 32], F32, "rpb_bc")
        rpbbf = A.alloc([128, 16 * 32], BF16, "rpb_bf")
        negt = A.alloc([128, 128], BF16, "neg_tile")

    def bmat_build():
        b_rpb = kb.buf("rpb")
        b_neg = kb.buf("neg")
        kb.op("vector", lambda E: E.memset(rpbbc[:], 0.0), writes=(b_rpb,))
        for h in range(NH):
            kb.dma("sync", rpbbc[:, h * 32:h * 32 + 25],
                   dram["L1_attn_rpb"][h:h + 1, :].partition_broadcast(128),
                   writes=(b_rpb,))
        kb.op("vector", lambda E: E.tensor_copy(rpbbf, rpbbc),
              reads=(b_rpb,), writes=(b_rpb,))
        kb.op("vector", lambda E: E.memset(negt[:], -30000.0), writes=(b_neg,))
        bm = bmat_hbm
        b_bm = kb.buf("bmat")
        for tt in range(3):
            for h in range(NH):
                for cc in range(2):
                    kb.dma(("sync", "scalar")[(tt + h + cc) % 2],
                           bm[tt, h, cc * 128:(cc + 1) * 128, :], negt[:],
                           reads=(b_neg,), writes=(b_bm,))
        for tt in range(3):
            ivals = l1attn_diag_intervals(tt)
            for h in range(NH):
                for (m, l0, l1) in ivals:
                    base = ((tt * NH + h) * 256) * 128
                    dst = bass.AP(bm.tensor, base + 128 * m + 129 * l0,
                                  [[129, l1 - l0]])
                    kb.dma(("sync", "scalar")[(h + m) % 2], dst,
                           rpbbf[l0:l1, h * 32 + m - 52:m - 51 + h * 32],
                           reads=(b_rpb,), writes=(b_bm,),
                           allow_slow_non_contiguous=True)
        l1f["b_bm"] = b_bm

    if STAGES >= 5:
        bmat_build()

    x_in = dram["L0_attn_x"]
    for t in range(NT):
        kb.dma("sync", xt(t), x_in[t * 128:(t + 1) * 128, :], writes=(xb[t],))


    def ln_stats_pass(read_tile, n_tiles=NT):
        for t in range(n_tiles):
            ap, bufs = read_tile(t)
            sb_sum[t] = kb.buf()
            sb_ssq[t] = kb.buf()
            kb.op("vector",
                  lambda E, ap=ap, t=t: E.reduce_sum(st_sum[:, t:t + 1], ap, axis=AX.X),
                  reads=bufs, writes=(sb_sum[t],))
            kb.op("scalar",
                  lambda E, ap=ap, t=t: E.activation(
                      dump[:], ap, AF.Square, accum_out=st_ssq[:, t:t + 1]),
                  reads=bufs, writes=(b_dump, sb_ssq[t]))
        kb.op("vector",
              lambda E: E.tensor_scalar(st_mu[:, 0:n_tiles], st_sum[:, 0:n_tiles],
                                        1.0 / D, None, op0=OP.mult),
              reads=sb_sum[:n_tiles], writes=(sb_mu,))
        kb.op("vector",
              lambda E: E.tensor_tensor(st_tmp[:, 0:n_tiles], st_mu[:, 0:n_tiles],
                                        st_mu[:, 0:n_tiles], op=OP.mult),
              reads=(sb_mu,), writes=(sb_tmp,))
        kb.op("vector",
              lambda E: E.scalar_tensor_tensor(
                  st_rstd[:, 0:n_tiles], st_ssq[:, 0:n_tiles], 1.0 / D,
                  st_tmp[:, 0:n_tiles], op0=OP.mult, op1=OP.subtract),
              reads=sb_ssq[:n_tiles] + [sb_tmp], writes=(sb_rstd,))
        eps_c = constcol(1e-5)
        kb.op("scalar",
              lambda E: E.activation(st_tmp[:, 0:n_tiles], st_rstd[:, 0:n_tiles],
                                     AF.Ln, bias=eps_c),
              reads=(sb_rstd, _cc_buf), writes=(sb_tmp,))
        kb.op("scalar",
              lambda E: E.activation(st_rstd[:, 0:n_tiles], st_tmp[:, 0:n_tiles],
                                     AF.Exp, scale=-0.5),
              reads=(sb_tmp,), writes=(sb_rstd,))
        kb.op("vector",
              lambda E: E.tensor_tensor(st_nmr[:, 0:n_tiles], st_mu[:, 0:n_tiles],
                                        st_rstd[:, 0:n_tiles], op=OP.mult),
              reads=(sb_mu, sb_rstd), writes=(sb_nmr,))
        kb.op("vector",
              lambda E: E.tensor_scalar(st_nmr[:, 0:n_tiles], st_nmr[:, 0:n_tiles],
                                        -1.0, None, op0=OP.mult),
              reads=(sb_nmr,), writes=(sb_nmr,))

    bstage = {"f": None, "b": None, "buf": None, "w": 0}

    def set_bias_stage(width):
        bstage["f"] = A.alloc([1, width], F32, "bstg_f")
        bstage["b"] = A.alloc([1, width], BF16, "bstg_b")
        bstage["buf"] = kb.buf("bstg")
        bstage["w"] = width

    def bias_rows(name, src_ap, width):
        assert width <= bstage["w"]
        rows = A.alloc([2, width], BF16, name + "_rows")
        bsrc = bstage["f"][0:1, 0:width]
        lo_b = bstage["b"][0:1, 0:width]
        b_stage = bstage["buf"]
        b_rows = kb.buf()
        kb.dma("sync", bsrc, src_ap, writes=(b_stage,))
        kb.op("vector", lambda E, rows=rows, bsrc=bsrc: E.tensor_copy(
            rows[0:1, :], bsrc), reads=(b_stage,), writes=(b_rows,))
        kb.op("vector", lambda E, rows=rows, bsrc=bsrc, lo_b=lo_b: E.tensor_tensor(
            lo_b, bsrc, rows[0:1, :], op=OP.subtract),
            reads=(b_stage, b_rows), writes=(b_stage,))
        kb.dma("sync", rows[1:2, :], lo_b, reads=(b_stage,), writes=(b_rows,))
        return rows, b_rows

    ones2 = A.alloc([2, 128], BF16, "ones2")
    b_ones2 = Buf()
    kb.op("vector", lambda E: E.memset(ones2[0:2, :], 1.0), writes=(b_ones2,))

    w0T = A.alloc([128, 8 * D], BF16, "w0T")
    b_w0T = Buf("w0T")
    A.push()
    pk = A.alloc([128, 2 * D], I32, "packed")
    b_pk = Buf()
    w0 = A.alloc([128, 8 * D], BF16, "w0_om")
    b_w0 = Buf("w0")
    tmp_i = A.alloc([128, D], I32, "unp_i")
    b_tmpi = Buf()
    pw = dram["L0_ffn_packed_weight"]
    kb.dma("sync", pk[:, 0:D], pw.rearrange("(r f) -> r f", f=D)[0:128, :], writes=(b_pk,))
    kb.dma("sync", pk[:, D:2 * D], pw.rearrange("(r f) -> r f", f=D)[128:256, :], writes=(b_pk,))
    for s in range(4):
        for half in range(2):
            src = pk[:, half * D:(half + 1) * D]
            j = (s * 256 + half * 128) // 128
            kb.op("vector",
                  lambda E, src=src, s=s: E.tensor_scalar(
                      tmp_i, src, 2 * s, 3, op0=OP.logical_shift_right,
                      op1=OP.bitwise_and),
                  reads=(b_pk,), writes=(b_tmpi,))
            kb.op("vector",
                  lambda E, j=j: E.tensor_scalar(
                      w0[:, j * D:(j + 1) * D], tmp_i, -1.0, None, op0=OP.add),
                  reads=(b_tmpi,), writes=(b_w0,))
    for i in range(8):
        for j in range(8):
            kb.dmat(kb.tq(), w0T[:, i * D + j * 128:i * D + (j + 1) * 128],
                    w0[:, j * D + i * 128:j * D + (i + 1) * 128],
                    reads=(b_w0,), writes=(b_w0T,))

    A.pop()
    kb.set_fence()

    bias0_bc = A.alloc([128, D], F32, "bias0_bc")
    b_bias0 = kb.buf("bias0")
    kb.dma("sync", bias0_bc, dram["L0_ffn_bias"][None, :].partition_broadcast(128),
           writes=(b_bias0,))
    kb.op("vector", lambda E: E.tensor_scalar(bias0_bc, bias0_bc, 0.5, None,
                                              op0=OP.mult),
          reads=(b_bias0,), writes=(b_bias0,))
    ws0 = A.alloc([128, 1], F32, "ws0")
    b_ws0 = kb.buf("ws0")
    kb.dma("sync", ws0, dram["L0_ffn_weight_scale"][None, :].partition_broadcast(128),
           writes=(b_ws0,))
    TAIL_OFF = (A.off + 31) & ~31


    def l0_ffn():
        A.push()
        kb.set_fence()
        ln_stats_pass(lambda t: (xt(t), (xb[t],)))
        lnx = A.alloc([128, D], F32, "lnx")
        amx = A.alloc([128, 4], F32, "amx")
        qi = A.alloc([128, D], I32, "qi")
        qf = A.alloc([128, D], F32, "qf")
        qm = A.alloc([128, D], F32, "qm")
        qb = [A.alloc([128, D], BF16, f"qb{i}") for i in range(2)]
        qT = [A.alloc([128, D], BF16, f"qT{i}") for i in range(2)]
        sc = [A.alloc([128, 2], F32, f"sc{i}") for i in range(2)]
        ep = [A.alloc([128, D], F32, f"ep{i}") for i in range(2)]
        b_lnx, b_amx, b_qi = kb.buf(), kb.buf(), kb.buf()
        b_qf, b_qm = kb.buf(), kb.buf()
        b_qb = [kb.buf(), kb.buf()]
        b_qT = [kb.buf(), kb.buf()]
        b_sc = [kb.buf(), kb.buf()]
        b_ep = [kb.buf(), kb.buf()]
        def epilogue(t):
            pp = t % 2
            for half in range(2):
                pidx = 4 + 2 * pp + half
                kb.op("vector",
                      lambda E, pidx=pidx, half=half, pp=pp: E.scalar_tensor_tensor(
                          ep[pp][:, half * 512:(half + 1) * 512],
                          psum[pidx][:, 0:512], sc[pp][:, 1:2],
                          bias0_bc[:, half * 512:(half + 1) * 512],
                          op0=OP.mult, op1=OP.add),
                      reads=(psb[pidx], b_sc[pp], b_bias0), writes=(b_ep[pp],))
            kb.op("vector",
                  lambda E, t=t, pp=pp: E.tensor_tensor(
                      xt(t), xt(t), ep[pp][:], op=OP.add),
                  reads=(b_ep[pp], xb[t]), writes=(xb[t],))

        for t in range(NT):
            pp = t % 2
            kb.op("vector",
                  lambda E, t=t: E.tensor_scalar(
                      lnx, xt(t), st_mu[:, t:t + 1], st_rstd[:, t:t + 1],
                      op0=OP.subtract, op1=OP.mult),
                  reads=(xb[t], sb_mu, sb_rstd), writes=(b_lnx,))
            kb.op("vector",
                  lambda E: E.reduce_max(amx[:, 0:1], lnx, axis=AX.X,
                                         apply_absolute_value=True),
                  reads=(b_lnx,), writes=(b_amx,))
            kb.op("vector",
                  lambda E: E.tensor_scalar(amx[:, 1:2], amx[:, 0:1], 1e-5, None,
                                            op0=OP.max),
                  reads=(b_amx,), writes=(b_amx,))
            kb.op("vector", lambda E: E.reciprocal(amx[:, 2:3], amx[:, 1:2]),
                  reads=(b_amx,), writes=(b_amx,))
            kb.op("vector",
                  lambda E, pp=pp: E.tensor_scalar(sc[pp][:, 0:1], amx[:, 2:3],
                                                   127.0, None, op0=OP.mult),
                  reads=(b_amx,), writes=(b_sc[pp],))
            kb.op("vector",
                  lambda E, pp=pp: E.tensor_scalar(sc[pp][:, 1:2], amx[:, 1:2],
                                                   ws0[:, 0:1], 0.5 / 127.0,
                                                   op0=OP.mult, op1=OP.mult),
                  reads=(b_amx, b_ws0), writes=(b_sc[pp],))
            kb.op("vector",
                  lambda E, pp=pp: E.tensor_scalar(lnx, lnx, sc[pp][:, 0:1], None,
                                                   op0=OP.mult),
                  reads=(b_lnx, b_sc[pp]), writes=(b_lnx,))
            kb.op("vector", lambda E: E.tensor_copy(qi, lnx),
                  reads=(b_lnx,), writes=(b_qi,))
            kb.op("scalar", lambda E: E.copy(qf, qi),
                  reads=(b_qi,), writes=(b_qf,))
            kb.op("vector", lambda E: E.tensor_tensor(lnx, lnx, qf,
                                                      op=OP.subtract),
                  reads=(b_lnx, b_qf), writes=(b_lnx,))
            kb.op("vector",
                  lambda E: E.scalar_tensor_tensor(qm, lnx, 0.5, qf,
                                                   op0=OP.is_ge, op1=OP.add),
                  reads=(b_lnx, b_qf), writes=(b_qm,))
            kb.op("vector",
                  lambda E: E.tensor_scalar(qf, lnx, -0.5, None, op0=OP.is_lt),
                  reads=(b_lnx,), writes=(b_qf,))
            kb.op("vector",
                  lambda E, pp=pp: E.tensor_tensor(qb[pp], qm, qf,
                                                   op=OP.subtract),
                  reads=(b_qm, b_qf), writes=(b_qb[pp],))
            if t == 0:
                tap("mu", st_mu[:, 0:NT], (sb_mu,))
                tap("rstd", st_rstd[:, 0:NT], (sb_rstd,))
                tap("sc", sc[0][:, 0:2], (b_sc[0],))
                tap("qb0", qb[0][:, 0:D], (b_qb[0],))
            for i in range(8):
                kb.dmat(kb.tq(), qT[pp][:, i * 128:(i + 1) * 128],
                        qb[pp][:, i * 128:(i + 1) * 128],
                        reads=(b_qb[pp],), writes=(b_qT[pp],))
            for half in range(2):
                pidx = 4 + 2 * pp + half
                for i in range(8):
                    kb.op("tensor",
                          lambda E, i=i, half=half, pidx=pidx, pp=pp: E.matmul(
                              psum[pidx][:, 0:512],
                              qT[pp][:, i * 128:(i + 1) * 128],
                              w0T[:, i * D + half * 512: i * D + half * 512 + 512],
                              start=(i == 0), stop=(i == 7)),
                          reads=(b_qT[pp], b_w0T), writes=(psb[pidx],))
            if t > 0:
                epilogue(t - 1)
        epilogue(NT - 1)
        A.pop()

    tap("w0T", w0T[:, 0:8 * D], (b_w0T,))
    tap("bias0", bias0_bc[:, 0:D], (b_bias0,))


    def l0_attn():
        EPSF = 1e-6
        ln_stats_pass(lambda t: (xt(t), (xb[t],)))
        xh = [kb.buf(f"xh[{t}]") for t in range(NT)]
        for t in range(NT):
            kb.dma("sync", x_hbm[t * 128:(t + 1) * 128, :], xt(t),
                   reads=(xb[t],), writes=(xh[t],))
        A.push()
        A.rebase(0, limit=128 * 1024)
        kb.set_fence()

        W = {}
        bW = {}
        wstg = [A.alloc([128, D], F32, f"fa_wstg{j}") for j in range(2)]
        bwstg = [kb.buf(), kb.buf()]
        cast_engs = ("vector", "scalar", "gpsimd")
        nload = 0
        for nm, dnm in (("q", "L0_attn_Wq"), ("k", "L0_attn_Wk"),
                        ("v", "L0_attn_Wv"), ("o", "L0_attn_Wo")):
            W[nm] = A.alloc([128, 8 * D], BF16, f"W{nm}_r")
            bW[nm] = kb.buf(f"W{nm}")
            for i in range(8):
                j = nload % 2
                kb.dma("sync", wstg[j], dram[dnm][i * 128:(i + 1) * 128, :],
                       writes=(bwstg[j],))
                eng = cast_engs[nload % 3]
                if eng == "scalar":
                    fn = lambda E, nm=nm, i=i, j=j: E.copy(
                        W[nm][:, i * D:(i + 1) * D], wstg[j])
                else:
                    fn = lambda E, nm=nm, i=i, j=j: E.tensor_copy(
                        W[nm][:, i * D:(i + 1) * D], wstg[j])
                kb.op(eng, fn, reads=(bwstg[j],), writes=(bW[nm],))
                nload += 1
        set_bias_stage(D)
        BR = {}
        for nm, dnm in (("q", "L0_attn_bq"), ("k", "L0_attn_bk"),
                        ("v", "L0_attn_bv"), ("o", "L0_attn_bo")):
            BR[nm] = bias_rows("l0b" + nm, dram[dnm][None, :], D)

        A.rebase(TAIL_OFF)
        xtile = A.alloc([128, D], F32, "fa_x")
        lnx_bf = A.alloc([128, D], BF16, "fa_lnx")
        lnxT = A.alloc([128, D], BF16, "fa_lnxT")
        t_bf = A.alloc([128, D], BF16, "fa_tanh")
        q_bf = A.alloc([128, D], BF16, "fa_q")
        k_bf = A.alloc([128, D], BF16, "fa_k")
        v_bf = A.alloc([128, D], BF16, "fa_v")
        qe_bf = A.alloc([128, D], BF16, "fa_qe")
        ke_bf = A.alloc([128, D], BF16, "fa_ke")
        kT = A.alloc([128, D], BF16, "fa_kT")
        qsiT = A.alloc([128, D], BF16, "fa_qsiT")
        kso_bf = A.alloc([128, D], BF16, "fa_kso")
        qsir_bf = A.alloc([128, D], BF16, "fa_qsir")
        vsc_bf = A.alloc([128, D], BF16, "fa_vsc")
        scbf = A.alloc([128, 48], BF16, "fa_scbf")
        bc_t = A.alloc([128, D], BF16, "fa_bc")
        sobc = sibc = scbc = bc_t
        tmp_bf = A.alloc([128, D], BF16, "fa_tmp")
        att_bf = A.alloc([128, D], BF16, "fa_att")
        attT = A.alloc([128, D], BF16, "fa_attT")
        Sm_bf = A.alloc([128, 128], BF16, "fa_Sm")
        KV_sb = A.alloc([128, 512], BF16, "fa_KVsb")
        KV_stg = A.alloc([64, 512], BF16, "fa_KVstg")
        cols = A.alloc([128, 16 * 13], F32, "fa_cols")
        col = lambda i: cols[:, i * 16:(i + 1) * 16]
        cs_bf = A.alloc([128, 16], BF16, "fa_csbf")
        csp_sb = A.alloc([128, 16], BF16, "fa_cspsb")
        e_sb = A.alloc([128, 32], F32, "fa_esb")
        eT_bf = A.alloc([8, 4 * 128], BF16, "fa_eT")

        bufs = {}
        for nm in ("xtile lnx lnxT tanh q k v qe ke kT qsiT kso qsir "
                   "vsc tmp att attT Sm KVsb KVstg csbf cspsb esb eT cols "
                   "sob sib scb bc").split():
            bufs[nm] = kb.buf("fa_" + nm)
        B = bufs

        def hd(ap):
            return ap.rearrange("p (h d) -> p h d", d=64)

        def dh(ap):
            return ap.rearrange("p (d h) -> p h d", h=16)

        def bcast(dst, dst_buf, src_cols, src_buf):
            f = src_cols
            src = bass.AP(f.tensor, f.offset,
                          [[f.ap[0][0], 128], [0, 64], [1, 16]])
            kb.dma("gpsimd", dst.rearrange("p (d h) -> p d h", h=16), src,
                   reads=(src_buf,), writes=(dst_buf,),
                   allow_slow_non_contiguous=True)

        names4 = ("k", "q", "kso", "qsi")

        for c in range(FA_CHUNKS):
            tok0 = c * 128
            kb.dma("sync", xtile, x_hbm[tok0:tok0 + 128, :],
                   reads=(xh[c],), writes=(B["xtile"],))
            kb.op("vector",
                  lambda E, c=c: E.tensor_scalar(
                      lnx_bf, xtile, st_mu[:, c:c + 1], st_rstd[:, c:c + 1],
                      op0=OP.subtract, op1=OP.mult),
                  reads=(B["xtile"], sb_mu, sb_rstd), writes=(B["lnx"],))
            for i in range(8):
                kb.dmat(kb.tq(), lnxT[:, i * 128:(i + 1) * 128],
                        lnx_bf[:, i * 128:(i + 1) * 128],
                        reads=(B["lnx"],), writes=(B["lnxT"],))
            def proj(nm, pb):
                for half in range(2):
                    pidx = pb[half]
                    for i in range(8):
                        kb.op("tensor",
                              lambda E, i=i, half=half, pidx=pidx, nm=nm: E.matmul(
                                  psum[pidx][:, 0:512],
                                  lnxT[:, i * 128:(i + 1) * 128],
                                  W[nm][:, i * D + half * 512:i * D + half * 512 + 512],
                                  start=(i == 0), stop=False),
                              reads=(B["lnxT"], bW[nm]), writes=(psb[pidx],))
                    kb.op("tensor",
                          lambda E, half=half, pidx=pidx, nm=nm: E.matmul(
                              psum[pidx][:, 0:512], ones2[:],
                              BR[nm][0][:, half * 512:half * 512 + 512],
                              start=False, stop=True),
                          reads=(b_ones2, BR[nm][1]), writes=(psb[pidx],))

            def sig_evac(pb, dst, dbuf):
                for half in range(2):
                    kb.op("scalar",
                          lambda E, half=half, pb=pb: E.activation(
                              t_bf[:, half * 512:half * 512 + 512],
                              psum[pb[half]][:, 0:512], AF.Tanh, scale=0.5),
                          reads=(psb[pb[half]],), writes=(B["tanh"],))
                kb.op("vector",
                      lambda E, dst=dst: E.tensor_scalar(dst, t_bf, 0.5, 0.5,
                                                         op0=OP.mult, op1=OP.add),
                      reads=(B["tanh"],), writes=(dbuf,))

            proj("q", (0, 1))
            proj("k", (2, 3))
            sig_evac((0, 1), q_bf, B["q"])
            sig_evac((2, 3), k_bf, B["k"])
            proj("v", (0, 1))
            for half in range(2):
                kb.op("scalar",
                      lambda E, half=half: E.copy(
                          v_bf[:, half * 512:half * 512 + 512],
                          psum[0 + half][:, 0:512]),
                      reads=(psb[0 + half],), writes=(B["v"],))
            kb.op("gpsimd",
                  lambda E: E.tensor_scalar(qe_bf, q_bf, EPSF, None, op0=OP.add),
                  reads=(B["q"],), writes=(B["qe"],))
            kb.op("gpsimd",
                  lambda E: E.tensor_scalar(ke_bf, k_bf, EPSF, None, op0=OP.add),
                  reads=(B["k"],), writes=(B["ke"],))

            def cum_group(pidx, rhs_ap, rhs_buf, eT_slot, c):
                kb.op("tensor",
                      lambda E: E.matmul(psum[pidx][:, 0:512], consts["triu"],
                                         rhs_ap, start=True, stop=True,
                                         skip_group_check=True),
                      reads=(cb["triu"], rhs_buf), writes=(psb[pidx],))
                if c > 0:
                    for ii in range(4):
                        i = (pidx % 2) * 4 + ii
                        kb.op("tensor",
                              lambda E, i=i, ii=ii: E.matmul(
                                  psum[pidx][:, ii * 128:ii * 128 + 128],
                                  consts["onehot8"][:, i * 128:(i + 1) * 128],
                                  eT_bf[:, eT_slot * 128:(eT_slot + 1) * 128],
                                  start=False, stop=True,
                                  skip_group_check=True),
                              reads=(cb["onehot8"], B["eT"]), writes=(psb[pidx],))

            for half in range(2):
                cum_group(half, k_bf[:, half * 512:half * 512 + 512], B["k"], 0, c)
                cum_group(2 + half, q_bf[:, half * 512:half * 512 + 512], B["q"], 1, c)

            if c == 1:
                tap("fa_eT", eT_bf[0:8, 0:4 * 128], (B["eT"],))
            if c in (0, 1) and DEBUG:
                cumdump = A
                kb.op("vector", lambda E: E.tensor_copy(dump[:, 0:512], psum[0][:, 0:512]),
                      reads=(psb[0],), writes=(b_dump,))
                kb.op("vector", lambda E: E.tensor_copy(dump[:, 512:1024], psum[1][:, 0:512]),
                      reads=(psb[1],), writes=(b_dump,))
                tap("fa_cumk" if c == 0 else "fa_cumk1", dump[:, 0:D], (b_dump,))
                kb.op("vector", lambda E: E.tensor_copy(dump[:, 0:512], psum[2][:, 0:512]),
                      reads=(psb[2],), writes=(b_dump,))
                kb.op("vector", lambda E: E.tensor_copy(dump[:, 512:1024], psum[3][:, 0:512]),
                      reads=(psb[3],), writes=(b_dump,))
                tap("fa_cumq" if c == 0 else "fa_cumq1", dump[:, 0:D], (b_dump,))
            for half in range(2):
                kb.op("vector",
                      lambda E, half=half: E.scalar_tensor_tensor(
                          tmp_bf[:, half * 512:half * 512 + 512],
                          psum[half][:, 0:512], EPSF,
                          qe_bf[:, half * 512:half * 512 + 512],
                          op0=OP.add, op1=OP.mult),
                      reads=(psb[half], B["qe"]), writes=(B["tmp"],))
            kb.op("vector",
                  lambda E: E.tensor_reduce(
                      col(0).unsqueeze(2),
                      tmp_bf.rearrange("p (h d) -> p h d", d=64),
                      axis=AX.X, op=OP.add),
                  reads=(B["tmp"],), writes=(B["cols"],))
            for half in range(2):
                kb.op("vector",
                      lambda E, half=half: E.scalar_tensor_tensor(
                          tmp_bf[:, half * 512:half * 512 + 512],
                          psum[2 + half][:, 0:512], EPSF,
                          ke_bf[:, half * 512:half * 512 + 512],
                          op0=OP.add, op1=OP.mult),
                      reads=(psb[2 + half], B["ke"]), writes=(B["tmp"],))
            kb.op("vector",
                  lambda E: E.tensor_reduce(
                      col(1).unsqueeze(2),
                      tmp_bf.rearrange("p (h d) -> p h d", d=64),
                      axis=AX.X, op=OP.add),
                  reads=(B["tmp"],), writes=(B["cols"],))
            kb.op("vector", lambda E: E.reciprocal(col(2), col(0)),
                  reads=(B["cols"],), writes=(B["cols"],))
            kb.op("vector", lambda E: E.reciprocal(col(3), col(1)),
                  reads=(B["cols"],), writes=(B["cols"],))
            kb.op("vector",
                  lambda E, c=c: E.tensor_scalar(col(4), col(2),
                                                 consts["normc"][:, c:c + 1], None,
                                                 op0=OP.mult),
                  reads=(B["cols"], cb["normc"]), writes=(B["cols"],))
            kb.op("vector",
                  lambda E, c=c: E.tensor_scalar(col(5), col(3),
                                                 consts["normc"][:, c:c + 1], None,
                                                 op0=OP.mult),
                  reads=(B["cols"], cb["normc"]), writes=(B["cols"],))
            kb.op("gpsimd", lambda E: E.tensor_copy(scbf[:, 0:16], col(5)),
                  reads=(B["cols"],), writes=(B["sob"],))
            kb.op("scalar", lambda E: E.copy(scbf[:, 16:32], col(4)),
                  reads=(B["cols"],), writes=(B["sib"],))
            bcast(sobc, B["bc"], scbf[:, 0:16], B["sob"])
            bcast(sibc, B["bc"], scbf[:, 16:32], B["sib"])
            kb.op("vector",
                  lambda E: E.tensor_tensor(hd(kso_bf), hd(k_bf), dh(sobc),
                                            op=OP.mult),
                  reads=(B["k"], B["bc"]), writes=(B["kso"],))
            kb.op("vector",
                  lambda E: E.tensor_tensor(hd(qsir_bf), hd(q_bf), dh(sibc),
                                            op=OP.mult),
                  reads=(B["q"], B["bc"]), writes=(B["qsir"],))

            for half in range(2):
                cum_group(half, kso_bf[:, half * 512:half * 512 + 512], B["kso"], 2, c)
                cum_group(2 + half, qsir_bf[:, half * 512:half * 512 + 512], B["qsir"], 3, c)

            for half in range(2):
                kb.op("vector",
                      lambda E, half=half: E.scalar_tensor_tensor(
                          tmp_bf[:, half * 512:half * 512 + 512],
                          psum[half][:, 0:512], EPSF,
                          qe_bf[:, half * 512:half * 512 + 512],
                          op0=OP.add, op1=OP.mult),
                      reads=(psb[half], B["qe"]), writes=(B["tmp"],))
            kb.op("vector",
                  lambda E: E.tensor_reduce(
                      col(6).unsqueeze(2),
                      tmp_bf.rearrange("p (h d) -> p h d", d=64),
                      axis=AX.X, op=OP.add),
                  reads=(B["tmp"],), writes=(B["cols"],))
            for half in range(2):
                kb.op("vector",
                      lambda E, half=half: E.scalar_tensor_tensor(
                          tmp_bf[:, half * 512:half * 512 + 512],
                          psum[2 + half][:, 0:512], EPSF,
                          ke_bf[:, half * 512:half * 512 + 512],
                          op0=OP.add, op1=OP.mult),
                      reads=(psb[2 + half], B["ke"]), writes=(B["tmp"],))
            kb.op("vector",
                  lambda E: E.tensor_reduce(
                      col(7).unsqueeze(2),
                      tmp_bf.rearrange("p (h d) -> p h d", d=64),
                      axis=AX.X, op=OP.add),
                  reads=(B["tmp"],), writes=(B["cols"],))
            kb.op("vector",
                  lambda E, c=c: E.tensor_scalar(col(6), col(6),
                                                 consts["rnormc"][:, c:c + 1], None,
                                                 op0=OP.mult),
                  reads=(B["cols"], cb["rnormc"]), writes=(B["cols"],))
            kb.op("vector",
                  lambda E, c=c: E.tensor_scalar(col(7), col(7),
                                                 consts["rnormc"][:, c:c + 1], None,
                                                 op0=OP.mult),
                  reads=(B["cols"], cb["rnormc"]), writes=(B["cols"],))
            kb.op("scalar",
                  lambda E: E.activation(col(9), col(6), AF.Tanh, scale=0.5),
                  reads=(B["cols"],), writes=(B["cols"],))
            kb.op("vector",
                  lambda E: E.tensor_scalar(col(9), col(9), 0.5, 0.5,
                                            op0=OP.mult, op1=OP.add),
                  reads=(B["cols"],), writes=(B["cols"],))
            kb.op("vector",
                  lambda E: E.tensor_tensor(col(12), col(9), col(2), op=OP.mult),
                  reads=(B["cols"],), writes=(B["cols"],))
            kb.op("scalar",
                  lambda E: E.activation(col(8), col(7), AF.Exp),
                  reads=(B["cols"],), writes=(B["cols"],))
            kb.op("vector", lambda E: E.tensor_copy(cs_bf, col(8)),
                  reads=(B["cols"],), writes=(B["csbf"],))

            if c > 0:
                kb.op("vector", lambda E: E.tensor_copy(csp_sb, psum[7][:, 32:48]),
                      reads=(psb[7],), writes=(B["cspsb"],))
            kb.op("tensor",
                  lambda E: E.matmul(psum[5][:, 0:16], consts["triu"], cs_bf,
                                     start=True, stop=True,
                                     skip_group_check=True),
                  reads=(cb["triu"], B["csbf"]), writes=(psb[5],))
            if c > 0:
                kb.op("tensor",
                      lambda E: E.matmul(psum[5][:, 0:16], consts["ones128"],
                                         csp_sb, start=False, stop=True,
                                         skip_group_check=True),
                      reads=(cb["ones128"], B["cspsb"]), writes=(psb[5],))
            kb.op("vector", lambda E: E.tensor_copy(col(10), psum[5][:, 0:16]),
                  reads=(psb[5],), writes=(B["cols"],))
            kb.op("vector", lambda E: E.reciprocal(col(10), col(10)),
                  reads=(B["cols"],), writes=(B["cols"],))
            kb.op("vector",
                  lambda E: E.tensor_tensor(col(11), col(8), col(10), op=OP.mult),
                  reads=(B["cols"],), writes=(B["cols"],))
            kb.op("vector",
                  lambda E, c=c: E.tensor_scalar(col(11), col(11),
                                                 consts["normc"][:, c:c + 1], None,
                                                 op0=OP.mult),
                  reads=(B["cols"], cb["normc"]), writes=(B["cols"],))
            kb.op("tensor",
                  lambda E, c=c: E.matmul(psum[7][:, 32:48], consts["ident"],
                                          cs_bf, start=(c == 0), stop=True,
                                          skip_group_check=True),
                  reads=(cb["ident"], B["csbf"]), writes=(psb[7],))
            srcs = {"k": (k_bf, "k"), "q": (q_bf, "q"),
                    "kso": (kso_bf, "kso"), "qsi": (qsir_bf, "qsir")}
            for qi_, nm in enumerate(names4):
                ap_, bn = srcs[nm]
                for i in range(8):
                    kb.op("tensor",
                          lambda E, ap_=ap_, i=i, qi_=qi_, c=c: E.matmul(
                              psum[7][:, qi_ * 8 + i:qi_ * 8 + i + 1],
                              ap_[:, i * 128:(i + 1) * 128],
                              consts["ones_col"], start=False, stop=True,
                              skip_group_check=True),
                          reads=(bufs[bn], cb["ones_col"]), writes=(psb[7],))
            if c < NT - 1:
                kb.op("vector", lambda E: E.tensor_copy(e_sb, psum[7][:, 0:32]),
                      reads=(psb[7],), writes=(B["esb"],))
                for qi_ in range(4):
                    kb.op("tensor",
                          lambda E, qi_=qi_: E.transpose(
                              psum[2][0:8, qi_ * 128:qi_ * 128 + 128],
                              e_sb[:, qi_ * 8:(qi_ + 1) * 8], consts["identf32"]),
                          reads=(B["esb"], cb["identf32"]), writes=(psb[2],))
                    kb.op("vector",
                          lambda E, qi_=qi_: E.tensor_copy(
                              eT_bf[:, qi_ * 128:(qi_ + 1) * 128],
                              psum[2][0:8, qi_ * 128:qi_ * 128 + 128]),
                          reads=(psb[2],), writes=(B["eT"],))
            kb.op("gpsimd", lambda E: E.tensor_copy(scbf[:, 32:48], col(11)),
                  reads=(B["cols"],), writes=(B["scb"],))
            bcast(scbc, B["bc"], scbf[:, 32:48], B["scb"])
            kb.op("vector",
                  lambda E: E.tensor_tensor(hd(vsc_bf), hd(v_bf), dh(scbc),
                                            op=OP.mult),
                  reads=(B["v"], B["bc"]), writes=(B["vsc"],))

            for i in range(8):
                kb.dmat(kb.tq(), kT[:, i * 128:(i + 1) * 128],
                        k_bf[:, i * 128:(i + 1) * 128],
                        reads=(B["k"],), writes=(B["kT"],))
                kb.dmat(kb.tq(), qsiT[:, i * 128:(i + 1) * 128],
                        q_bf[:, i * 128:(i + 1) * 128],
                        reads=(B["q"],), writes=(B["qsiT"],))
            if c > 0:
                kb.op("vector",
                      lambda E: E.tensor_copy(KV_sb[0:64, :], psum[6][0:64, 0:512]),
                      reads=(psb[6],), writes=(B["KVsb"],))
                kb.op("vector",
                      lambda E: E.tensor_copy(KV_stg[:], psum[4][0:64, 0:512]),
                      reads=(psb[4],), writes=(B["KVstg"],))
                kb.dma("sync", KV_sb[64:128, :], KV_stg[:],
                       reads=(B["KVstg"],), writes=(B["KVsb"],))

            if c == 1:
                tap("fa_KV", KV_sb[:, 0:512], (B["KVsb"],))
            for h in range(NH):
                i_ = h // 2
                r0 = (h % 2) * 64
                stb = 5 if h % 2 == 0 else 3
                ob = h % 2
                Smt = Sm_bf if h % 2 == 0 else tmp_bf[:, 0:128]
                Smb = B["Sm"] if h % 2 == 0 else B["tmp"]
                kb.op("tensor",
                      lambda E, i_=i_, r0=r0, stb=stb: E.matmul(
                          psum[stb][:, 0:128],
                          kT[r0:r0 + 64, i_ * 128:(i_ + 1) * 128],
                          qsiT[r0:r0 + 64, i_ * 128:(i_ + 1) * 128],
                          start=True, stop=True),
                      reads=(B["kT"], B["qsiT"]), writes=(psb[stb],))
                kb.op("vector",
                      lambda E, stb=stb, Smt=Smt: E.tensor_tensor(
                          Smt, psum[stb][:, 0:128],
                          consts["triu"], op=OP.mult),
                      reads=(psb[stb], cb["triu"]), writes=(Smb,))
                kb.op("tensor",
                      lambda E, h=h, ob=ob, Smt=Smt: E.matmul(
                          psum[ob][:, 0:64], Smt,
                          vsc_bf[:, h * 64:(h + 1) * 64],
                          start=True, stop=True),
                      reads=(Smb, B["vsc"]), writes=(psb[ob],))
                if c > 0:
                    kb.op("tensor",
                          lambda E, h=h, i_=i_, r0=r0, ob=ob: E.matmul(
                              psum[ob][:, 0:64],
                              qsiT[r0:r0 + 64, i_ * 128:(i_ + 1) * 128],
                              KV_sb[r0:r0 + 64, i_ * 64:(i_ + 1) * 64],
                              start=False, stop=True,
                              skip_group_check=True),
                          reads=(B["qsiT"], B["KVsb"]), writes=(psb[ob],))
                kb.op("vector",
                      lambda E, h=h, ob=ob: E.tensor_scalar(
                          att_bf[:, h * 64:(h + 1) * 64], psum[ob][:, 0:64],
                          col(12)[:, h:h + 1], None, op0=OP.mult),
                      reads=(psb[ob], B["cols"]), writes=(B["att"],))
                kvb = 6 if h % 2 == 0 else 4
                kb.op("tensor",
                      lambda E, h=h, i_=i_, c=c, kvb=kvb: E.matmul(
                          psum[kvb][0:64, i_ * 64:(i_ + 1) * 64],
                          k_bf[:, h * 64:(h + 1) * 64],
                          vsc_bf[:, h * 64:(h + 1) * 64],
                          start=(c == 0 and h < 2), stop=True,
                          skip_group_check=True),
                      reads=(B["k"], B["vsc"]), writes=(psb[kvb],))


            if c == (1 if FA_CHUNKS > 1 else 0):
                tap("fa_qe", qe_bf[:, 0:D], (B["qe"],))
                tap("fa_tmp", tmp_bf[:, 0:D], (B["tmp"],))
                tap("fa_q", q_bf[:, 0:D], (B["q"],))
                tap("fa_k", k_bf[:, 0:D], (B["k"],))
                tap("fa_v", v_bf[:, 0:D], (B["v"],))
                tap("fa_cols", cols[:, 0:16 * 12], (B["cols"],))
                tap("fa_kso", kso_bf[:, 0:D], (B["kso"],))
                tap("fa_vsc", vsc_bf[:, 0:D], (B["vsc"],))
                tap("fa_att", att_bf[:, 0:D], (B["att"],))
            for i in range(8):
                kb.dmat(kb.tq(), attT[:, i * 128:(i + 1) * 128],
                        att_bf[:, i * 128:(i + 1) * 128],
                        reads=(B["att"],), writes=(B["attT"],))
            for half in range(2):
                pidx = 2 + half
                for i in range(8):
                    kb.op("tensor",
                          lambda E, i=i, half=half, pidx=pidx: E.matmul(
                              psum[pidx][:, 0:512],
                              attT[:, i * 128:(i + 1) * 128],
                              W["o"][:, i * D + half * 512:i * D + half * 512 + 512],
                              start=(i == 0), stop=False),
                          reads=(B["attT"], bW["o"]), writes=(psb[pidx],))
                kb.op("tensor",
                      lambda E, half=half, pidx=pidx: E.matmul(
                          psum[pidx][:, 0:512], ones2[:],
                          BR["o"][0][:, half * 512:half * 512 + 512],
                          start=False, stop=True),
                      reads=(b_ones2, BR["o"][1]), writes=(psb[pidx],))
                kb.op("vector",
                      lambda E, half=half, pidx=pidx: E.tensor_tensor(
                          xtile[:, half * 512:half * 512 + 512],
                          xtile[:, half * 512:half * 512 + 512],
                          psum[pidx][:, 0:512], op=OP.add),
                      reads=(psb[pidx],), writes=(B["xtile"],))
            kb.dma("sync", x_hbm[tok0:tok0 + 128, :], xtile,
                   reads=(B["xtile"],), writes=(xh[c],))
        A.pop()
        return xh

    if STAGES >= 1:
        l0_ffn()



    def hbm_reader(n=2):
        tiles = [A.alloc([128, D], F32, f"xr{j}") for j in range(n)]
        tb = [kb.buf() for j in range(n)]

        def read_tile(t):
            j = t % n
            kb.dma("sync", tiles[j], x_hbm[t * 128:(t + 1) * 128, :],
                   reads=(xh[t],), writes=(tb[j],))
            return tiles[j], (tb[j],)
        return read_tile


    def _stream_rhs_mm(w_dram, rows0, cols0, ncols, lhsT_ap, lhsT_buf, m_rows,
                       pidx, stg, b_stg, stgbf, b_stgbf, extra_reads=()):
        for i in range(8):
            kb.dma("sync", stg[:, 0:ncols],
                   w_dram[rows0 + i * 128:rows0 + (i + 1) * 128,
                          cols0:cols0 + ncols],
                   writes=(b_stg,))
            kb.op("vector", lambda E, ncols=ncols: E.tensor_copy(
                stgbf[:, 0:ncols], stg[:, 0:ncols]),
                reads=(b_stg,), writes=(b_stgbf,))
            kb.op("tensor",
                  lambda E, i=i, pidx=pidx, m_rows=m_rows, ncols=ncols: E.matmul(
                      psum[pidx][0:m_rows, 0:ncols],
                      lhsT_ap[:, i * m_rows:(i + 1) * m_rows],
                      stgbf[:, 0:ncols], start=(i == 0), stop=False),
                  reads=(b_stgbf, lhsT_buf) + tuple(extra_reads),
                  writes=(psb[pidx],))

    def bias_mm(pidx, m_rows, rows_t, cols0, ncols):
        kb.op("tensor",
              lambda E: E.matmul(psum[pidx][0:m_rows, 0:ncols],
                                 ones2[:, 0:m_rows],
                                 rows_t[0][:, cols0:cols0 + ncols],
                                 start=False, stop=True),
              reads=(b_ones2, rows_t[1]), writes=(psb[pidx],))

    def l1_ffn_prep():
        kb.set_fence()
        A.push()
        A.rebase(0, limit=128 * 1024)
        p = l1f
        p["w1a"] = A.alloc([128, 8 * 512], BF16, "lf_w1a")
        p["b_w1a"] = kb.buf()
        p["w2"] = A.alloc([128, 4 * 64], BF16, "lf_w2")
        p["b_w2"] = kb.buf()
        p["Sini"] = A.alloc([64, D], F32, "lf_Sini")
        p["b_Sini"] = kb.buf()
        p["SiniT"] = A.alloc([128, 512], BF16, "lf_SiniT")
        p["b_SiniT"] = kb.buf()
        p["gh"] = A.alloc([64, 3 * D], BF16, "lf_gh")
        p["b_gh"] = kb.buf()
        p["stau"] = A.alloc([128, 1], F32, "lf_stau")
        p["b_stau"] = kb.buf()
        p["lngb"] = A.alloc([64, 2 * D], F32, "lf_lngb")
        p["b_lngb"] = kb.buf()
        p["logits"] = A.alloc([128, NT * 64], F32, "lf_logits")
        p["top8"] = A.alloc([128, NT * 8], F32, "lf_top8")
        p["alpha"] = A.alloc([128, NT * 64], BF16, "lf_alpha")
        p["alphaT"] = A.alloc([64, NT * 128], BF16, "lf_alphaT")
        p["hb"] = A.alloc([2, 512], BF16, "lf_hb")
        p["b_hb"] = kb.buf()
        A.push()
        A.rebase(TAIL_OFF)
        set_bias_stage(3 * D)
        A.pop()
        p["bih"] = bias_rows("lf_bih", dram["L1_ffn_bih"][None, :], 3 * D)
        p["bv"] = bias_rows("lf_bv", dram["L1_ffn_bv"][None, :], D)
        p["bvp"] = bias_rows("lf_bvp", dram["L1_ffn_bvp"][None, :], D)
        p["bo"] = bias_rows("lf_bo", dram["L1_ffn_bo"][None, :], D)
        p["b2r"] = bias_rows("lf_b2", dram["L1_ffn_b2"][None, :], 64)
        b1r = bias_rows("lf_b1", dram["L1_ffn_b1"][None, :], 512)
        bhh = bias_rows("lf_bhh", dram["L1_ffn_bhh"][None, :], 3 * D)
        l1f["end"] = A.off
        kb.set_fence()
        A.push()
        A.rebase(TAIL_OFF)
        stg = A.alloc([128, D], F32, "lf_stg")
        b_stg = kb.buf()
        stgbf = A.alloc([128, D], BF16, "lf_stgbf")
        b_stgbf = kb.buf()
        for i in range(8):
            kb.dma("sync", stg[:, 0:512],
                   dram["L1_ffn_w1"][i * 128:(i + 1) * 128, :], writes=(b_stg,))
            kb.op("vector", lambda E, i=i: E.tensor_copy(
                p["w1a"][:, i * 512:(i + 1) * 512], stg[:, 0:512]),
                reads=(b_stg,), writes=(p["b_w1a"],))
        for i in range(4):
            kb.dma("sync", stg[:, 0:64],
                   dram["L1_ffn_w2"][i * 128:(i + 1) * 128, :], writes=(b_stg,))
            kb.op("vector", lambda E, i=i: E.tensor_copy(
                p["w2"][:, i * 64:(i + 1) * 64], stg[:, 0:64]),
                reads=(b_stg,), writes=(p["b_w2"],))
        kb.dma("sync", p["Sini"], dram["L1_ffn_slot_init"], writes=(p["b_Sini"],))
        kb.op("vector", lambda E: E.tensor_copy(stgbf[0:64, :], p["Sini"]),
              reads=(p["b_Sini"],), writes=(b_stgbf,))
        for i in range(8):
            kb.dmat(kb.tq(), p["SiniT"][:, i * 64:(i + 1) * 64],
                    stgbf[0:64, i * 128:(i + 1) * 128],
                    reads=(b_stgbf,), writes=(p["b_SiniT"],))
        kb.dma("sync", p["lngb"][0:64, 0:D],
               dram["L1_ffn_ln_g"][None, :].partition_broadcast(64),
               writes=(p["b_lngb"],))
        kb.dma("sync", p["lngb"][0:64, D:2 * D],
               dram["L1_ffn_ln_b"][None, :].partition_broadcast(64),
               writes=(p["b_lngb"],))
        tau1 = A.alloc([1, 4], F32, "lf_tau")
        b_tau = kb.buf()
        kb.dma("sync", tau1[0:1, 0:1], dram["L1_ffn_tau"][None, :], writes=(b_tau,))
        kb.op("scalar", lambda E: E.activation(
            tau1[0:1, 1:2], tau1[0:1, 0:1], AF.Abs),
            reads=(b_tau,), writes=(b_tau,))
        kb.op("vector", lambda E: E.tensor_scalar(
            tau1[0:1, 1:2], tau1[0:1, 1:2], 0.1, None, op0=OP.add),
            reads=(b_tau,), writes=(b_tau,))
        kb.op("vector", lambda E: E.reciprocal(tau1[0:1, 2:3], tau1[0:1, 1:2]),
              reads=(b_tau,), writes=(b_tau,))
        kb.dma("sync", row_hbm[0:1, 0:1], tau1[0:1, 2:3], reads=(b_tau,),
               writes=(b_tau,))
        kb.dma("sync", p["stau"], row_hbm[0:1, 0:1].partition_broadcast(128),
               reads=(b_tau,), writes=(p["b_stau"],))
        kb.op("tensor",
              lambda E: E.matmul(psum[0][0:1, 0:512], consts["o64_col"],
                                 stgbf[0:64, 0:512], start=True, stop=True),
              reads=(b_stgbf, cb["o64_col"]), writes=(psb[0],))
        kb.op("tensor",
              lambda E: E.matmul(psum[1][0:1, 0:512], consts["o64_col"],
                                 stgbf[0:64, 512:1024], start=True, stop=True),
              reads=(b_stgbf, cb["o64_col"]), writes=(psb[1],))
        sm = A.alloc([1, D], F32, "lf_sm")
        b_sm = kb.buf()
        kb.op("vector", lambda E: E.tensor_scalar(
            sm[0:1, 0:512], psum[0][0:1, 0:512], 1.0 / 64, None, op0=OP.mult),
            reads=(psb[0],), writes=(b_sm,))
        kb.op("vector", lambda E: E.tensor_scalar(
            sm[0:1, 512:1024], psum[1][0:1, 0:512], 1.0 / 64, None, op0=OP.mult),
            reads=(psb[1],), writes=(b_sm,))
        kb.dma("sync", row_hbm[1:2, 0:D], sm[0:1, :], reads=(b_sm,), writes=(b_sm,))
        smT = A.alloc([128, 8], F32, "lf_smT")
        smTb = A.alloc([128, 8], BF16, "lf_smTb")
        b_smT = kb.buf()
        for i in range(8):
            kb.dma("gpsimd", smT[:, i:i + 1],
                   row_hbm[1:2, i * 128:(i + 1) * 128].rearrange("a b -> b a"),
                   reads=(b_sm,), writes=(b_smT,), allow_slow_non_contiguous=True)
        kb.op("vector", lambda E: E.tensor_copy(smTb, smT),
              reads=(b_smT,), writes=(b_smT,))
        for i in range(8):
            kb.dma("sync", stg[:, 0:512],
                   dram["L1_ffn_w1"][D + i * 128:D + (i + 1) * 128, :],
                   writes=(b_stg,))
            kb.op("vector", lambda E: E.tensor_copy(stgbf[:, 0:512], stg[:, 0:512]),
                  reads=(b_stg,), writes=(b_stgbf,))
            kb.op("tensor",
                  lambda E, i=i: E.matmul(psum[2][0:1, 0:512], smTb[:, i:i + 1],
                                          stgbf[:, 0:512], start=(i == 0),
                                          stop=False),
                  reads=(b_smT, b_stgbf), writes=(psb[2],))
        bias_mm(2, 1, b1r, 0, 512)
        hbf = A.alloc([1, 512], F32, "lf_hbf")
        hblo = A.alloc([1, 512], BF16, "lf_hblo")
        b_hbf = kb.buf()
        kb.op("vector", lambda E: E.tensor_copy(hbf, psum[2][0:1, 0:512]),
              reads=(psb[2],), writes=(b_hbf,))
        kb.op("vector", lambda E: E.tensor_copy(p["hb"][0:1, :], hbf),
              reads=(b_hbf,), writes=(p["b_hb"],))
        kb.op("vector", lambda E: E.tensor_tensor(hblo, hbf, p["hb"][0:1, :],
                                                  op=OP.subtract),
              reads=(b_hbf, p["b_hb"]), writes=(b_hbf,))
        kb.dma("sync", p["hb"][1:2, :], hblo, reads=(b_hbf,), writes=(p["b_hb"],))
        _gru_mm(dram["L1_ffn_Whh"], p["SiniT"], p["b_SiniT"], bhh,
                p["gh"], p["b_gh"], stg, b_stg)
        A.pop()
        A.pop()

    def _gru_mm(w_dram, lhsT_ap, lhsT_buf, brows, out_sb, out_buf, stg, b_stg):
        A.push()
        wT = A.alloc([128, 8 * D], BF16, "gru_wT")
        b_wT = kb.buf()
        wbfs = A.alloc([128, D], BF16, "gru_wbf")
        b_wbfs = kb.buf()
        for g in range(3):
            for oc in range(8):
                kb.dma("sync", stg,
                       w_dram[g * D + oc * 128:g * D + (oc + 1) * 128, :],
                       writes=(b_stg,))
                kb.op("vector", lambda E: E.tensor_copy(wbfs, stg),
                      reads=(b_stg,), writes=(b_wbfs,))
                for i in range(8):
                    kb.dmat(kb.tq(), wT[:, i * D + oc * 128:i * D + (oc + 1) * 128],
                            wbfs[:, i * 128:(i + 1) * 128],
                            reads=(b_wbfs,), writes=(b_wT,))
            for half in range(2):
                pidx = half
                for i in range(8):
                    kb.op("tensor",
                          lambda E, i=i, half=half, pidx=pidx: E.matmul(
                              psum[pidx][0:64, 0:512],
                              lhsT_ap[:, i * 64:(i + 1) * 64],
                              wT[:, i * D + half * 512:i * D + half * 512 + 512],
                              start=(i == 0), stop=False),
                          reads=(lhsT_buf, b_wT), writes=(psb[pidx],))
                kb.op("tensor",
                      lambda E, half=half, pidx=pidx, g=g: E.matmul(
                          psum[pidx][0:64, 0:512], ones2[:, 0:64],
                          brows[0][:, g * D + half * 512:g * D + half * 512 + 512],
                          start=False, stop=True),
                      reads=(b_ones2, brows[1]), writes=(psb[pidx],))
                kb.op("vector",
                      lambda E, half=half, pidx=pidx, g=g: E.tensor_copy(
                          out_sb[0:64, g * D + half * 512:g * D + half * 512 + 512],
                          psum[pidx][0:64, 0:512]),
                      reads=(psb[pidx],), writes=(out_buf,))
        A.pop()

    def l1_ffn():
        p = l1f
        kb.set_fence()
        A.push()
        A.rebase(l1f["end"], limit=128 * 1024)
        lnx_bf = A.alloc([128, D], BF16, "l1_lnx")
        lnxT = A.alloc([128, D], BF16, "l1_lnxT")
        h_bf = A.alloc([128, 512], BF16, "l1_h")
        hT = A.alloc([128, 512], BF16, "l1_hT")
        t64 = A.alloc([128, 64], F32, "l1_t64")
        m64 = A.alloc([128, 8], F32, "l1_m64")
        cols = A.alloc([128, 8], F32, "l1_cols")
        alpha32 = A.alloc([128, 64], F32, "l1_a32")
        b_a32 = kb.buf()
        Bb = {nm: kb.buf("l1_" + nm) for nm in
              "lnx lnxT h hT t64 m64 cols".split()}
        A.push()
        rd_stats = hbm_reader()
        ln_stats_pass(rd_stats)
        rd_a = hbm_reader()
        for t in range(NT):
            xa, xab = rd_a(t)
            kb.op("vector",
                  lambda E, t=t, xa=xa: E.tensor_scalar(
                      lnx_bf, xa, st_mu[:, t:t + 1], st_rstd[:, t:t + 1],
                      op0=OP.subtract, op1=OP.mult),
                  reads=xab + (sb_mu, sb_rstd), writes=(Bb["lnx"],))
            for i in range(8):
                kb.dmat(kb.tq(), lnxT[:, i * 128:(i + 1) * 128],
                        lnx_bf[:, i * 128:(i + 1) * 128],
                        reads=(Bb["lnx"],), writes=(Bb["lnxT"],))
            for i in range(8):
                kb.op("tensor",
                      lambda E, i=i: E.matmul(
                          psum[0][:, 0:512], lnxT[:, i * 128:(i + 1) * 128],
                          p["w1a"][:, i * 512:(i + 1) * 512],
                          start=(i == 0), stop=False),
                      reads=(Bb["lnxT"], p["b_w1a"]), writes=(psb[0],))
            kb.op("tensor",
                  lambda E: E.matmul(psum[0][:, 0:512], ones2[:], p["hb"][:],
                                     start=False, stop=True),
                  reads=(b_ones2, p["b_hb"]), writes=(psb[0],))
            kb.op("scalar",
                  lambda E: E.activation(h_bf, psum[0][:, 0:512], AF.Gelu),
                  reads=(psb[0],), writes=(Bb["h"],))
            for i in range(4):
                kb.dmat(kb.tq(), hT[:, i * 128:(i + 1) * 128],
                        h_bf[:, i * 128:(i + 1) * 128],
                        reads=(Bb["h"],), writes=(Bb["hT"],))
            for i in range(4):
                kb.op("tensor",
                      lambda E, i=i: E.matmul(
                          psum[1][:, 0:64], hT[:, i * 128:(i + 1) * 128],
                          p["w2"][:, i * 64:(i + 1) * 64],
                          start=(i == 0), stop=False),
                      reads=(Bb["hT"], p["b_w2"]), writes=(psb[1],))
            bias_mm(1, 128, p["b2r"], 0, 64)
            kb.op("vector",
                  lambda E, t=t: E.tensor_scalar(
                      p["logits"][:, t * 64:(t + 1) * 64], psum[1][:, 0:64],
                      p["stau"][:, 0:1], None, op0=OP.mult),
                  reads=(psb[1], p["b_stau"]), writes=(kb.buf(),))
            kb.op("vector",
                  lambda E, t=t: E.max(p["top8"][:, t * 8:(t + 1) * 8],
                                       p["logits"][:, t * 64:(t + 1) * 64]),
                  reads=(), writes=(kb.buf(),))
        blog = kb.buf("logits_all")
        btop = kb.buf("top8_all")
        kb.op("vector", lambda E: E.engine_nop(), reads=(), writes=(blog, btop))
        rd_a2 = hbm_reader()
        for t in range(NT):
            lg = p["logits"][:, t * 64:(t + 1) * 64]
            t8 = p["top8"][:, t * 8:(t + 1) * 8]
            kb.op("vector",
                  lambda E, t8=t8: E.tensor_scalar(cols[:, 0:1], t8[:, 0:1], -1.0,
                                                   None, op0=OP.mult),
                  reads=(btop,), writes=(Bb["cols"],))
            kb.op("scalar",
                  lambda E, t8=t8: E.activation(m64[:, 0:8], t8, AF.Exp,
                                                bias=cols[:, 0:1],
                                                accum_out=cols[:, 1:2]),
                  reads=(btop, Bb["cols"]), writes=(Bb["m64"], Bb["cols"]))
            kb.op("scalar",
                  lambda E, lg=lg: E.activation(t64, lg, AF.Exp,
                                                bias=cols[:, 0:1]),
                  reads=(blog, Bb["cols"]), writes=(Bb["t64"],))
            kb.op("vector", lambda E: E.reciprocal(cols[:, 2:3], cols[:, 1:2]),
                  reads=(Bb["cols"],), writes=(Bb["cols"],))
            kb.op("vector",
                  lambda E, t8=t8: E.tensor_scalar(
                      m64[:, 0:1], t8[:, 7:8], 1.0, None, op0=OP.mult),
                  reads=(btop,), writes=(Bb["m64"],))
            kb.op("vector",
                  lambda E, lg=lg: E.scalar_tensor_tensor(
                      t64, lg, m64[:, 0:1], t64, op0=OP.is_ge, op1=OP.mult),
                  reads=(blog, Bb["m64"], Bb["t64"]), writes=(Bb["t64"],))
            kb.op("vector",
                  lambda E: E.tensor_scalar(alpha32, t64, cols[:, 2:3],
                                            None, op0=OP.mult),
                  reads=(Bb["t64"], Bb["cols"]), writes=(b_a32,))
            kb.op("vector",
                  lambda E, t=t: E.tensor_copy(
                      p["alpha"][:, t * 64:(t + 1) * 64], alpha32),
                  reads=(b_a32,), writes=(kb.buf(),))
            kb.op("tensor",
                  lambda E, t=t: E.transpose(
                      psum[2][0:64, 0:128], alpha32, consts["identf32"]),
                  reads=(b_a32, cb["identf32"]), writes=(psb[2],))
            kb.op("vector",
                  lambda E, t=t: E.tensor_copy(
                      p["alphaT"][0:64, t * 128:(t + 1) * 128],
                      psum[2][0:64, 0:128]),
                  reads=(psb[2],), writes=(kb.buf(),))
            xa, xab = rd_a2(t)
            kb.op("vector",
                  lambda E, t=t, xa=xa: E.tensor_scalar(
                      lnx_bf, xa, st_mu[:, t:t + 1], st_rstd[:, t:t + 1],
                      op0=OP.subtract, op1=OP.mult),
                  reads=xab + (sb_mu, sb_rstd), writes=(Bb["lnx"],))
            for half in range(2):
                kb.op("tensor",
                      lambda E, t=t, half=half: E.matmul(
                          psum[half][0:64, 0:512],
                          p["alpha"][:, t * 64:(t + 1) * 64],
                          lnx_bf[:, half * 512:half * 512 + 512],
                          start=(t == 0), stop=(t == NT - 1),
                          skip_group_check=True),
                      reads=(Bb["lnx"],), writes=(psb[half],))
            kb.op("tensor",
                  lambda E, t=t: E.matmul(
                      psum[3][0:64, 0:1], p["alpha"][:, t * 64:(t + 1) * 64],
                      consts["ones_col"], start=(t == 0), stop=(t == NT - 1),
                      skip_group_check=True),
                  reads=(cb["ones_col"],), writes=(psb[3],))
        balpha = kb.buf("alpha_all")
        balphaT = kb.buf("alphaT_all")
        kb.op("vector", lambda E: E.engine_nop(), reads=(), writes=(balpha, balphaT))

        A.pop()
        kb.set_fence()
        A.push()
        sl = A.alloc([64, D], F32, "l1_sl")
        sl2 = A.alloc([64, D], F32, "l1_sl2")
        sl3 = A.alloc([64, D], F32, "l1_sl3")
        Snew = A.alloc([64, D], F32, "l1_Snew")
        sbf = A.alloc([64, D], BF16, "l1_sbf")
        sT = A.alloc([128, 512], BF16, "l1_sT")
        qkT = A.alloc([128, 256], BF16, "l1_qkT")
        attn64 = A.alloc([64, 64], BF16, "l1_attn")
        sv_bf = A.alloc([64, D], BF16, "l1_sv")
        scol = A.alloc([64, 16], F32, "l1_scol")
        bqT = A.alloc([128, 8], F32, "l1_bqT")
        gi_sb = A.alloc([64, 3 * D], BF16, "l1_gi")
        stg2 = A.alloc([128, D], F32, "l1_stg")
        stg2bf = A.alloc([128, D], BF16, "l1_stgbf")
        S = {nm: kb.buf("l1s_" + nm) for nm in
             "sl sl2 sl3 Snew sbf sT qkT attn64 sv scol bqT gi stg stgbf".split()}

        def slot_T(src_ap, src_buf):
            kb.op("vector", lambda E, src_ap=src_ap: E.tensor_copy(sbf, src_ap),
                  reads=(src_buf,), writes=(S["sbf"],))
            for i in range(8):
                kb.dmat(kb.tq(), sT[:, i * 64:(i + 1) * 64],
                        sbf[0:64, i * 128:(i + 1) * 128],
                        reads=(S["sbf"],), writes=(S["sT"],))

        kb.op("vector", lambda E: E.tensor_scalar(
            scol[:, 0:1], psum[3][0:64, 0:1], 1e-8, None, op0=OP.add),
            reads=(psb[3],), writes=(S["scol"],))
        kb.op("vector", lambda E: E.reciprocal(scol[:, 1:2], scol[:, 0:1]),
              reads=(S["scol"],), writes=(S["scol"],))
        for half in range(2):
            kb.op("vector",
                  lambda E, half=half: E.tensor_scalar(
                      sl[0:64, half * 512:half * 512 + 512],
                      psum[half][0:64, 0:512], scol[:, 1:2], None, op0=OP.mult),
                  reads=(psb[half], S["scol"]), writes=(S["sl"],))
        slot_T(sl[0:64, :], S["sl"])
        _gru_mm(dram["L1_ffn_Wih"], sT, S["sT"], p["bih"], gi_sb, S["gi"],
                stg2, S["stg"])
        def gate_pre(g, dst, dbuf):
            kb.op("vector",
                  lambda E, g=g, dst=dst: E.tensor_tensor(
                      dst, gi_sb[0:64, g * D:(g + 1) * D],
                      p["gh"][0:64, g * D:(g + 1) * D], op=OP.add),
                  reads=(S["gi"], p["b_gh"]), writes=(dbuf,))

        def sigm(ap, buf):
            kb.op("scalar", lambda E: E.activation(ap, ap, AF.Tanh, scale=0.5),
                  reads=(buf,), writes=(buf,))
            kb.op("vector", lambda E: E.tensor_scalar(ap, ap, 0.5, 0.5,
                                                      op0=OP.mult, op1=OP.add),
                  reads=(buf,), writes=(buf,))

        gate_pre(0, sl2[0:64, :], S["sl2"])
        sigm(sl2[0:64, :], S["sl2"])
        gate_pre(1, sl3[0:64, :], S["sl3"])
        sigm(sl3[0:64, :], S["sl3"])
        kb.op("vector",
              lambda E: E.tensor_tensor(sl[0:64, :], sl2[0:64, :],
                                        p["gh"][0:64, 2 * D:3 * D], op=OP.mult),
              reads=(S["sl2"], p["b_gh"]), writes=(S["sl"],))
        kb.op("vector",
              lambda E: E.tensor_tensor(sl[0:64, :], sl[0:64, :],
                                        gi_sb[0:64, 2 * D:3 * D], op=OP.add),
              reads=(S["sl"], S["gi"]), writes=(S["sl"],))
        kb.op("scalar", lambda E: E.activation(sl[0:64, :], sl[0:64, :], AF.Tanh),
              reads=(S["sl"],), writes=(S["sl"],))
        kb.op("vector",
              lambda E: E.tensor_tensor(Snew[0:64, :], p["Sini"][0:64, :],
                                        sl[0:64, :], op=OP.subtract),
              reads=(p["b_Sini"], S["sl"]), writes=(S["Snew"],))
        kb.op("vector",
              lambda E: E.tensor_tensor(Snew[0:64, :], Snew[0:64, :],
                                        sl3[0:64, :], op=OP.mult),
              reads=(S["Snew"], S["sl3"]), writes=(S["Snew"],))
        kb.op("vector",
              lambda E: E.tensor_tensor(Snew[0:64, :], Snew[0:64, :],
                                        sl[0:64, :], op=OP.add),
              reads=(S["Snew"], S["sl"]), writes=(S["Snew"],))
        slot_T(Snew[0:64, :], S["Snew"])
        kb.dma("sync", row_hbm[2:3, 0:256], dram["L1_ffn_bq"][None, :],
               writes=(S["bqT"],))
        kb.dma("sync", row_hbm[3:4, 0:256], dram["L1_ffn_bk"][None, :],
               writes=(S["bqT"],))
        for i in range(2):
            kb.dma("gpsimd", bqT[:, i:i + 1],
                   row_hbm[2:3, i * 128:(i + 1) * 128].rearrange("a b -> b a"),
                   reads=(S["bqT"],), writes=(S["bqT"],),
                   allow_slow_non_contiguous=True)
            kb.dma("gpsimd", bqT[:, 2 + i:3 + i],
                   row_hbm[3:4, i * 128:(i + 1) * 128].rearrange("a b -> b a"),
                   reads=(S["bqT"],), writes=(S["bqT"],),
                   allow_slow_non_contiguous=True)
        kb.op("vector", lambda E: E.tensor_scalar(bqT[:, 0:2], bqT[:, 0:2],
                                                  0.0625, None, op0=OP.mult),
              reads=(S["bqT"],), writes=(S["bqT"],))
        for nm, base, bcol, sc_ in (("L1_ffn_wq", 0, 0, 0.0625),
                                    ("L1_ffn_wk", 128, 2, 1.0)):
            for i in range(8):
                kb.dma("sync", stg2[:, 0:256], dram[nm][i * 128:(i + 1) * 128, :],
                       writes=(S["stg"],))
                kb.op("vector", lambda E: E.tensor_copy(stg2bf[:, 0:256],
                                                        stg2[:, 0:256]),
                      reads=(S["stg"],), writes=(S["stgbf"],))
                for dqc in range(2):
                    kb.op("tensor",
                          lambda E, i=i, dqc=dqc: E.matmul(
                              psum[4 + dqc][0:128, 0:64],
                              stg2bf[:, dqc * 128:(dqc + 1) * 128],
                              sT[:, i * 64:(i + 1) * 64],
                              start=(i == 0), stop=(i == 7)),
                          reads=(S["stgbf"], S["sT"]), writes=(psb[4 + dqc],))
            for dqc in range(2):
                kb.op("scalar",
                      lambda E, dqc=dqc, base=base, sc_=sc_, bcol=bcol: E.activation(
                          qkT[:, base + dqc * 64:base + dqc * 64 + 64],
                          psum[4 + dqc][0:128, 0:64], AF.Identity,
                          bias=bqT[:, bcol + dqc:bcol + dqc + 1], scale=sc_),
                      reads=(psb[4 + dqc], S["bqT"]), writes=(S["qkT"],))
        for dqc in range(2):
            kb.op("tensor",
                  lambda E, dqc=dqc: E.matmul(
                      psum[2][0:64, 0:64], qkT[:, dqc * 64:dqc * 64 + 64],
                      qkT[:, 128 + dqc * 64:128 + dqc * 64 + 64],
                      start=(dqc == 0), stop=(dqc == 1)),
                  reads=(S["qkT"],), writes=(psb[2],))
        kb.op("vector",
              lambda E: E.reduce_max(scol[:, 2:3], psum[2][0:64, 0:64], axis=AX.X),
              reads=(psb[2],), writes=(S["scol"],))
        kb.op("vector", lambda E: E.tensor_scalar(scol[:, 2:3], scol[:, 2:3],
                                                  -1.0, None, op0=OP.mult),
              reads=(S["scol"],), writes=(S["scol"],))
        kb.op("scalar",
              lambda E: E.activation(sl2[0:64, 0:64], psum[2][0:64, 0:64], AF.Exp,
                                     bias=scol[:, 2:3], accum_out=scol[:, 3:4]),
              reads=(psb[2], S["scol"]), writes=(S["sl2"], S["scol"]))
        kb.op("vector", lambda E: E.reciprocal(scol[:, 4:5], scol[:, 3:4]),
              reads=(S["scol"],), writes=(S["scol"],))
        kb.op("vector",
              lambda E: E.tensor_scalar(sl2[0:64, 0:64], sl2[0:64, 0:64],
                                        scol[:, 4:5], None, op0=OP.mult),
              reads=(S["sl2"], S["scol"]), writes=(S["sl2"],))
        kb.op("tensor",
              lambda E: E.transpose(psum[2][0:64, 0:64], sl2[0:64, 0:64],
                                    consts["identf32"][0:64, 0:64]),
              reads=(S["sl2"], cb["identf32"]), writes=(psb[2],))
        kb.op("vector",
              lambda E: E.tensor_copy(attn64, psum[2][0:64, 0:64]),
              reads=(psb[2],), writes=(S["attn64"],))
        for half in range(2):
            _stream_rhs_mm(dram["L1_ffn_wv"], 0, half * 512, 512, sT, S["sT"],
                           64, 4 + half, stg2, S["stg"], stg2bf, S["stgbf"])
            bias_mm(4 + half, 64, p["bv"], half * 512, 512)
            kb.op("vector",
                  lambda E, half=half: E.tensor_copy(
                      sv_bf[0:64, half * 512:half * 512 + 512],
                      psum[4 + half][0:64, 0:512]),
                  reads=(psb[4 + half],), writes=(S["sv"],))
        for half in range(2):
            kb.op("tensor",
                  lambda E, half=half: E.matmul(
                      psum[4 + half][0:64, 0:512], attn64,
                      sv_bf[0:64, half * 512:half * 512 + 512],
                      start=True, stop=True),
                  reads=(S["attn64"], S["sv"]), writes=(psb[4 + half],))
            kb.op("vector",
                  lambda E, half=half: E.tensor_tensor(
                      sl[0:64, half * 512:half * 512 + 512],
                      Snew[0:64, half * 512:half * 512 + 512],
                      psum[4 + half][0:64, 0:512], op=OP.add),
                  reads=(S["Snew"], psb[4 + half]), writes=(S["sl"],))
        kb.op("vector", lambda E: E.reduce_sum(scol[:, 5:6], sl[0:64, :], axis=AX.X),
              reads=(S["sl"],), writes=(S["scol"],))
        kb.op("scalar",
              lambda E: E.activation(sl2[0:64, :], sl[0:64, :], AF.Square,
                                     accum_out=scol[:, 6:7]),
              reads=(S["sl"],), writes=(S["sl2"], S["scol"]))
        kb.op("vector", lambda E: E.tensor_scalar(scol[:, 7:8], scol[:, 5:6],
                                                  1.0 / D, None, op0=OP.mult),
              reads=(S["scol"],), writes=(S["scol"],))
        kb.op("vector",
              lambda E: E.tensor_tensor(scol[:, 8:9], scol[:, 7:8], scol[:, 7:8],
                                        op=OP.mult),
              reads=(S["scol"],), writes=(S["scol"],))
        kb.op("vector",
              lambda E: E.scalar_tensor_tensor(scol[:, 9:10], scol[:, 6:7],
                                               1.0 / D, scol[:, 8:9],
                                               op0=OP.mult, op1=OP.subtract),
              reads=(S["scol"],), writes=(S["scol"],))
        eps_c2 = constcol(1e-5)
        kb.op("scalar",
              lambda E, eps_c2=eps_c2: E.activation(scol[:, 10:11], scol[:, 9:10],
                                                    AF.Ln, bias=eps_c2[0:64, :]),
              reads=(S["scol"], _cc_buf), writes=(S["scol"],))
        kb.op("scalar",
              lambda E: E.activation(scol[:, 11:12], scol[:, 10:11], AF.Exp,
                                     scale=-0.5),
              reads=(S["scol"],), writes=(S["scol"],))
        kb.op("vector",
              lambda E: E.tensor_scalar(sl[0:64, :], sl[0:64, :], scol[:, 7:8],
                                        scol[:, 11:12], op0=OP.subtract,
                                        op1=OP.mult),
              reads=(S["sl"], S["scol"]), writes=(S["sl"],))
        kb.op("vector",
              lambda E: E.tensor_tensor(sl[0:64, :], sl[0:64, :],
                                        p["lngb"][0:64, 0:D], op=OP.mult),
              reads=(S["sl"], p["b_lngb"]), writes=(S["sl"],))
        kb.op("vector",
              lambda E: E.tensor_tensor(sl[0:64, :], sl[0:64, :],
                                        p["lngb"][0:64, D:2 * D], op=OP.add),
              reads=(S["sl"], p["b_lngb"]), writes=(S["sl"],))
        slot_T(sl[0:64, :], S["sl"])
        for half in range(2):
            _stream_rhs_mm(dram["L1_ffn_wvp"], 0, half * 512, 512, sT, S["sT"],
                           64, 4 + half, stg2, S["stg"], stg2bf, S["stgbf"])
            bias_mm(4 + half, 64, p["bvp"], half * 512, 512)
            kb.op("vector",
                  lambda E, half=half: E.tensor_copy(
                      sv_bf[0:64, half * 512:half * 512 + 512],
                      psum[4 + half][0:64, 0:512]),
                  reads=(psb[4 + half],), writes=(S["sv"],))

        kb.set_fence()
        A.push()
        A.rebase(TAIL_OFF)
        wo_c = A.alloc([128, 8 * D], BF16, "l1_woc")
        b_woc = kb.buf()
        for i in range(8):
            kb.dma("sync", stg2, dram["L1_ffn_wo"][i * 128:(i + 1) * 128, :],
                   writes=(S["stg"],))
            kb.op("vector", lambda E, i=i: E.tensor_copy(
                wo_c[:, i * D:(i + 1) * D], stg2),
                reads=(S["stg"],), writes=(b_woc,))
        o1_bf = A.alloc([128, D], BF16, "l1_o1")
        o1T = A.alloc([128, D], BF16, "l1_o1T")
        b_o1 = kb.buf(); b_o1T = kb.buf()
        rd_b = hbm_reader()
        for t in range(NT):
            for half in range(2):
                kb.op("tensor",
                      lambda E, t=t, half=half: E.matmul(
                          psum[half][:, 0:512],
                          p["alphaT"][0:64, t * 128:(t + 1) * 128],
                          sv_bf[0:64, half * 512:half * 512 + 512],
                          start=True, stop=True),
                      reads=(balphaT, S["sv"]), writes=(psb[half],))
                kb.op("vector",
                      lambda E, half=half: E.tensor_copy(
                          o1_bf[:, half * 512:half * 512 + 512],
                          psum[half][:, 0:512]),
                      reads=(psb[half],), writes=(b_o1,))
            for i in range(8):
                kb.dmat(kb.tq(), o1T[:, i * 128:(i + 1) * 128],
                        o1_bf[:, i * 128:(i + 1) * 128],
                        reads=(b_o1,), writes=(b_o1T,))
            xa, xab = rd_b(t)
            for half in range(2):
                pidx = 2 + half
                for i in range(8):
                    kb.op("tensor",
                          lambda E, i=i, half=half, pidx=pidx: E.matmul(
                              psum[pidx][:, 0:512], o1T[:, i * 128:(i + 1) * 128],
                              wo_c[:, i * D + half * 512:i * D + half * 512 + 512],
                              start=(i == 0), stop=False),
                          reads=(b_o1T, b_woc), writes=(psb[pidx],))
                bias_mm(pidx, 128, p["bo"], half * 512, 512)
                kb.op("vector",
                      lambda E, t=t, half=half, pidx=pidx, xa=xa:
                      E.scalar_tensor_tensor(
                          xa[:, half * 512:half * 512 + 512],
                          psum[pidx][:, 0:512], 0.5,
                          xa[:, half * 512:half * 512 + 512],
                          op0=OP.mult, op1=OP.add),
                      reads=(psb[pidx],) + xab, writes=xab)
            kb.dma("sync", x_hbm[t * 128:(t + 1) * 128, :], xa,
                   reads=xab, writes=(xh[t],))
        A.pop()
        A.pop()
        A.pop()



    def l1_attn_phase(ph):
        kb.set_fence()
        A.push()
        A.rebase(l1f["end"], limit=128 * 1024)
        h0 = ph * 8
        qkvT = A.alloc([128, 12 * D], BF16, "la_qkvT")
        b_qkvT = kb.buf()
        projT = A.alloc([128, 4 * D], BF16, "la_projT")
        b_projT = kb.buf()
        expB = A.alloc([128, 3 * 8 * 256], BF16, "la_expB")
        b_expB = kb.buf()
        stg = A.alloc([128, D], F32, "la_stg")
        b_stg = kb.buf()
        stgb = A.alloc([128, D], BF16, "la_stgb")
        b_stgb = kb.buf()
        qbT = A.alloc([128, 8], F32, "la_qbT")
        b_qbT = kb.buf()
        set_bias_stage(D)
        vbias = bias_rows("la_vb%d" % ph, dram["L1_attn_qkv_b"][None,
                          2 * D + h0 * 64:2 * D + h0 * 64 + 512], 512)
        pbias = None
        if ph == 0:
            pbias = bias_rows("la_pb", dram["L1_attn_proj_b"][None, :], D)

        ocs = [h0 * 64 + j * 128 for j in range(4)] + \
              [D + h0 * 64 + j * 128 for j in range(4)] + \
              [2 * D + h0 * 64 + j * 128 for j in range(4)]
        for jj, orow in enumerate(ocs):
            kb.dma("sync", stg, dram["L1_attn_qkv_w"][orow:orow + 128, :],
                   writes=(b_stg,))
            kb.op("vector", lambda E: E.tensor_copy(stgb, stg),
                  reads=(b_stg,), writes=(b_stgb,))
            for i in range(8):
                kb.dmat(kb.tq(),
                        qkvT[:, i * 1536 + jj * 128:i * 1536 + (jj + 1) * 128],
                        stgb[:, i * 128:(i + 1) * 128],
                        reads=(b_stgb,), writes=(b_qkvT,))
        for oc in range(8):
            kb.dma("sync", stg[:, 0:512],
                   dram["L1_attn_proj_w"][oc * 128:(oc + 1) * 128,
                                          ph * 512:ph * 512 + 512],
                   writes=(b_stg,))
            kb.op("vector", lambda E: E.tensor_copy(stgb[:, 0:512], stg[:, 0:512]),
                  reads=(b_stg,), writes=(b_stgb,))
            for fc in range(4):
                kb.dmat(kb.tq(),
                        projT[:, fc * D + oc * 128:fc * D + (oc + 1) * 128],
                        stgb[:, fc * 128:(fc + 1) * 128],
                        reads=(b_stgb,), writes=(b_projT,))
        for tt in range(3):
            for hh in range(8):
                for cc in range(2):
                    kb.dma("sync", stgb[:, 0:128],
                           bmat_hbm[tt, h0 + hh, cc * 128:(cc + 1) * 128, :],
                           reads=(l1f["b_bm"],), writes=(b_stgb,))
                    kb.op("scalar",
                          lambda E, tt=tt, hh=hh, cc=cc: E.activation(
                              expB[:, (tt * 8 + hh) * 256 + cc * 128:
                                   (tt * 8 + hh) * 256 + (cc + 1) * 128],
                              stgb[:, 0:128], AF.Exp),
                          reads=(b_stgb,), writes=(b_expB,))
        kb.dma("sync", row_hbm[4:5, 0:512],
               dram["L1_attn_qkv_b"][None, h0 * 64:h0 * 64 + 512],
               writes=(b_qbT,))
        kb.dma("sync", row_hbm[5:6, 0:512],
               dram["L1_attn_qkv_b"][None, D + h0 * 64:D + h0 * 64 + 512],
               writes=(b_qbT,))
        for j in range(4):
            kb.dma("gpsimd", qbT[:, j:j + 1],
                   row_hbm[4:5, j * 128:(j + 1) * 128].rearrange("a b -> b a"),
                   reads=(b_qbT,), writes=(b_qbT,), allow_slow_non_contiguous=True)
            kb.dma("gpsimd", qbT[:, 4 + j:5 + j],
                   row_hbm[5:6, j * 128:(j + 1) * 128].rearrange("a b -> b a"),
                   reads=(b_qbT,), writes=(b_qbT,), allow_slow_non_contiguous=True)
        kb.op("vector", lambda E: E.tensor_scalar(qbT[:, 0:4], qbT[:, 0:4],
                                                  0.125, None, op0=OP.mult),
              reads=(b_qbT,), writes=(b_qbT,))

        A.rebase(TAIL_OFF)
        kTr = A.alloc([128, 4 * 768], BF16, "la_kTr")
        b_kTr = kb.buf()
        qTc = A.alloc([128, 2 * 512], BF16, "la_qTc")
        b_qTc = kb.buf()
        vsh = A.alloc([128, 4 * 512], BF16, "la_vsh")
        b_vsh = kb.buf()
        v_bf = A.alloc([128, 512], BF16, "la_vbf")
        b_vbf = kb.buf()
        lnx_bf = A.alloc([128, D], BF16, "la_lnx")
        lnxT = A.alloc([128, D], BF16, "la_lnxT")
        b_lnx = kb.buf(); b_lnxT = kb.buf()
        et = [A.alloc([128, 256], BF16, f"la_et{j}") for j in range(2)]
        b_et = [kb.buf(), kb.buf()]
        pfm = [A.alloc([128, 256], BF16, f"la_pfm{j}") for j in range(2)]
        b_pfm = [kb.buf(), kb.buf()]
        att = A.alloc([128, 512], BF16, "la_att")
        aoT = A.alloc([128, 512], BF16, "la_aoT")
        b_att = kb.buf(); b_aoT = kb.buf()
        rcol = A.alloc([128, 2], F32, "la_rcol")
        b_rcol = kb.buf()
        xtl = [A.alloc([128, D], F32, f"la_x{j}") for j in range(3)]
        b_xtl = [kb.buf() for j in range(3)]
        kb.op("vector", lambda E: E.memset(kTr[:], 0.0), writes=(b_kTr,))
        kb.op("vector", lambda E: E.memset(vsh[:], 0.0), writes=(b_vsh,))

        if ph == 0:
            ln_stats_pass(hbm_reader())

        def qkv_tile(u):
            j = u % 3
            slot = u % 4
            qslot = u % 2
            kb.dma("sync", xtl[j], x_hbm[u * 128:(u + 1) * 128, :],
                   reads=(xh[u],), writes=(b_xtl[j],))
            if ph == 0:
                kb.op("vector",
                      lambda E, u=u, j=j: E.tensor_scalar(
                          lnx_bf, xtl[j], st_mu[:, u:u + 1], st_rstd[:, u:u + 1],
                          op0=OP.subtract, op1=OP.mult),
                      reads=(b_xtl[j], sb_mu, sb_rstd), writes=(b_lnx,))
                kb.dma("sync", lnx_hbm[u * 128:(u + 1) * 128, :], lnx_bf,
                       reads=(b_lnx,), writes=(b_lnx,))
            else:
                kb.dma("sync", lnx_bf, lnx_hbm[u * 128:(u + 1) * 128, :],
                       writes=(b_lnx,))
            for i in range(8):
                kb.dmat(kb.tq(), lnxT[:, i * 128:(i + 1) * 128],
                        lnx_bf[:, i * 128:(i + 1) * 128],
                        reads=(b_lnx,), writes=(b_lnxT,))
            for bofs, bank in ((0, 0), (4, 1)):
                for pr in range(4):
                    for i in range(8):
                        kb.op("tensor",
                              lambda E, bofs=bofs, pr=pr, i=i, bank=bank: E.matmul(
                                  psum[bank][:, pr * 128:(pr + 1) * 128],
                                  qkvT[:, i * 1536 + (bofs + pr) * 128:
                                       i * 1536 + (bofs + pr + 1) * 128],
                                  lnxT[:, i * 128:(i + 1) * 128],
                                  start=(pr == 0 and i == 0),
                                  stop=(i == 7),
                                  skip_group_check=True),
                              reads=(b_qkvT, b_lnxT), writes=(psb[bank],))
            for i in range(8):
                kb.op("tensor",
                      lambda E, i=i: E.matmul(
                          psum[2][:, 0:512], lnxT[:, i * 128:(i + 1) * 128],
                          qkvT[:, i * 1536 + 1024:i * 1536 + 1536],
                          start=(i == 0), stop=False),
                      reads=(b_qkvT, b_lnxT), writes=(psb[2],))
            kb.op("tensor",
                  lambda E: E.matmul(psum[2][:, 0:512], ones2[:], vbias[0][:],
                                     start=False, stop=True),
                  reads=(b_ones2, vbias[1]), writes=(psb[2],))
            for pr in range(4):
                kb.op("scalar",
                      lambda E, pr=pr, qslot=qslot: E.activation(
                          qTc[:, qslot * 512 + pr * 128:qslot * 512 + (pr + 1) * 128],
                          psum[0][:, pr * 128:(pr + 1) * 128], AF.Identity,
                          bias=qbT[:, pr:pr + 1], scale=0.125),
                      reads=(psb[0], b_qbT), writes=(b_qTc,))
                kb.op("scalar",
                      lambda E, pr=pr, slot=slot: E.activation(
                          kTr[:, pr * 768 + slot * 128:pr * 768 + (slot + 1) * 128],
                          psum[1][:, pr * 128:(pr + 1) * 128], AF.Identity,
                          bias=qbT[:, 4 + pr:5 + pr]),
                      reads=(psb[1], b_qbT), writes=(b_kTr,))
                if slot < 2:
                    kb.dma("sync",
                           kTr[:, pr * 768 + (slot + 4) * 128:
                               pr * 768 + (slot + 5) * 128],
                           kTr[:, pr * 768 + slot * 128:pr * 768 + (slot + 1) * 128],
                           reads=(b_kTr,), writes=(b_kTr,))
            kb.op("vector", lambda E: E.tensor_copy(v_bf, psum[2][:, 0:512]),
                  reads=(psb[2],), writes=(b_vbf,))
            kb.dma("sync", vsh[64:128, slot * 512:(slot + 1) * 512],
                   v_bf[0:64, :], reads=(b_vbf,), writes=(b_vsh,))
            kb.dma("sync", vsh[0:64, ((u + 1) % 4) * 512:((u + 1) % 4 + 1) * 512],
                   v_bf[64:128, :], reads=(b_vbf,), writes=(b_vsh,))

        def attn_tile(t):
            tt = 0 if t == 0 else (2 if t == NT - 1 else 1)
            qslot = t % 2
            win = ((t - 1) % 4) * 128 + 64
            for hh in range(8):
                pr = hh // 2
                r0 = (hh % 2) * 64
                sb5 = 3 if hh % 2 == 0 else 7
                pp = hh % 2
                for cc in range(2):
                    kb.op("tensor",
                          lambda E, pr=pr, r0=r0, cc=cc, win=win, qslot=qslot,
                          sb5=sb5: E.matmul(
                              psum[sb5][:, cc * 128:(cc + 1) * 128],
                              kTr[r0:r0 + 64,
                                  pr * 768 + win + cc * 128:
                                  pr * 768 + win + (cc + 1) * 128],
                              qTc[r0:r0 + 64,
                                  qslot * 512 + pr * 128:qslot * 512 + (pr + 1) * 128],
                              start=(cc == 0), stop=True,
                              skip_group_check=True),
                          reads=(b_kTr, b_qTc), writes=(psb[sb5],))
                kb.op("scalar",
                      lambda E, sb5=sb5, pp=pp: E.activation(
                          et[pp][:], psum[sb5][:, 0:256], AF.Exp),
                      reads=(psb[sb5],), writes=(b_et[pp],))
                kb.op("vector",
                      lambda E, tt=tt, hh=hh, pp=pp: E.tensor_tensor(
                          pfm[pp][:], et[pp][:],
                          expB[:, (tt * 8 + hh) * 256:(tt * 8 + hh) * 256 + 256],
                          op=OP.mult),
                      reads=(b_et[pp], b_expB), writes=(b_pfm[pp],))
                for cc in range(2):
                    vs = ((t + cc) % 4) * 512 + hh * 64
                    kb.op("tensor",
                          lambda E, cc=cc, vs=vs, pp=pp: E.matmul(
                              psum[4][:, 0:64],
                              pfm[pp][:, cc * 128:(cc + 1) * 128],
                              vsh[:, vs:vs + 64],
                              start=(cc == 0), stop=True,
                              skip_group_check=True),
                          reads=(b_pfm[pp], b_vsh), writes=(psb[4],))
                    kb.op("tensor",
                          lambda E, cc=cc, pp=pp: E.matmul(
                              psum[4][:, 64:65],
                              pfm[pp][:, cc * 128:(cc + 1) * 128],
                              consts["ones_col"],
                              start=False, stop=True,
                              skip_group_check=True),
                          reads=(b_pfm[pp], cb["ones_col"]), writes=(psb[4],))
                kb.op("vector", lambda E: E.reciprocal(rcol[:, 0:1],
                                                       psum[4][:, 64:65]),
                      reads=(psb[4],), writes=(b_rcol,))
                kb.op("vector",
                      lambda E, hh=hh: E.tensor_scalar(
                          att[:, hh * 64:(hh + 1) * 64], psum[4][:, 0:64],
                          rcol[:, 0:1], None, op0=OP.mult),
                      reads=(psb[4], b_rcol), writes=(b_att,))
                if DEBUG and ph == 0 and t == 4 and hh == 0:
                    tap("la_pfm", pfm[0][:, 0:256], (b_pfm[0],))
                    tap("la_et", et[0][:, 0:256], (b_et[0],))
            if DEBUG and ph == 0 and t == 4:
                tap("la_att", att[:, 0:512], (b_att,))
                tap("la_qTc", qTc[:, 0:1024], (b_qTc,))
                tap("la_kTr", kTr[:, 0:768], (b_kTr,))
                tap("la_expB0", expB[:, 8 * 256:9 * 256], (b_expB,))
                tap("la_vsh", vsh[:, 0:2048], (b_vsh,))
            for fc in range(4):
                kb.dmat(kb.tq(), aoT[:, fc * 128:(fc + 1) * 128],
                        att[:, fc * 128:(fc + 1) * 128],
                        reads=(b_att,), writes=(b_aoT,))
            j = t % 3
            for half in range(2):
                pidx = 5 + half
                for fc in range(4):
                    kb.op("tensor",
                          lambda E, fc=fc, half=half, pidx=pidx: E.matmul(
                              psum[pidx][:, 0:512], aoT[:, fc * 128:(fc + 1) * 128],
                              projT[:, fc * D + half * 512:fc * D + half * 512 + 512],
                              start=(fc == 0), stop=(ph == 1 and fc == 3)),
                          reads=(b_aoT, b_projT), writes=(psb[pidx],))
                if ph == 0:
                    kb.op("tensor",
                          lambda E, half=half, pidx=pidx: E.matmul(
                              psum[pidx][:, 0:512], ones2[:],
                              pbias[0][:, half * 512:half * 512 + 512],
                              start=False, stop=True),
                          reads=(b_ones2, pbias[1]), writes=(psb[pidx],))
                kb.op("vector",
                      lambda E, half=half, pidx=pidx, j=j: E.tensor_tensor(
                          xtl[j][:, half * 512:half * 512 + 512],
                          xtl[j][:, half * 512:half * 512 + 512],
                          psum[pidx][:, 0:512], op=OP.add),
                      reads=(psb[pidx],), writes=(b_xtl[j],))
            kb.dma("sync", x_hbm[t * 128:(t + 1) * 128, :], xtl[j],
                   reads=(b_xtl[j],), writes=(xh[t],))

        qkv_tile(0)
        for t in range(NT):
            if t + 1 < NT:
                qkv_tile(t + 1)
            attn_tile(t)
        A.pop()

    x_in_hbm = False
    if STAGES >= 2:
        xh = l0_attn()
        x_in_hbm = True

    if STAGES >= 3:
        kb.set_fence()
        for t in range(NT):
            xb[t] = kb.buf(f"x[{t}]")
            kb.dma("sync", xt(t), x_hbm[t * 128:(t + 1) * 128, :],
                   reads=(xh[t],), writes=(xb[t],))
        x_in_hbm = False
        l0_ffn()

    if STAGES >= 4:
        kb.set_fence()
        for t in range(NT):
            xh[t] = kb.buf(f"xh4[{t}]")
            kb.dma("sync", x_hbm[t * 128:(t + 1) * 128, :], xt(t),
                   reads=(xb[t],), writes=(xh[t],))
        x_in_hbm = True
        l1_ffn_prep()
        l1_ffn()

    if STAGES >= 5:
        l1_attn_phase(0)
        l1_attn_phase(1)

    if STAGES >= 6:
        l1_ffn()

    out_toks = []
    for t in range(NT):
        if x_in_hbm:
            out_toks.append(
                kb.dma("sync", out_d[t * 128:(t + 1) * 128, :],
                       x_hbm[t * 128:(t + 1) * 128, :], reads=(xh[t],)))
        else:
            out_toks.append(
                kb.dma("sync", out_d[t * 128:(t + 1) * 128, :], xt(t),
                       reads=(xb[t],)))
    kb.op("sync", lambda E: E.nop(), extra=out_toks)


_CACHE = {}


def kernel(**inputs):
    if "nc" not in _CACHE:
        _CACHE["nc"] = build_program()
    nc = _CACHE["nc"]
    consts = host_constants()
    x = np.ascontiguousarray(inputs["L0_attn_x"])
    in_maps = []
    for c in range(8):
        m = {}
        for name, shape, dt in INPUT_SPECS:
            if name == "L0_attn_x":
                m[name] = np.ascontiguousarray(x[c])
            else:
                m[name] = np.ascontiguousarray(inputs[name])
        for cname, arr in consts.items():
            m["c_" + cname] = arr
        in_maps.append(m)
    res = run_bass_kernel_spmd(nc, in_maps, core_ids=list(range(8)))
    out = np.stack([r["out"] for r in res.results], axis=0)
    if DEBUG:
        kernel.taps = [{n: r[n] for n, _, _ in DEBUG_TAPS} for r in res.results]
    return out
